# version 12
# speedup vs baseline: 1.0182x; 1.0028x over previous
import contextlib
import numpy as np
import concourse.bass as bass
import concourse.mybir as mybir
from concourse.bass_utils import run_bass_kernel_spmd

F32 = mybir.dt.float32
BF16 = mybir.dt.bfloat16
I8 = mybir.dt.int8
AX = mybir.AxisListType.X
OP = mybir.AluOpType
ACT = mybir.ActivationFunctionType

N_CORES = 8
S = 2048
T = 2 * S
D = 1024
HD = 64
I_FFN = 2816
LN_EPS = 1e-5
RMS_EPS = 1.1920929e-07
HID = 2048
MZ = 256

CT = D // 128
NT = T // 128
NTB = S // 128
ITL = I_FFN // 128
SKEW_W = 2432
MAGIC_RND = 12582912.0

ENGS = ("sync", "gpsimd", "scalar", "vector", "tensor")
ESEM = {"tensor": "eT", "vector": "eV", "scalar": "eS", "gpsimd": "eG"}


class Bld:
    def __init__(self, nc, es=None):
        self.nc = nc
        self.es = es
        self.cnt = {}
        self.sems = {}

    def sem(self, name, es=None):
        h = (es or self.es).enter_context(self.nc.semaphore(name))
        self.sems[name] = h
        self.cnt[name] = 0
        return h

    def sem_for(self, ch):
        if ch not in self.sems:
            self.sem(ch)
        return ch


class Rec:

    def __init__(self, b):
        self.b = b
        self.st = {e: [] for e in ENGS}
        self.c = dict(b.cnt)

    def op(self, eng, fn, wait=(), sem=None, n=1):
        ev = None
        if sem is not None:
            self.c[sem] = self.c.get(sem, self.b.cnt.get(sem, 0)) + n
            ev = (sem, self.c[sem])
        self.st[eng].append((fn, tuple(w for w in wait if w is not None), sem, n))
        return ev

    def dma(self, eng, out, in_, wait=(), ch=None, **kw):
        assert ch is not None, "dma needs a channel"
        sem = self.b.sem_for(ch)
        return self.op(eng, lambda E: E.dma_start(out=out, in_=in_, **kw),
                       wait, sem, 16)

    def t(self, fn, wait=(), inc=True):
        return self.op("tensor", fn, wait, "eT" if inc else None)

    def v(self, fn, wait=(), inc=True):
        return self.op("vector", fn, wait, "eV")

    def s(self, fn, wait=(), inc=True):
        return self.op("scalar", fn, wait, "eS")

    def g(self, fn, wait=(), inc=True):
        return self.op("gpsimd", fn, wait, "eG")

    def fence(self):
        snap = tuple(self.c.items())
        for e in ENGS:
            self.st[e].append((None, snap, None, 0))

    def now(self):
        return tuple(self.c.items())

    def flush(self, block):
        b = self.b
        for eng in ENGS:
            items = self.st[eng]
            if not any(fn is not None or wait for fn, wait, _, _ in items):
                continue

            def make(items=items, eng=eng):
                own = ESEM.get(eng)
                serial = eng in ("vector", "scalar", "gpsimd")
                start = b.cnt.get(own, 0) if own else 0

                def sec(E):
                    hi = {}
                    cur = start
                    for fn, wait, sem, n in items:
                        for (s_, v) in wait:
                            if s_ == own:
                                continue
                            if hi.get(s_, -1) >= v:
                                continue
                            E.wait_ge(b.sems[s_], v)
                            hi[s_] = v
                        if fn is not None:
                            if serial and cur > 0:
                                E.wait_ge(b.sems[own], cur)
                            ins = fn(E)
                            if sem is not None:
                                ins.then_inc(b.sems[sem], n)
                                b.cnt[sem] += n
                                if sem == own:
                                    cur = b.cnt[sem]
                return sec

            getattr(block, eng)(make())
        for k, v in self.c.items():
            assert b.cnt.get(k, 0) == v, (k, b.cnt.get(k, 0), v)


def ftap(hbm_ap, row_pitch, row0, nrow, col0, ncol, nblk=1, blk_stride=0):
    off = row0 * row_pitch + col0
    dims = [[row_pitch, nrow]]
    if nblk > 1:
        dims.append([blk_stride, nblk])
    dims.append([1, ncol])
    return bass.AP(hbm_ap.tensor, off, dims)


def wsplit(r, dst_sb, hbm, row_pitch, f_total, nblk, blk_stride, ch,
           wait=(), nchunk=4, col0=0):
    fc = f_total // nchunk
    d3 = dst_sb[:, :].rearrange("p (c f) -> p c f", f=f_total)
    evs = []
    for i in range(nchunk):
        ev = r.dma("gpsimd", d3[:, :, i * fc:(i + 1) * fc],
                   ftap(hbm, row_pitch, 0, 128, col0 + i * fc, fc,
                        nblk=nblk, blk_stride=blk_stride),
                   wait=list(wait) if i == 0 else [], ch=ch)
        evs.append(((i + 1) * fc, ev))

    def at(f_end):
        for hi, ev in evs:
            if f_end <= hi:
                return ev
        return evs[-1][1]
    return at


def build(debug=(), stop_after=None, skip_qs=False):
    nc = bass.Bass("TRN2", target_bir_lowering=False, debug=False)
    import itertools as _it
    _ctr = _it.count()
    _sb, _ps = nc.sbuf_tensor, nc.psum_tensor
    nc.sbuf_tensor = lambda name, *a, **k: _sb(f"{name}_{next(_ctr)}", *a, **k)
    nc.psum_tensor = lambda name, *a, **k: _ps(f"{name}_{next(_ctr)}", *a, **k)

    def dram(name, shape, dt=F32, out=False):
        return nc.dram_tensor(name, shape, dt,
                              kind="ExternalOutput" if out else "ExternalInput").ap()

    def scratch(name, shape, dt=F32):
        if name in debug:
            return nc.dram_tensor(name, shape, dt, kind="ExternalOutput").ap()
        return nc.dram_tensor(name, shape, dt).ap()

    x_in = dram("x", [T, D])
    wqT = dram("wqT", [D, D]); wkT = dram("wkT", [D, D]); wvT = dram("wvT", [D, D])
    w_b = dram("w_b", [3 * D])
    projT = dram("projT", [D, D]); proj_b = dram("proj_b", [D])
    ln_g = dram("ln_g", [D]); ln_b = dram("ln_b", [D])
    rpbT = dram("rpbT", [16, 2 * S - 1])
    clip_x = dram("clip_x", [1]); clip_down = dram("clip_down", [1])
    w_gate = dram("w_gate", [I_FFN, D]); w_up = dram("w_up", [I_FFN, D])
    w_down = dram("w_down", [D, I_FFN])
    w_inT = dram("w_inT", [D, 2 * D])
    w_conv = dram("w_conv", [D, 4]); b_conv = dram("b_conv", [D])
    w_gatesT = dram("w_gatesT", [D, 2 * D]); b_gates = dram("b_gates", [2 * D])
    fbase = dram("fbase", [D])
    w_outT = dram("w_outT", [D, D])
    n1_w = dram("n1_w", [D]); n2_w = dram("n2_w", [D])
    l1wqT = dram("l1wqT", [D, D]); l1wkT = dram("l1wkT", [D, D])
    l1wvT = dram("l1wvT", [D, D]); l1woT = dram("l1woT", [D, D])
    qn_w = dram("qn_w", [HD]); kn_w = dram("kn_w", [HD])
    gate_wT = dram("gate_wT", [D, 4])
    e0_upT = dram("e0_upT", [D, HID]); e0_downT = dram("e0_downT", [HID, D])
    e1_w1T = dram("e1_w1T", [D, HID]); e1_w2T = dram("e1_w2T", [D, HID])
    e1_downT = dram("e1_downT", [HID, D])
    e2_l1T = dram("e2_l1T", [D, HID]); e2_l2T = dram("e2_l2T", [HID, HID])
    e2_l3T = dram("e2_l3T", [HID, HID]); e2_l4T = dram("e2_l4T", [HID, D])
    e3_downT = dram("e3_downT", [D, D]); e3_upT = dram("e3_upT", [D, HID])
    e3_outT = dram("e3_outT", [HID, D])
    y_out = dram("y", [T, D], out=True)

    ident = nc.inline_tensor(np.eye(128, dtype=np.float32), name="ident").ap()
    import ml_dtypes as _mld
    identb = nc.inline_tensor(np.eye(128, dtype=_mld.bfloat16), name="identb").ap()
    c01np = np.zeros(2 * S - 1, dtype=np.float32); c01np[:S] = 1.0
    causal01 = nc.inline_tensor(c01np, name="causal01").ap()

    xn_tf = scratch("xn_tf", [T, D])
    xnT = scratch("xnT", [D, T])
    s0_tf = scratch("s0_tf", [T, D])
    hpre = scratch("hpre", [I_FFN, T])
    hq_d = scratch("hq_d", [I_FFN, T], BF16)
    x1_tf = scratch("x1_tf", [T, D])
    xn1_tf = scratch("xn1_tf", [T, D])
    xn1T = scratch("xn1T", [D, T])
    h1T_d = scratch("h1T_d", [D, T], BF16)
    gg_d = scratch("gg_d", [D, T], BF16)
    y_hk = scratch("y_hk", [D, T], BF16)
    s1_tf = scratch("s1_tf", [T, D])
    xi_tf = scratch("xi_tf", [T, D])
    xfT_d = scratch("xfT_d", [D, T], BF16)
    m2_d = scratch("m2_d", [HID, T], BF16)
    spill = [scratch(f"spill{e}", [T, D - MZ]) for e in range(4)]
    exprpb = scratch("exprpb", [16, 2 * S - 1], BF16)
    bounce_h = scratch("bounce_h", [256, 1024])
    dbg_qT = scratch("dbg_qT", [128, 8 * S], BF16)
    dbg_kT = scratch("dbg_kT", [128, 8 * S], BF16)
    dbg_v = scratch("dbg_v", [128, NTB * 16 * 65], BF16)
    dbg_oft = scratch("dbg_oft", [128, 8 * S], BF16)
    dbg_xq = scratch("dbg_xq", [128, CT * T], BF16)
    dbg_wgq = scratch("dbg_wgq", [128, CT * I_FFN], BF16)
    dbg_wuq = scratch("dbg_wuq", [128, CT * I_FFN], BF16)
    dbg_sc = scratch("dbg_sc", [128, 2 * ITL + 2])
    sd_row = scratch("sd_row", [1, D])
    cc_io = [nc.dram_tensor(f"cc_{i}", [1, 2], F32).ap() for i in range(4)]
    m128a = nc.dram_tensor("m128a", [128], F32).ap()
    m128b = nc.dram_tensor("m128b", [128], F32).ap()

    es = contextlib.ExitStack()
    with es:
        block = es.enter_context(nc.Block())
        b = Bld(nc, es)
        for s_ in ("eT", "eV", "eS", "eG", "cc"):
            b.sem(s_)

        P = lambda name, shape, dt=F32: es.enter_context(nc.sbuf_tensor(name, shape, dt))
        sb_id = P("sb_id", [128, 128])
        sb_idb = P("sb_idb", [128, 128], BF16)
        bc_projb = P("bc_projb", [128, D]); bc_lng = P("bc_lng", [128, D])
        bc_lnb = P("bc_lnb", [128, D]); bc_vb = P("bc_vb", [128, D])
        bc_n1 = P("bc_n1", [128, D]); bc_n2 = P("bc_n2", [128, D])
        bc_qn = P("bc_qn", [128, HD]); bc_kn = P("bc_kn", [128, HD])
        bc_sd = P("bc_sd", [128, D])
        colv_qb = P("colv_qb", [128, CT]); colv_kb = P("colv_kb", [128, CT])
        colv_bc = P("colv_bc", [128, CT]); colv_bgf = P("colv_bgf", [128, CT])
        colv_bgi = P("colv_bgi", [128, CT]); colv_ce = P("colv_ce", [128, CT])
        wconv_c = P("wconv_c", [128, CT * 4])
        sig1r = P("sig1r", [128, 1]); sig1s = P("sig1s", [128, 1])
        sig2r = P("sig2r", [128, 1]); sig2s = P("sig2s", [128, 1])
        sgx = P("sgx", [128, ITL]); sux = P("sux", [128, ITL])
        sgt = P("sgt", [128, ITL]); sut = P("sut", [128, ITL]); sdt = P("sdt", [128, CT])
        skew01 = P("skew01", [128, SKEW_W], BF16)
        rmax1 = P("rmax1", [128, 1]); rmax2 = P("rmax2", [128, 1])
        wts_sb = P("wts_sb", [128, 4 * NT])
        r1_sb = P("r1_sb", [128, NT])
        stat_m = P("stat_m", [128, NT]); stat_r = P("stat_r", [128, NT])
        clip1 = P("clip1", [128, 1]); clip2 = P("clip2", [128, 1])
        tiny = P("tiny", [128, 64])

        with contextlib.ExitStack() as px:
            rpbs = px.enter_context(nc.sbuf_tensor("p0rpb", [16, 2 * S - 1], F32))
            rpbb = px.enter_context(nc.sbuf_tensor("p0rpbb", [16, 2 * S - 1], BF16))
            r = Rec(b)

            def brow(sb_t, src, n, q="sync"):
                ap = bass.AP(src.tensor, src.offset, [[0, 128], [1, n]])
                return r.dma(q, sb_t[:, 0:n], ap, ch="k0")

            def cload(sb_t, src, ntile=CT):
                ap = bass.AP(src.tensor, src.offset, [[1, 128], [128, ntile]])
                return r.dma("sync", sb_t[:, 0:ntile], ap, ch="k0",
                             allow_slow_non_contiguous=True)

            evs = [
                brow(bc_projb, proj_b, D), brow(bc_lng, ln_g, D), brow(bc_lnb, ln_b, D),
                brow(bc_vb, w_b[2 * D:3 * D], D), brow(bc_n1, n1_w, D), brow(bc_n2, n2_w, D),
                brow(bc_qn, qn_w, HD), brow(bc_kn, kn_w, HD),
                cload(colv_qb, w_b[0:D]), cload(colv_kb, w_b[D:2 * D]),
                cload(colv_bc, b_conv), cload(colv_bgf, b_gates[0:D]),
                cload(colv_bgi, b_gates[D:2 * D]), cload(colv_ce, fbase),
                brow(clip1, clip_x, 1), brow(clip2, clip_down, 1),
            ]
            wc_ap = bass.AP(w_conv.tensor, 0, [[4, 128], [512, CT], [1, 4]])
            evs.append(r.dma("sync", wconv_c.ap().rearrange("p (c k) -> p c k", k=4),
                             wc_ap, ch="k0"))
            ev_rpb = r.dma("sync", rpbs[:, :], rpbT, ch="k0")
            sk_ap = bass.AP(causal01.tensor, 0, [[1, 128], [1, SKEW_W]])
            evs.append(r.dma("gpsimd", skew01[:, :], sk_ap, ch="k0g"))
            evs.append(r.dma("sync", sb_id[:, :], ident, ch="k0"))
            evs.append(r.dma("sync", sb_idb[:, :], identb, ch="k0"))
            e_all = ("k0", r.c["k0"])
            e_allg = ("k0g", r.c["k0g"])

            ev1 = r.s(lambda E: E.activation(rpbb[:, :], rpbs[:, :], ACT.Exp),
                      wait=[e_all, e_allg])
            r.s(lambda E: E.activation(clip1[:, :], clip1[:, :], ACT.Sigmoid), wait=[e_all])
            r.s(lambda E: E.activation(clip2[:, :], clip2[:, :], ACT.Sigmoid), wait=[e_all])
            ev2 = r.s(lambda E: E.activation(colv_ce[:, :], colv_ce[:, :], ACT.Exp), wait=[e_all])
            ev3 = r.s(lambda E: E.activation(colv_ce[:, :], colv_ce[:, :], ACT.Ln, bias=1.0), wait=[ev2])
            ev4 = r.v(lambda E: E.memset(rpbb[:, S:2 * S - 1], 0.0), wait=[ev1])
            r.v(lambda E: E.memset(rmax1[:, :], 0.0))
            r.v(lambda E: E.memset(rmax2[:, :], 0.0))
            ev5 = r.v(lambda E: E.tensor_scalar_mul(colv_ce[:, :], colv_ce[:, :], -8.0), wait=[ev3])
            r.v(lambda E: E.tensor_scalar_mul(bc_qn[:, :], bc_qn[:, :], 0.125), wait=[e_all])
            r.dma("gpsimd", exprpb, rpbb[:, :], wait=[ev4], ch="k1")
            r.fence()
            r.flush(block)

        def ln_two_pass(src_tf, dst_tf, dstT, absmax_to=None, h1_also=False):
            with contextlib.ExitStack() as px:
                A = lambda name, shape, dt=F32: px.enter_context(nc.sbuf_tensor(name, shape, dt))
                xt = [A(f"lnx{i}", [128, D]) for i in range(3)]
                sqsc = A("lnsq", [128, D])
                vee = A("lnve2", [128, NT])
                nb_ = [A(f"lnn{i}", [128, D]) for i in range(2)]
                stT = [A(f"lnsT{i}", [128, CT * 128]) for i in range(2)]
                stH = [A(f"lnsH{i}", [128, CT * 128], BF16) for i in range(2)] if h1_also else None
                h1b = A("lnh1", [128, D], BF16) if h1_also else None
                red = A("lnred", [128, 1])
                pst = [px.enter_context(nc.psum_tensor(f"lnp{i}", [128, 128], F32)) for i in range(2)]
                psth = ([px.enter_context(nc.psum_tensor(f"lnph{i}", [128, 128], BF16))
                         for i in range(2)] if h1_also else None)

                r = Rec(b)
                r.fence()
                ev_st = {}
                ev_sq = {}
                ld = {}
                for tt in range(NT):
                    ld[tt] = r.dma("sync", xt[tt % 3][:, :],
                                   src_tf[tt * 128:(tt + 1) * 128, :],
                                   wait=[ev_st.get(tt - 3), ev_sq.get(tt - 3)],
                                   ch=f"lnA{tt % 3}")
                    ev_st[tt] = r.v(lambda E, tt=tt: E.tensor_reduce(
                        stat_m[:, tt:tt + 1], xt[tt % 3][:, :], axis=AX, op=OP.add),
                        wait=[ld[tt]])
                    ev_sq[tt] = r.s(lambda E, tt=tt: E.activation(
                        sqsc[:, :], xt[tt % 3][:, :], ACT.Square,
                        accum_out=stat_r[:, tt:tt + 1]), wait=[ld[tt]])
                e1 = r.v(lambda E: E.tensor_scalar_mul(stat_m[:, 0:NT], stat_m[:, 0:NT],
                                                       1.0 / D),
                         wait=[ev_st[NT - 1], ev_sq[NT - 1]])
                r.v(lambda E: E.tensor_tensor(vee[:, 0:NT], stat_m[:, 0:NT],
                                              stat_m[:, 0:NT], OP.mult), inc=False)
                r.v(lambda E: E.tensor_scalar_mul(stat_r[:, 0:NT], stat_r[:, 0:NT], 1.0 / D),
                    inc=False)
                r.v(lambda E: E.tensor_tensor(stat_r[:, 0:NT], stat_r[:, 0:NT], vee[:, 0:NT],
                                              OP.subtract), inc=False)
                e1 = r.v(lambda E: E.tensor_scalar(vee[:, 0:NT], stat_r[:, 0:NT], 1.0, LN_EPS,
                                                   OP.mult, OP.add))
                if h1_also:
                    r.v(lambda E: E.reciprocal(r1_sb[:, 0:NT], vee[:, 0:NT]), wait=[e1],
                        inc=False)
                    r.v(lambda E: E.tensor_tensor(r1_sb[:, 0:NT], r1_sb[:, 0:NT],
                                                  stat_r[:, 0:NT], OP.mult), inc=False)
                    e1b = r.v(lambda E: E.tensor_scalar_add(r1_sb[:, 0:NT], r1_sb[:, 0:NT],
                                                            RMS_EPS))
                    e2b = r.s(lambda E: E.activation(r1_sb[:, 0:NT], r1_sb[:, 0:NT],
                                                     ACT.Sqrt), wait=[e1b])
                    e3b = r.v(lambda E: E.reciprocal(r1_sb[:, 0:NT], r1_sb[:, 0:NT]),
                              wait=[e2b])
                else:
                    e3b = None
                e2 = r.s(lambda E: E.activation(vee[:, 0:NT], vee[:, 0:NT], ACT.Sqrt),
                         wait=[e1])
                e3 = r.v(lambda E: E.reciprocal(stat_r[:, 0:NT], vee[:, 0:NT]), wait=[e2])

                ev_cons = {}
                st_done = {}
                psg = [None, None]
                psgh = [None, None]
                stTg = [None, None]
                stHg = [None, None]
                last_tr = {}
                last_trh = {}
                for tt in range(NT):
                    ldb = r.dma("sync", xt[tt % 3][:, :], src_tf[tt * 128:(tt + 1) * 128, :],
                                wait=[e3, ev_cons.get(tt - 3)], ch=f"lnA{tt % 3}")
                    nb = nb_[tt % 2]
                    evn = r.v(lambda E, tt=tt, nb=nb: E.tensor_scalar(
                        nb[:, :], xt[tt % 3][:, :], stat_m[:, tt:tt + 1], stat_r[:, tt:tt + 1],
                        OP.subtract, OP.mult),
                        wait=[ldb, st_done.get(tt - 2), last_tr.get(tt - 2)])
                    ev_cons[tt] = evn
                    if absmax_to is not None:
                        r.v(lambda E, nb=nb: E.tensor_reduce(red[:, :], nb[:, :], axis=AX,
                                                             op=OP.max, apply_absolute_value=True),
                            inc=False)
                        r.v(lambda E: E.tensor_max(absmax_to[:, :], absmax_to[:, :], red[:, :]),
                            inc=False)
                    evh = None
                    if h1_also:
                        r.v(lambda E, tt=tt, nb=nb: E.tensor_scalar_mul(
                            h1b[:, :], nb[:, :], r1_sb[:, tt:tt + 1]),
                            wait=[e3b, last_trh.get(tt - 1)], inc=False)
                        evh = r.v(lambda E: E.tensor_tensor(h1b[:, :], h1b[:, :], bc_n1[:, :], OP.mult))
                    st_done[tt] = r.dma("scalar", dst_tf[tt * 128:(tt + 1) * 128, :], nb[:, :],
                                        wait=[evn], ch=f"lnB{tt % 2}")
                    for ct in range(CT):
                        i = ct % 2
                        evt = r.t(lambda E, nb=nb, ct=ct: E.transpose(
                            pst[ct % 2][:, :], nb[:, ct * 128:(ct + 1) * 128], sb_id[:, :]),
                            wait=[evn, psg[i]])
                        last_tr[tt] = evt
                        psg[i] = r.v(lambda E, tt=tt, ct=ct: E.tensor_copy(
                            stT[tt % 2][:, ct * 128:(ct + 1) * 128], pst[ct % 2][:, :]),
                            wait=[evt, stTg[tt % 2] if ct == 0 else None])
                    stTg[tt % 2] = r.dma(
                        "scalar",
                        ftap(dstT, T, 0, 128, tt * 128, 128, nblk=CT, blk_stride=128 * T),
                        stT[tt % 2].ap().rearrange("p (c w) -> p c w", w=128),
                        wait=[psg[(CT - 1) % 2]], ch=f"lnC{tt % 2}")
                    if h1_also:
                        for ct in range(CT):
                            i = ct % 2
                            evt = r.t(lambda E, ct=ct: E.transpose(
                                psth[ct % 2][:, :], h1b[:, ct * 128:(ct + 1) * 128],
                                sb_idb[:, :]), wait=[evh, psgh[i]])
                            last_trh[tt] = evt
                            psgh[i] = r.v(lambda E, tt=tt, ct=ct: E.tensor_copy(
                                stH[tt % 2][:, ct * 128:(ct + 1) * 128], psth[ct % 2][:, :]),
                                wait=[evt, stHg[tt % 2] if ct == 0 else None])
                        stHg[tt % 2] = r.dma(
                            "scalar",
                            ftap(h1T_d, T, 0, 128, tt * 128, 128, nblk=CT, blk_stride=128 * T),
                            stH[tt % 2].ap().rearrange("p (c w) -> p c w", w=128),
                            wait=[psgh[(CT - 1) % 2]], ch=f"lnD{tt % 2}")
                r.fence()
                r.flush(block)

        ln_two_pass(x_in, xn_tf, xnT, absmax_to=rmax1)

        def quant_scale_phase(rmax_t, cc_in_t, cc_out_t, clip_t, sig_s, sig_r, m128_d):
            with contextlib.ExitStack() as px:
                A = lambda name, shape, dt=F32: px.enter_context(nc.sbuf_tensor(name, shape, dt))
                row = A("qsrow", [1, 128])
                mloc = A("qsm", [128, 2])
                r = Rec(b)
                r.fence()
                e1 = r.dma("gpsimd", m128_d, rmax_t[:, :], ch="qs")
                e2 = r.dma("gpsimd", row[:, :], m128_d.unsqueeze(0), wait=[e1], ch="qs")
                e3 = r.v(lambda E: E.tensor_reduce(mloc[0:1, 0:1], row[:, :], axis=AX,
                                                   op=OP.max), wait=[e2])
                e4 = r.v(lambda E: E.tensor_copy(
                    mloc[0:1, 0:2], mloc.ap()[0:1, 0:1].broadcast_to([1, 2])))
                e5 = r.dma("gpsimd", cc_in_t, mloc[0:1, 0:2], wait=[e4], ch="qs")
                e6 = r.op("gpsimd", lambda E: E.collective_compute(
                    "AllReduce", OP.max, replica_groups=[list(range(N_CORES))],
                    ins=[cc_in_t.opt()], outs=[cc_out_t.opt()]), wait=[e5], sem="cc")
                bcap = bass.AP(cc_out_t.tensor, 0, [[0, 128], [1, 1]])
                e7 = r.dma("gpsimd", mloc[:, 0:1], bcap, wait=[e6], ch="qs")
                r.v(lambda E: E.tensor_tensor(sig_s[:, :], clip_t[:, :], mloc[:, 0:1],
                                              OP.mult), wait=[e7], inc=False)
                r.v(lambda E: E.tensor_scalar_mul(sig_s[:, :], sig_s[:, :], 1.0 / 127.0),
                    inc=False)
                r.v(lambda E: E.reciprocal(sig_r[:, :], sig_s[:, :]), inc=False)
                r.fence()
                r.flush(block)

        if not skip_qs:
            quant_scale_phase(rmax1, cc_io[0], cc_io[1], clip1, sig1s, sig1r,
                              m128a)

        def attn_batch(qT_ap, kT_ap, v_sb, oft_ap, is_l0):
            ROT = 2
            SR = 3
            v4 = v_sb.ap().rearrange("p (j h d) -> p j h d", h=16, d=65)
            with contextlib.ExitStack() as px:
                A = lambda name, shape, dt=F32: px.enter_context(nc.sbuf_tensor(name, shape, dt))
                PS = lambda name, shape: px.enter_context(nc.psum_tensor(name, shape, F32))
                skew = ([A(f"at_sk{i}", [128, 2 * SKEW_W], BF16) for i in range(2)]
                        if is_l0 else None)
                pe16 = [A(f"at_pe{i}", [128, 1024], BF16) for i in range(SR)]
                pr = [A(f"at_pr{i}", [128, 1024], BF16) for i in range(SR)]
                srow = [A(f"at_sr{i}", [1, 1024]) for i in range(2)]
                rr8 = [A(f"at_r8{i}", [128, 8]) for i in range(2)]
                rro = [A(f"at_ro{i}", [128, 8]) for i in range(2)]
                rbc = [A(f"at_rb{i}", [64, 1024], BF16) for i in range(2)]
                ps_s = [PS(f"at_ps{i}", [128, 1024]) for i in range(ROT)]
                ps_o = [PS(f"at_po{h}", [65, 512]) for h in range(4)]

                r = Rec(b)
                r.fence()
                ev_exp, ev_prb, ev_ocons = {}, {}, {}
                last_pe_rd = {}
                t_mults = [None, None]
                t_dma1 = [None, None]
                t_dma3 = [None, None]
                t_recip = [None, None]
                ev_skg = [None, None]
                step = 0
                pend_o = None

                def emit_o(po):
                    jt, jm, sr, h0, h1, first, st_, par = po
                    w_o = [ev_prb[st_]]
                    if first:
                        w_o += list(t_mults[par] or ())
                    r.t(lambda E, jt=jt, h0=h0, sr=sr, jm=jm, par=par: E.matmul(
                        ps_o[2 * par][:, :], v4[:, jt, h0, 0:65], pr[sr][:, 0:512],
                        start=(jt == 0), stop=(jt == jm)), wait=w_o, inc=False)
                    ev = r.t(lambda E, jt=jt, h1=h1, sr=sr, jm=jm, par=par: E.matmul(
                        ps_o[2 * par + 1][:, :], v4[:, jt, h1, 0:65], pr[sr][:, 512:1024],
                        start=(jt == 0), stop=(jt == jm)))
                    ev_ocons[st_] = ev
                    return ev

                for hp in range(8):
                    h0, h1 = 2 * hp, 2 * hp + 1
                    ev_sk = None
                    if is_l0:
                        skb = hp % 2
                        ev_sk = r.dma(
                            "scalar",
                            skew[skb].ap().rearrange("p (h w) -> p h w", h=2),
                            bass.AP(exprpb.tensor, 2 * hp * (2 * S - 1),
                                    [[1, 128], [2 * S - 1, 2], [1, SKEW_W]]),
                            wait=[ev_skg[skb]], ch=f"atsk{skb}")
                    for ic in range(4):
                        par = ic % 2
                        jmax = 4 * ic + 3
                        for jt in range(jmax + 1):
                            rot = step % ROT
                            sr = step % SR
                            w_sc = [ev_exp.get(step - ROT)]
                            c0 = 2047 + 128 * jt - 512 * ic
                            mm0 = r.t(lambda E, hp=hp, jt=jt, ic=ic, rot=rot: E.matmul(
                                ps_s[rot][:, 0:512],
                                kT_ap[0:64, hp * S + jt * 128: hp * S + (jt + 1) * 128],
                                qT_ap[0:64, hp * S + ic * 512: hp * S + (ic + 1) * 512],
                                start=True, stop=True, tile_position=(0, 0)), wait=w_sc)
                            mm1 = r.t(lambda E, hp=hp, jt=jt, ic=ic, rot=rot: E.matmul(
                                ps_s[rot][:, 512:1024],
                                kT_ap[64:128, hp * S + jt * 128: hp * S + (jt + 1) * 128],
                                qT_ap[64:128, hp * S + ic * 512: hp * S + (ic + 1) * 512],
                                start=True, stop=True, tile_position=(64, 0)), wait=w_sc)
                            if pend_o is not None:
                                emit_o(pend_o)
                                pend_o = None
                            diag = jt >= 4 * ic
                            if is_l0 or diag:
                                ev_e = r.s(lambda E, rot=rot, sr=sr: E.activation(
                                    pe16[sr][:, :], ps_s[rot][:, :], ACT.Exp),
                                    wait=[mm1, last_pe_rd.get(sr)])
                                ev_exp[step] = ev_e
                                if is_l0:
                                    skt = skew[hp % 2].ap().rearrange(
                                        "p (h w) -> p h w", h=2)
                                    if c0 - 512 < 0:
                                        sl = (lambda skt=skt, c0=c0: skt[:, :, c0::-1])
                                    else:
                                        sl = (lambda skt=skt, c0=c0:
                                              skt[:, :, c0:c0 - 512:-1])
                                else:
                                    if c0 - 512 < 0:
                                        sl_ = skew01[:, c0::-1]
                                    else:
                                        sl_ = skew01[:, c0:c0 - 512:-1]
                                    sl_ = sl_.unsqueeze(1).broadcast_to([128, 2, 512])
                                    sl = (lambda sl_=sl_: sl_)
                                ev_p = r.v(lambda E, sr=sr, sl=sl: E.tensor_tensor(
                                    pr[sr].ap().rearrange("p (h w) -> p h w", h=2),
                                    pe16[sr].ap().rearrange("p (h w) -> p h w", h=2),
                                    sl(), OP.mult),
                                    wait=[ev_e, ev_sk, ev_ocons.get(step - SR)])
                                last_pe_rd[sr] = ev_p
                                ev_prb[step] = ev_p
                                ev_skg[hp % 2] = ev_p
                            else:
                                ev_e = r.s(lambda E, rot=rot, sr=sr: E.activation(
                                    pr[sr][:, :], ps_s[rot][:, :], ACT.Exp),
                                    wait=[mm1, ev_ocons.get(step - SR)])
                                ev_exp[step] = ev_e
                                ev_prb[step] = ev_e
                            pend_o = (jt, jmax, sr, h0, h1, jt == 0, step, par)
                            step += 1
                        ev_o = emit_o(pend_o)
                        pend_o = None
                        idx = attn_batch.slot
                        attn_batch.slot += 1
                        ev_c0 = r.v(lambda E, par=par: E.tensor_copy(
                            srow[par][0:1, 0:512], ps_o[2 * par][64:65, 0:512]),
                            wait=[ev_o, t_dma1[par]])
                        ev_c1 = r.v(lambda E, par=par: E.tensor_copy(
                            srow[par][0:1, 512:1024], ps_o[2 * par + 1][64:65, 0:512]))
                        d1 = r.dma("sync", ftap(bounce_h, 1024, 2 * idx, 1, 0, 1024),
                                   srow[par][:, :], wait=[ev_c0, ev_c1],
                                   ch=f"tl{par}")
                        t_dma1[par] = d1
                        d2 = r.dma("sync", rr8[par][:, :],
                                   bass.AP(bounce_h.tensor, 2 * idx * 1024,
                                           [[8, 128], [1, 8]]),
                                   wait=[d1, t_recip[par]], ch=f"tl{par}")
                        ev_r = r.v(lambda E, par=par: E.reciprocal(
                            rro[par][:, :], rr8[par][:, :]), wait=[d2, t_dma3[par]])
                        t_recip[par] = ev_r
                        d3 = r.dma("sync",
                                   bass.AP(bounce_h.tensor, (2 * idx + 1) * 1024,
                                           [[8, 128], [1, 8]]),
                                   rro[par][:, :], wait=[ev_r], ch=f"tl{par}")
                        t_dma3[par] = d3
                        d4 = r.dma("gpsimd", rbc[par][:, :],
                                   bass.AP(bounce_h.tensor, (2 * idx + 1) * 1024,
                                           [[0, 64], [1, 1024]]),
                                   wait=[d3] + list(t_mults[par] or ()),
                                   ch=f"tl{par}")
                        ev_m0 = r.v(lambda E, hp=hp, ic=ic, par=par: E.tensor_tensor(
                            oft_ap[0:64, hp * S + ic * 512:hp * S + (ic + 1) * 512],
                            ps_o[2 * par][0:64, 0:512],
                            rbc[par][:, 0:512], OP.mult), wait=[d4])
                        ev_m1 = r.v(lambda E, hp=hp, ic=ic, par=par: E.tensor_tensor(
                            oft_ap[64:128, hp * S + ic * 512:hp * S + (ic + 1) * 512],
                            ps_o[2 * par + 1][0:64, 0:512],
                            rbc[par][:, 512:1024], OP.mult))
                        t_mults[par] = (ev_m0, ev_m1)
                r.fence()
                r.flush(block)
        attn_batch.slot = 0

        for bi in range(2) if stop_after != "p0" else ():
            with contextlib.ExitStack() as pb:
                A = lambda name, shape, dt=F32: pb.enter_context(nc.sbuf_tensor(name, shape, dt))
                qT_sb = A("p1q", [128, 8 * S], BF16)
                kT_sb = A("p1k", [128, 8 * S], BF16)
                v_sb = A("p1v", [128, NTB * 16 * 65], BF16)
                with contextlib.ExitStack() as px:
                    A2 = lambda name, shape, dt=F32: px.enter_context(nc.sbuf_tensor(name, shape, dt))
                    w_sb = A2("p1w", [128, CT * D], BF16)
                    xch = [A2(f"p1x{i}", [128, CT * 512], BF16) for i in range(2)]
                    ps_a = [px.enter_context(nc.psum_tensor(f"p1pa{i}", [128, 512], F32))
                            for i in range(2)]
                    r = Rec(b)
                    r.fence()
                    ev_ones = r.v(lambda E: E.memset(
                        v_sb.ap().rearrange("p (j h d) -> p j h d", h=16, d=65)[:, :, :, 64:65],
                        1.0))
                    w3 = w_sb.ap().rearrange("p (c f) -> p c f", f=D)
                    psa_g = [None, None]
                    last_w_use = None
                    for which, srcw in enumerate((wqT, wkT, wvT)):
                        ev_w = wsplit(r, w_sb, srcw, D, D, CT, 128 * D, "p1w",
                                      wait=[last_w_use])
                        ld_g = {}
                        for ch in range(4):
                            ld = r.dma("gpsimd",
                                       xch[ch % 2][:, :].rearrange("p (c w) -> p c w", w=512),
                                       ftap(xnT, T, 0, 128, bi * S + ch * 512, 512,
                                            nblk=CT, blk_stride=128 * T),
                                       wait=[ld_g.get(ch - 2)], ch=f"p1x{ch % 2}")
                            x3 = xch[ch % 2].ap().rearrange("p (c w) -> p c w", w=512)
                            if which < 2:
                                colv, sc, dst = ((colv_qb, 1.0 / 32.0, qT_sb) if which == 0
                                                 else (colv_kb, 1.0, kT_sb))
                                for ft in range(8):
                                    rot = ft % 2
                                    for ct in range(CT):
                                        r.t(lambda E, ct=ct, ft=ft, rot=rot, x3=x3: E.matmul(
                                            ps_a[rot][:, :],
                                            w3[:, ct, ft * 128:(ft + 1) * 128],
                                            x3[:, ct, :], start=(ct == 0),
                                            stop=(ct == CT - 1)),
                                            wait=[ld, ev_w((ft + 1) * 128), psa_g[rot]],
                                            inc=(ct == CT - 1))
                                    ev_mm = ("eT", r.c["eT"])
                                    psa_g[rot] = r.v(
                                        lambda E, dst=dst, colv=colv, sc=sc, ft=ft, ch=ch,
                                        rot=rot: E.tensor_scalar(
                                            dst[:, ft * S + ch * 512: ft * S + (ch + 1) * 512],
                                            ps_a[rot][:, :], colv[:, ft:ft + 1], sc,
                                            OP.add, OP.mult), wait=[ev_mm])
                                    last_w_use = psa_g[rot]
                            else:
                                for tt4 in range(4):
                                    for fch in range(2):
                                        rot = (tt4 * 2 + fch) % 2
                                        for ct in range(CT):
                                            r.t(lambda E, x3=x3, ct=ct, tt4=tt4, fch=fch,
                                                rot=rot: E.matmul(
                                                    ps_a[rot][:, :],
                                                    x3[:, ct, tt4 * 128:(tt4 + 1) * 128],
                                                    w3[:, ct, fch * 512:(fch + 1) * 512],
                                                    start=(ct == 0), stop=(ct == CT - 1)),
                                                wait=[ld, ev_w((fch + 1) * 512), psa_g[rot]],
                                                inc=(ct == CT - 1))
                                        ev_mm = ("eT", r.c["eT"])
                                        jt = ch * 4 + tt4
                                        psa_g[rot] = r.v(
                                            lambda E, jt=jt, fch=fch, rot=rot:
                                            E.scalar_tensor_tensor(
                                                v_sb.ap().rearrange(
                                                    "p (j h d) -> p j h d", h=16, d=65)
                                                [:, jt, fch * 8:(fch + 1) * 8, 0:64],
                                                ps_a[rot].ap().rearrange(
                                                    "p (h d) -> p h d", d=64),
                                                0.0,
                                                bc_vb.ap().rearrange("p (h d) -> p h d", d=64)
                                                [:, fch * 8:(fch + 1) * 8, :],
                                                OP.add, OP.add),
                                            wait=[ev_mm, ev_ones])
                                        last_w_use = psa_g[rot]
                            ld_g[ch] = last_w_use
                    r.fence()
                    r.flush(block)
                oft_sb = A("p1o", [128, 8 * S], BF16)
                if bi == 0 and "dbg_qT" in debug:
                    r = Rec(b); r.fence()
                    r.dma("gpsimd", dbg_qT, qT_sb[:, :], ch="dbg")
                    r.dma("gpsimd", dbg_kT, kT_sb[:, :], ch="dbg")
                    r.dma("gpsimd", dbg_v, v_sb[:, :], ch="dbg")
                    r.fence(); r.flush(block)
                attn_batch(qT_sb.ap(), kT_sb.ap(), v_sb, oft_sb.ap(), is_l0=True)
                if bi == 0 and "dbg_oft" in debug:
                    r = Rec(b); r.fence()
                    r.dma("gpsimd", dbg_oft, oft_sb[:, :], ch="dbg")
                    r.fence(); r.flush(block)
                with contextlib.ExitStack() as px:
                    A2 = lambda name, shape, dt=F32: px.enter_context(nc.sbuf_tensor(name, shape, dt))
                    pj_sb = A2("p1pj", [128, CT * D], BF16)
                    xnt = [A2(f"p1xn{i}", [128, D]) for i in range(2)]
                    yt = [A2(f"p1y{i}", [128, D]) for i in range(2)]
                    sq = A2("p1sq", [128, D])
                    sm = A2("p1sm", [128, 1]); ss = A2("p1ss", [128, 1])
                    rv = A2("p1rv", [128, 1])
                    ps_p = [px.enter_context(nc.psum_tensor(f"p1pp{i}", [128, 512], F32))
                            for i in range(2)]
                    r = Rec(b)
                    r.fence()
                    ev_w = wsplit(r, pj_sb, projT, D, D, CT, 128 * D, "p1w")
                    o3 = oft_sb.ap().rearrange("p (c w) -> p c w", w=S)
                    pj3 = pj_sb.ap().rearrange("p (c f) -> p c f", f=D)
                    ev_done = {}
                    ev_ytile = {}
                    prev_fch = [None, None]
                    for tt in range(NTB):
                        gt = bi * NTB + tt
                        ldx = r.dma("sync", xnt[tt % 2][:, :],
                                    xn_tf[gt * 128:(gt + 1) * 128, :],
                                    wait=[ev_ytile.get(tt - 2)], ch=f"p1n{tt % 2}")
                        ev_fch = []
                        for fch in range(2):
                            for ct in range(CT):
                                r.t(lambda E, ct=ct, tt=tt, fch=fch: E.matmul(
                                    ps_p[fch][:, :], o3[:, ct, tt * 128:(tt + 1) * 128],
                                    pj3[:, ct, fch * 512:(fch + 1) * 512],
                                    start=(ct == 0), stop=(ct == CT - 1)),
                                    wait=[ev_w((fch + 1) * 512), prev_fch[fch]],
                                    inc=(ct == CT - 1))
                            ev_mm = ("eT", r.c["eT"])
                            evf = r.v(lambda E, tt=tt, fch=fch: E.scalar_tensor_tensor(
                                yt[tt % 2][:, fch * 512:(fch + 1) * 512], ps_p[fch][:, :], 0.0,
                                bc_projb[:, fch * 512:(fch + 1) * 512], OP.add, OP.add),
                                wait=[ev_mm, ev_done.get(tt - 2)])
                            ev_fch.append(evf)
                            prev_fch[fch] = evf
                        ev_add = r.v(lambda E, tt=tt: E.tensor_tensor(
                            yt[tt % 2][:, :], yt[tt % 2][:, :], xnt[tt % 2][:, :], OP.add),
                            wait=[ldx] + ev_fch)
                        ev_m = r.v(lambda E, tt=tt: E.tensor_reduce(
                            sm[:, :], yt[tt % 2][:, :], axis=AX, op=OP.add))
                        ev_q = r.s(lambda E, tt=tt: E.activation(
                            sq[:, :], yt[tt % 2][:, :], ACT.Square, accum_out=ss[:, 0:1]),
                            wait=[ev_add])
                        r.v(lambda E: E.tensor_scalar_mul(sm[:, :], sm[:, :], 1.0 / D),
                            wait=[ev_q], inc=False)
                        r.v(lambda E: E.tensor_scalar_mul(ss[:, :], ss[:, :], 1.0 / D),
                            inc=False)
                        r.v(lambda E: E.tensor_tensor(rv[:, :], sm[:, :], sm[:, :], OP.mult),
                            inc=False)
                        r.v(lambda E: E.tensor_tensor(rv[:, :], ss[:, :], rv[:, :],
                                                      OP.subtract), inc=False)
                        ev_v = r.v(lambda E: E.tensor_scalar_add(rv[:, :], rv[:, :], LN_EPS))
                        ev_s = r.s(lambda E: E.activation(rv[:, :], rv[:, :], ACT.Sqrt),
                                   wait=[ev_v])
                        r.v(lambda E: E.reciprocal(rv[:, :], rv[:, :]), wait=[ev_s], inc=False)
                        r.v(lambda E, tt=tt: E.tensor_scalar(
                            yt[tt % 2][:, :], yt[tt % 2][:, :], sm[:, 0:1], rv[:, 0:1],
                            OP.subtract, OP.mult), inc=False)
                        r.v(lambda E, tt=tt: E.tensor_tensor(
                            yt[tt % 2][:, :], yt[tt % 2][:, :], bc_lng[:, :], OP.mult), inc=False)
                        ev_y = r.v(lambda E, tt=tt: E.tensor_tensor(
                            yt[tt % 2][:, :], yt[tt % 2][:, :], bc_lnb[:, :], OP.add))
                        ev_ytile[tt] = ev_y
                        ev_done[tt] = r.dma("scalar", s0_tf[gt * 128:(gt + 1) * 128, :],
                                            yt[tt % 2][:, :], wait=[ev_y],
                                            ch=f"p1s{tt % 2}")
                    r.fence()
                    r.flush(block)

        if stop_after not in ("p0", "p1"):
            _phase_l0ffn(nc, b, block, Rec, attn_batch, loc=locals())
            if stop_after != "p2":
                _phase_hawk(nc, b, block, Rec, loc=locals())
                if stop_after != "p3":
                    _phase_l1ffn(nc, b, block, Rec, attn_batch, loc=locals())
                    if stop_after != "p4":
                        _phase_final(nc, b, block, Rec, loc=locals())

    return nc


class _G:
    def __init__(self, loc):
        self.__dict__.update(loc)


def _phase_l0ffn(nc, b, block, Rec, attn_batch, loc):
    g = _G(loc)
    debug = loc["debug"]
    loc_dbg = {k: loc[k] for k in ("dbg_xq", "dbg_wgq", "dbg_wuq", "dbg_sc")}
    loc_sc = None
    with contextlib.ExitStack() as pb:
        A = lambda name, shape, dt=F32: pb.enter_context(nc.sbuf_tensor(name, shape, dt))
        wgq_sb = A("p2wg", [128, CT * I_FFN], BF16)
        wuq_sb = A("p2wu", [128, CT * I_FFN], BF16)
        xq_sb = A("p2xq", [128, CT * T], BF16)
        loc_sc = A("p2dsc", [128, 2 * ITL + 2]) if "dbg_xq" in debug else None
        with contextlib.ExitStack() as px:
            A2 = lambda name, shape, dt=F32: px.enter_context(nc.sbuf_tensor(name, shape, dt))
            nat = [A2(f"p2n{i}", [128, D]) for i in range(2)]
            i8t = A2("p2i8", [128, D], I8)
            bfst = A2("p2bf", [128, D], BF16)
            red = A2("p2red", [128, 1])
            rr = A2("p2rr", [128, 2])
            pst = [px.enter_context(nc.psum_tensor(f"p2ps{i}", [128, 128], BF16))
                   for i in range(2)]
            r = Rec(b)
            r.fence()
            psg = [None, None]
            ld_g = {}
            eva_g = {}
            for wi, (nat_src, dst_sb, sc_dst) in enumerate(
                    ((g.w_gate, wgq_sb, g.sgt), (g.w_up, wuq_sb, g.sut))):
                for itl in range(ITL):
                    k = wi * ITL + itl
                    ld = r.dma("sync", nat[k % 2][:, :],
                               ftap(nat_src, D, itl * 128, 128, 0, D),
                               wait=[ld_g.get(k - 2)], ch=f"p2n{k % 2}")
                    r.v(lambda E, k=k: E.tensor_reduce(red[:, :], nat[k % 2][:, :], axis=AX,
                                                       op=OP.max, apply_absolute_value=True),
                        wait=[ld], inc=False)
                    evr = r.v(lambda E, k=k: E.reciprocal(rr[:, k % 2:k % 2 + 1], red[:, :]),
                              wait=[eva_g.get(k - 2)], inc=True)
                    r.v(lambda E, k=k: E.tensor_scalar_mul(rr[:, k % 2:k % 2 + 1],
                                                           rr[:, k % 2:k % 2 + 1], 127.0),
                        inc=False)
                    evm = r.v(lambda E: E.engine_nop())
                    eva = r.s(lambda E, k=k: E.activation(
                        nat[k % 2][:, :], nat[k % 2][:, :], ACT.Copy,
                        scale=rr[:, k % 2:k % 2 + 1], bias=MAGIC_RND), wait=[evm])
                    eva_g[k] = eva
                    r.v(lambda E, k=k: E.tensor_scalar(nat[k % 2][:, :], nat[k % 2][:, :],
                                                       -MAGIC_RND, -128.0, OP.add, OP.max),
                        wait=[eva], inc=False)
                    ev_bf = r.v(lambda E, k=k: E.tensor_scalar_min(bfst[:, :],
                                                                   nat[k % 2][:, :], 127.0))
                    ld_g[k] = ev_bf
                    r.v(lambda E, sc_dst=sc_dst, itl=itl: E.tensor_scalar_mul(
                        sc_dst[:, itl:itl + 1], red[:, :], 1.0 / 127.0), inc=False)
                    for ct in range(CT):
                        evt = r.t(lambda E, ct=ct: E.transpose(
                            pst[ct % 2][:, :], bfst[:, ct * 128:(ct + 1) * 128],
                            g.sb_idb[:, :]), wait=[ev_bf, psg[ct % 2]])
                        psg[ct % 2] = r.v(lambda E, dst_sb=dst_sb, ct=ct, itl=itl: E.tensor_copy(
                            dst_sb[:, ct * I_FFN + itl * 128: ct * I_FFN + (itl + 1) * 128],
                            pst[ct % 2][:, :]), wait=[evt])
            r.v(lambda E: E.tensor_scalar_mul(g.sgx[:, :], g.sgt[:, :], g.sig1s[:, 0:1]),
                wait=[psg[0], psg[1]], inc=False)
            r.v(lambda E: E.tensor_scalar_mul(g.sux[:, :], g.sut[:, :], g.sig1s[:, 0:1]),
                inc=False)
            ev_q = {}
            for k in range(4 * CT):
                ct, qt = k // 4, k % 4
                ld = r.dma("sync", nat[k % 2][:, :],
                           ftap(g.xnT, T, ct * 128, 128, qt * 1024, 1024),
                           wait=[ev_q.get(k - 2, ld_g.get(2 * ITL - 2 + (k % 2)))],
                           ch=f"p2n{k % 2}")
                eva = r.s(lambda E, k=k: E.activation(
                    nat[k % 2][:, :], nat[k % 2][:, :], ACT.Copy,
                    scale=g.sig1r[:, 0:1], bias=MAGIC_RND), wait=[ld])
                r.v(lambda E, k=k: E.tensor_scalar(
                    nat[k % 2][:, :], nat[k % 2][:, :], -MAGIC_RND, -128.0,
                    OP.add, OP.max), wait=[eva], inc=False)
                ev_q[k] = r.v(lambda E, k=k, ct=ct, qt=qt: E.tensor_scalar_min(
                    xq_sb[:, ct * T + qt * 1024: ct * T + (qt + 1) * 1024],
                    nat[k % 2][:, :], 127.0))
            r.fence()
            r.flush(block)

        if "dbg_xq" in debug:
            r = Rec(b); r.fence()
            r.dma("gpsimd", loc_dbg["dbg_xq"], xq_sb[:, :], ch="dbg")
            r.dma("gpsimd", loc_dbg["dbg_wgq"], wgq_sb[:, :], ch="dbg")
            r.dma("gpsimd", loc_dbg["dbg_wuq"], wuq_sb[:, :], ch="dbg")
            r.v(lambda E: E.tensor_copy(loc_sc[:, 0:ITL], g.sgt[:, :]))
            r.v(lambda E: E.tensor_copy(loc_sc[:, ITL:2 * ITL], g.sut[:, :]))
            r.v(lambda E: E.tensor_copy(loc_sc[:, 2 * ITL:2 * ITL + 1], g.sig1s[:, :]))
            ev = r.v(lambda E: E.tensor_copy(loc_sc[:, 2 * ITL + 1:2 * ITL + 2],
                                             g.sig1r[:, :]))
            r.dma("gpsimd", loc_dbg["dbg_sc"], loc_sc[:, :], wait=[ev], ch="dbg")
            r.fence(); r.flush(block)
        with contextlib.ExitStack() as px:
            A2 = lambda name, shape, dt=F32: px.enter_context(nc.sbuf_tensor(name, shape, dt))
            hstrip = [A2(f"p2h{i}", [128, 512]) for i in range(2)]
            silt = A2("p2sil", [128, 512])
            red = A2("p2red2", [128, 1])
            psG = [px.enter_context(nc.psum_tensor(f"p2pg{i}", [128, 512], F32)) for i in range(2)]
            psU = [px.enter_context(nc.psum_tensor(f"p2pu{i}", [128, 512], F32)) for i in range(2)]
            r = Rec(b)
            r.fence()
            x3 = xq_sb.ap().rearrange("p (c w) -> p c w", w=T)
            wg3 = wgq_sb.ap().rearrange("p (c i) -> p c i", i=I_FFN)
            wu3 = wuq_sb.ap().rearrange("p (c i) -> p c i", i=I_FFN)
            st_g = {}
            gg_ = [None, None]
            gu_ = [None, None]
            silt_g = None
            for itl in range(ITL):
                for ch in range(8):
                    k = itl * 8 + ch
                    rot = ch % 2
                    for ct in range(CT):
                        r.t(lambda E, ct=ct, itl=itl, ch=ch, rot=rot: E.matmul(
                            psG[rot][:, :], wg3[:, ct, itl * 128:(itl + 1) * 128],
                            x3[:, ct, ch * 512:(ch + 1) * 512],
                            start=(ct == 0), stop=(ct == CT - 1)),
                            wait=[gg_[rot]], inc=(ct == CT - 1))
                    mmg = ("eT", r.c["eT"])
                    for ct in range(CT):
                        r.t(lambda E, ct=ct, itl=itl, ch=ch, rot=rot: E.matmul(
                            psU[rot][:, :], wu3[:, ct, itl * 128:(itl + 1) * 128],
                            x3[:, ct, ch * 512:(ch + 1) * 512],
                            start=(ct == 0), stop=(ct == CT - 1)),
                            wait=[gu_[rot]], inc=(ct == CT - 1))
                    mmu = ("eT", r.c["eT"])
                    ev_sil = r.s(lambda E, itl=itl, rot=rot: E.activation(
                        silt[:, :], psG[rot][:, :], ACT.Silu, scale=g.sgx[:, itl:itl + 1]),
                        wait=[mmg, silt_g])
                    gg_[rot] = ev_sil
                    ev_h = r.v(lambda E, k=k, itl=itl, rot=rot: E.scalar_tensor_tensor(
                        hstrip[k % 2][:, :], psU[rot][:, :],
                        g.sux[:, itl:itl + 1], silt[:, :], OP.mult, OP.mult),
                        wait=[mmu, ev_sil, st_g.get(k - 2)])
                    gu_[rot] = ev_h
                    silt_g = ev_h
                    r.v(lambda E, k=k: E.tensor_reduce(red[:, :], hstrip[k % 2][:, :], axis=AX,
                                                       op=OP.max, apply_absolute_value=True),
                        inc=False)
                    evm = r.v(lambda E: E.tensor_max(g.rmax2[:, :], g.rmax2[:, :], red[:, :]))
                    st_g[k] = r.dma("scalar",
                                    ftap(g.hpre, T, itl * 128, 128, ch * 512, 512),
                                    hstrip[k % 2][:, :], wait=[evm], ch=f"p2h{k % 2}")
            r.fence()
            r.flush(block)

    loc["quant_scale_phase"](g.rmax2, g.cc_io[2], g.cc_io[3], g.clip2, g.sig2s, g.sig2r,
                             g.m128b)
    with contextlib.ExitStack() as pb:
        A = lambda name, shape, dt=F32: pb.enter_context(nc.sbuf_tensor(name, shape, dt))
        wdq_sb = A("p2wd", [128, ITL * D], BF16)
        with contextlib.ExitStack() as px:
            A2 = lambda name, shape, dt=F32: px.enter_context(nc.sbuf_tensor(name, shape, dt))
            nat = [A2(f"p2dn{i}", [128, I_FFN]) for i in range(2)]
            i8t = A2("p2di8", [128, I_FFN], I8)
            bfst = A2("p2dbf", [128, I_FFN], BF16)
            red = A2("p2dr", [128, 1]); rr = A2("p2drr", [128, 2])
            pst = [px.enter_context(nc.psum_tensor(f"p2dps{i}", [128, 128], BF16))
                   for i in range(2)]
            hq8 = A2("p2hq8", [128, T], I8)
            hqb = A2("p2hqb", [128, T], BF16)
            r = Rec(b)
            r.fence()
            psg = [None, None]
            ld_g = {}
            eva_g = {}
            ev_bf = None
            for ht in range(CT):
                ld = r.dma("sync", nat[ht % 2][:, :],
                           ftap(g.w_down, I_FFN, ht * 128, 128, 0, I_FFN),
                           wait=[ld_g.get(ht - 2)], ch=f"p2n{ht % 2}")
                r.v(lambda E, ht=ht: E.tensor_reduce(red[:, :], nat[ht % 2][:, :], axis=AX,
                                                     op=OP.max, apply_absolute_value=True),
                    wait=[ld], inc=False)
                r.v(lambda E, ht=ht: E.reciprocal(rr[:, ht % 2:ht % 2 + 1], red[:, :]),
                    wait=[eva_g.get(ht - 2)], inc=False)
                r.v(lambda E, ht=ht: E.tensor_scalar_mul(rr[:, ht % 2:ht % 2 + 1],
                                                         rr[:, ht % 2:ht % 2 + 1], 127.0),
                    inc=False)
                evm = r.v(lambda E: E.engine_nop())
                eva = r.s(lambda E, ht=ht: E.activation(
                    nat[ht % 2][:, :], nat[ht % 2][:, :], ACT.Copy,
                    scale=rr[:, ht % 2:ht % 2 + 1], bias=MAGIC_RND), wait=[evm])
                eva_g[ht] = eva
                r.v(lambda E, ht=ht: E.tensor_scalar(nat[ht % 2][:, :], nat[ht % 2][:, :],
                                                     -MAGIC_RND, -128.0, OP.add, OP.max),
                    wait=[eva], inc=False)
                ev_bf = r.v(lambda E, ht=ht: E.tensor_scalar_min(bfst[:, :],
                                                                 nat[ht % 2][:, :], 127.0))
                ld_g[ht] = ev_bf
                r.v(lambda E, ht=ht: E.tensor_scalar_mul(g.sdt[:, ht:ht + 1], red[:, :],
                                                         1.0 / 127.0), inc=False)
                for itl in range(ITL):
                    evt = r.t(lambda E, itl=itl: E.transpose(
                        pst[itl % 2][:, :], bfst[:, itl * 128:(itl + 1) * 128],
                        g.sb_idb[:, :]), wait=[ev_bf, psg[itl % 2]])
                    psg[itl % 2] = r.v(lambda E, itl=itl, ht=ht: E.tensor_copy(
                        wdq_sb[:, itl * D + ht * 128: itl * D + (ht + 1) * 128],
                        pst[itl % 2][:, :]), wait=[evt])
            ev1 = r.dma("scalar", bass.AP(g.sd_row.tensor, 0, [[1, 128], [128, CT]]),
                        g.sdt[:, 0:CT], wait=[psg[0], psg[1], ev_bf],
                        allow_slow_non_contiguous=True, ch="qs2")
            ev2 = r.dma("sync", g.bc_sd[:, :],
                        bass.AP(g.sd_row.tensor, 0, [[0, 128], [1, D]]), wait=[ev1],
                        ch="qs3")
            r.v(lambda E: E.tensor_scalar_mul(g.bc_sd[:, :], g.bc_sd[:, :], g.sig2s[:, 0:1]),
                wait=[ev2], inc=False)
            r.fence()
            r.flush(block)
        with contextlib.ExitStack() as px:
            A2 = lambda name, shape, dt=F32: px.enter_context(nc.sbuf_tensor(name, shape, dt))
            hpq = [A2(f"p2hp{i}", [128, ITL * 128]) for i in range(2)]
            hqblk = [A2(f"p2k{i}", [128, ITL * 512], BF16) for i in range(2)]
            ut = [A2(f"p2u{i}", [128, D]) for i in range(2)]
            gt32 = A2("p2g32", [128, D])
            xt_ = [A2(f"p2xx{i}", [128, D]) for i in range(2)]
            st_ = [A2(f"p2ss{i}", [128, D]) for i in range(2)]
            psD = [px.enter_context(nc.psum_tensor(f"p2pd{i}", [128, 512], F32))
                   for i in range(2)]
            r = Rec(b)
            r.fence()
            wd3 = wdq_sb.ap().rearrange("p (i f) -> p i f", f=D)
            ev_x1 = {}
            psg = [None, None]
            ld_g = {}
            qz_g = {}
            for tb in range(8):
                qdone = {}
                h3 = hqblk[tb % 2].ap().rearrange("p (i w) -> p i w", w=512)
                for q in range(4):
                    j = tb * 4 + q
                    ldh = r.dma("gpsimd",
                                hpq[j % 2][:, :].rearrange("p (i w) -> p i w", w=128),
                                ftap(g.hpre, T, 0, 128, tb * 512 + q * 128, 128,
                                     nblk=ITL, blk_stride=128 * T),
                                wait=[qz_g.get(j - 2)], ch=f"p2k{j % 2}")
                    eq1 = r.s(lambda E, j=j: E.activation(
                        hpq[j % 2][:, :], hpq[j % 2][:, :], ACT.Copy,
                        scale=g.sig2r[:, 0:1], bias=MAGIC_RND), wait=[ldh])
                    r.v(lambda E, j=j: E.tensor_scalar(
                        hpq[j % 2][:, :], hpq[j % 2][:, :], -MAGIC_RND, -128.0,
                        OP.add, OP.max), wait=[eq1, ld_g.get(tb - 2)],
                        inc=False)
                    ev = r.v(lambda E, j=j, q=q, h3=h3: E.tensor_scalar_min(
                        h3[:, :, q * 128:(q + 1) * 128],
                        hpq[j % 2][:, :].rearrange("p (i w) -> p i w", w=128), 127.0))
                    qz_g[j] = ev
                    qdone[q] = ev
                for t4 in range(4):
                    gt = tb * 4 + t4
                    ldx = r.dma("sync", xt_[gt % 2][:, :],
                                g.x_in[gt * 128:(gt + 1) * 128, :], wait=[ev_x1.get(gt - 2)],
                                ch=f"p2x{gt % 2}")
                    lds = r.dma("sync", st_[gt % 2][:, :],
                                g.s0_tf[gt * 128:(gt + 1) * 128, :], wait=[ev_x1.get(gt - 2)],
                                ch=f"p2s{gt % 2}")
                    ev_u = []
                    for fch in range(2):
                        for itl in range(ITL):
                            r.t(lambda E, itl=itl, t4=t4, fch=fch, h3=h3: E.matmul(
                                psD[fch][:, :], h3[:, itl, t4 * 128:(t4 + 1) * 128],
                                wd3[:, itl, fch * 512:(fch + 1) * 512],
                                start=(itl == 0), stop=(itl == ITL - 1)),
                                wait=[qdone[t4], psg[fch]], inc=(itl == ITL - 1))
                        mmd = ("eT", r.c["eT"])
                        evu = r.v(lambda E, gt=gt, fch=fch: E.scalar_tensor_tensor(
                            ut[gt % 2][:, fch * 512:(fch + 1) * 512], psD[fch][:, :], 0.0,
                            g.bc_sd[:, fch * 512:(fch + 1) * 512], OP.add, OP.mult),
                            wait=[mmd, ev_x1.get(gt - 2)])
                        ev_u.append(evu)
                        psg[fch] = evu
                    ev_g = r.s(lambda E, gt=gt: E.activation(gt32[:, :], ut[gt % 2][:, :],
                                                             ACT.Silu),
                               wait=ev_u + [ev_x1.get(gt - 1)])
                    r.v(lambda E, gt=gt: E.tensor_tensor(gt32[:, :], st_[gt % 2][:, :],
                                                         gt32[:, :], OP.mult),
                        wait=[ev_g, lds], inc=False)
                    ev_v = r.v(lambda E, gt=gt: E.tensor_tensor(
                        xt_[gt % 2][:, :], xt_[gt % 2][:, :], gt32[:, :], OP.add), wait=[ldx])
                    ev_x1[gt] = r.dma("scalar", g.x1_tf[gt * 128:(gt + 1) * 128, :],
                                      xt_[gt % 2][:, :], wait=[ev_v], ch=f"p2y{gt % 2}")
                ld_g[tb] = ev_x1[tb * 4 + 3]
            r.fence()
            r.flush(block)

    loc["ln_two_pass"](g.x1_tf, g.xn1_tf, g.xn1T, absmax_to=None, h1_also=True)


def _phase_hawk(nc, b, block, Rec, loc):
    g = _G(loc)
    with contextlib.ExitStack() as pb:
        A = lambda name, shape, dt=F32: pb.enter_context(nc.sbuf_tensor(name, shape, dt))
        xc_sb = A("p3xc", [128, CT * T], BF16)
        with contextlib.ExitStack() as px:
            A2 = lambda name, shape, dt=F32: px.enter_context(nc.sbuf_tensor(name, shape, dt))
            win_sb = A2("p3wi", [128, CT * 2 * D], BF16)
            xch = [A2(f"p3x{i}", [128, CT * 512], BF16) for i in range(2)]
            ggbuf = [A2(f"p3gg{i}", [128, 8 * 512], BF16) for i in range(2)]
            xap = A2("p3xa", [128, 515])
            tails = A2("p3tl", [128, CT * 3])
            acc = [A2(f"p3ac{i}", [128, 512]) for i in range(2)]
            ps_ = [px.enter_context(nc.psum_tensor(f"p3p{i}", [128, 512], F32))
                   for i in range(2)]
            r = Rec(b)
            r.fence()
            ev_w = wsplit(r, win_sb, g.w_inT, 2 * D, 2 * D, CT, 128 * 2 * D, "p3w")
            w3 = win_sb.ap().rearrange("p (c f) -> p c f", f=2 * D)
            psg = [None, None]
            ld_g = {}
            gg_st = {}
            for ch in range(8):
                ld = r.dma("gpsimd", xch[ch % 2][:, :].rearrange("p (c w) -> p c w", w=512),
                           ftap(g.xn1T, T, 0, 128, ch * 512, 512, nblk=CT, blk_stride=128 * T),
                           wait=[ld_g.get(ch - 2)], ch=f"p3x{ch % 2}")
                x3 = xch[ch % 2].ap().rearrange("p (c w) -> p c w", w=512)
                last_ev = None
                for ft in range(16):
                    rot = ft % 2
                    for ct in range(CT):
                        r.t(lambda E, ct=ct, ft=ft, rot=rot, x3=x3: E.matmul(
                            ps_[rot][:, :], w3[:, ct, ft * 128:(ft + 1) * 128], x3[:, ct, :],
                            start=(ct == 0), stop=(ct == CT - 1)),
                            wait=[ld, ev_w((ft + 1) * 128), psg[rot]], inc=(ct == CT - 1))
                    mm = ("eT", r.c["eT"])
                    if ft < 8:
                        ev = r.s(lambda E, ft=ft, ch=ch, rot=rot: E.activation(
                            ggbuf[ch % 2][:, ft * 512:(ft + 1) * 512], ps_[rot][:, :],
                            ACT.Gelu), wait=[mm, gg_st.get(ch - 2)])
                        psg[rot] = ev
                        gg_last = ev
                    else:
                        et = ft - 8
                        ev0 = r.v(lambda E, rot=rot: E.tensor_copy(xap[:, 3:515], ps_[rot][:, :]),
                                  wait=[mm, last_ev])
                        psg[rot] = ev0
                        if ch % 4 == 0:
                            r.v(lambda E: E.memset(xap[:, 0:3], 0.0), inc=False)
                        else:
                            r.v(lambda E, et=et: E.tensor_copy(
                                xap[:, 0:3], tails[:, et * 3:(et + 1) * 3]), inc=False)
                        r.v(lambda E, et=et: E.tensor_copy(
                            tails[:, et * 3:(et + 1) * 3], xap[:, 512:515]), inc=False)
                        wc4 = g.wconv_c.ap().rearrange("p (c k) -> p c k", k=4)
                        r.v(lambda E, et=et: E.tensor_scalar_mul(
                            acc[0][:, :], xap[:, 0:512], wc4[:, et, 0:1]), inc=False)
                        r.v(lambda E, et=et: E.scalar_tensor_tensor(
                            acc[1][:, :], xap[:, 1:513], wc4[:, et, 1:2], acc[0][:, :],
                            OP.mult, OP.add), inc=False)
                        r.v(lambda E, et=et: E.scalar_tensor_tensor(
                            acc[0][:, :], xap[:, 2:514], wc4[:, et, 2:3], acc[1][:, :],
                            OP.mult, OP.add), inc=False)
                        r.v(lambda E, et=et: E.scalar_tensor_tensor(
                            acc[1][:, :], xap[:, 3:515], wc4[:, et, 3:4], acc[0][:, :],
                            OP.mult, OP.add), inc=False)
                        last_ev = r.v(lambda E, et=et, ch=ch: E.tensor_single_scalar(
                            xc_sb[:, et * T + ch * 512: et * T + (ch + 1) * 512],
                            acc[1][:, :], g.colv_bc[:, et:et + 1], OP.add))
                ld_g[ch] = last_ev
                gg_st[ch] = r.dma("scalar",
                                  ftap(g.gg_d, T, 0, 128, ch * 512, 512, nblk=8,
                                       blk_stride=128 * T),
                                  ggbuf[ch % 2][:, :].rearrange("p (f w) -> p f w", w=512),
                                  wait=[psg[0], psg[1], gg_last], ch=f"p3g{ch % 2}")
            r.fence()
            r.flush(block)

        with contextlib.ExitStack() as px:
            A2 = lambda name, shape, dt=F32: px.enter_context(nc.sbuf_tensor(name, shape, dt))
            wgs = [A2(f"p3wg{i}", [128, CT * 256], BF16) for i in range(2)]
            ggs = A2("p3ggs", [128, T], BF16)
            ys = A2("p3ys", [128, T], BF16)
            alph = [A2(f"p3al{i}", [128, 8 * 512]) for i in range(2)]
            sgi = [A2(f"p3si{i}", [128, 8 * 512], BF16) for i in range(2)]
            sq8 = A2("p3sq", [128, 4 * 512])
            bet8 = A2("p3be", [128, 8 * 512], BF16)
            ub = A2("p3ub", [128, 512])
            hb = [A2(f"p3h{i}", [128, 512]) for i in range(2)]
            psF = [px.enter_context(nc.psum_tensor(f"p3pf{i}", [128, 512], F32))
                   for i in range(2)]
            psI = [px.enter_context(nc.psum_tensor(f"p3pi{i}", [128, 512], F32))
                   for i in range(2)]
            r = Rec(b)
            r.fence()
            xc3 = xc_sb.ap().rearrange("p (c w) -> p c w", w=T)
            y_st = {}
            gg_ld = {}
            pf_g = [None, None]
            pi_g = [None, None]
            scan_ev = {}
            ub_ev = {}
            sqr_ev = {}
            mm_done = {}
            ld_w = {}
            for gi in range(8):
                gp = gi % 2
                w4 = wgs[gp].ap().rearrange("p (c h f) -> p c h f", h=2, f=128)
                ev_wa = r.dma("gpsimd", w4[:, :, 0, :],
                              ftap(g.w_gatesT, 2 * D, 0, 128, gi * 128, 128, nblk=CT,
                                   blk_stride=128 * 2 * D),
                              wait=[mm_done.get(gi - 2)], ch=f"p3w{gp}")
                ev_w = r.dma("gpsimd", w4[:, :, 1, :],
                             ftap(g.w_gatesT, 2 * D, 0, 128, (8 + gi) * 128, 128,
                                  nblk=CT, blk_stride=128 * 2 * D), ch=f"p3w{gp}")
                ldg = r.dma("sync", ggs[:, :], ftap(g.gg_d, T, gi * 128, 128, 0, T),
                            wait=[gg_ld.get(gi - 1)], ch="p3gg")
                sigf_ev = {}
                sigi_ev = {}
                for k in range(8):
                    t0 = (k // 4) * S + (k % 4) * 512
                    rot = k % 2
                    for ct in range(CT):
                        r.t(lambda E, ct=ct, w4=w4, t0=t0, rot=rot: E.matmul(
                            psF[rot][:, :], w4[:, ct, 0, :],
                            xc3[:, ct, t0:t0 + 512],
                            start=(ct == 0), stop=(ct == CT - 1)),
                            wait=[ev_wa, ev_w, pf_g[rot]], inc=(ct == CT - 1))
                    mmf = ("eT", r.c["eT"])
                    for ct in range(CT):
                        r.t(lambda E, ct=ct, w4=w4, t0=t0, rot=rot: E.matmul(
                            psI[rot][:, :], w4[:, ct, 1, :],
                            xc3[:, ct, t0:t0 + 512],
                            start=(ct == 0), stop=(ct == CT - 1)),
                            wait=[ev_w, pi_g[rot]], inc=(ct == CT - 1))
                    mmi = ("eT", r.c["eT"])
                    if k == 7:
                        mm_done[gi] = mmi
                    e1 = r.s(lambda E, gi=gi, k=k, gp=gp, rot=rot: E.activation(
                        alph[gp][:, k * 512:(k + 1) * 512], psF[rot][:, :], ACT.Sigmoid,
                        bias=g.colv_bgf[:, gi:gi + 1]),
                        wait=[mmf, scan_ev.get((gi - 2, k))])
                    sigf_ev[k] = e1
                    pf_g[rot] = e1
                    e3 = r.s(lambda E, gi=gi, k=k, gp=gp, rot=rot: E.activation(
                        sgi[gp][:, k * 512:(k + 1) * 512], psI[rot][:, :], ACT.Sigmoid,
                        bias=g.colv_bgi[:, gi:gi + 1]),
                        wait=[mmi, ub_ev.get((gi - 2, k))])
                    sigi_ev[k] = e3
                    pi_g[rot] = e3
                exp_ev = {}
                for k in range(8):
                    exp_ev[k] = r.s(lambda E, gi=gi, k=k, gp=gp: E.activation(
                        alph[gp][:, k * 512:(k + 1) * 512],
                        alph[gp][:, k * 512:(k + 1) * 512], ACT.Exp,
                        scale=g.colv_ce[:, gi:gi + 1]), wait=[sigf_ev[k]])
                sq_ev = {}
                for k in range(8):
                    ks = k % 4
                    e4 = r.s(lambda E, k=k, ks=ks, gp=gp: E.activation(
                        sq8[:, ks * 512:(ks + 1) * 512],
                        alph[gp][:, k * 512:(k + 1) * 512], ACT.Square),
                        wait=[exp_ev[k],
                              sqr_ev.get((gi, k - 4), sqr_ev.get((gi - 1, k + 4)))])
                    sq_ev[k] = r.v(lambda E, ks=ks: E.tensor_scalar(
                        sq8[:, ks * 512:(ks + 1) * 512], sq8[:, ks * 512:(ks + 1) * 512],
                        -1.0, 1.0 + 1e-6, OP.mult, OP.add), wait=[e4])
                    sqr_ev[(gi, k)] = r.s(lambda E, k=k, ks=ks: E.activation(
                        bet8[:, k * 512:(k + 1) * 512], sq8[:, ks * 512:(ks + 1) * 512],
                        ACT.Sqrt), wait=[sq_ev[k], ub_ev.get((gi - 1, k))])
                ev_y = None
                for k in range(8):
                    t0 = (k // 4) * S + (k % 4) * 512
                    e7 = r.v(lambda E, k=k, gp=gp: E.tensor_tensor(
                        ub[:, :], bet8[:, k * 512:(k + 1) * 512],
                        sgi[gp][:, k * 512:(k + 1) * 512], OP.mult),
                        wait=[sqr_ev[(gi, k)], sigi_ev[k]])
                    ub_ev[(gi, k)] = e7
                    r.v(lambda E, gi=gi, t0=t0: E.tensor_tensor(
                        ub[:, :], ub[:, :], xc3[:, gi, t0:t0 + 512], OP.mult),
                        inc=False)
                    init = 0.0 if k % 4 == 0 else hb[(k - 1) % 2][:, 511:512]
                    e9 = r.v(lambda E, k=k, gp=gp, init=init: E.tensor_tensor_scan(
                        hb[k % 2][:, :], alph[gp][:, k * 512:(k + 1) * 512], ub[:, :],
                        init, OP.mult, OP.add))
                    scan_ev[(gi, k)] = e9
                    ev_y = r.g(lambda E, gi=gi, t0=t0, k=k: E.tensor_tensor(
                        ys[:, t0:t0 + 512], ggs[:, t0:t0 + 512], hb[k % 2][:, :],
                        OP.mult), wait=[ldg, y_st.get(gi - 1), e9])
                gg_ld[gi] = ev_y
                y_st[gi] = r.dma("scalar", ftap(g.y_hk, T, gi * 128, 128, 0, T), ys[:, :],
                                 wait=[ev_y], ch="p3y")
            r.fence()
            r.flush(block)

        with contextlib.ExitStack() as px:
            A2 = lambda name, shape, dt=F32: px.enter_context(nc.sbuf_tensor(name, shape, dt))
            wo_sb = A2("p3wo", [128, CT * D], BF16)
            yblk = [A2(f"p3yb{i}", [128, CT * 512], BF16) for i in range(2)]
            s1t = [A2(f"p3s1{i}", [128, D]) for i in range(2)]
            psO = [px.enter_context(nc.psum_tensor(f"p3po{i}", [128, 512], F32))
                   for i in range(2)]
            r = Rec(b)
            r.fence()
            ev_w = wsplit(r, wo_sb, g.w_outT, D, D, CT, 128 * D, "p3w")
            wo3 = wo_sb.ap().rearrange("p (c f) -> p c f", f=D)
            st_g = {}
            ld_g = {}
            psg = [None, None]
            for tb in range(8):
                ld = r.dma("gpsimd", yblk[tb % 2][:, :].rearrange("p (c w) -> p c w", w=512),
                           ftap(g.y_hk, T, 0, 128, tb * 512, 512, nblk=CT, blk_stride=128 * T),
                           wait=[ld_g.get(tb - 2)], ch=f"p3b{tb % 2}")
                y3 = yblk[tb % 2].ap().rearrange("p (c w) -> p c w", w=512)
                for t4 in range(4):
                    gt = tb * 4 + t4
                    evs = []
                    for fch in range(2):
                        for et in range(CT):
                            r.t(lambda E, et=et, t4=t4, fch=fch, y3=y3: E.matmul(
                                psO[fch][:, :], y3[:, et, t4 * 128:(t4 + 1) * 128],
                                wo3[:, et, fch * 512:(fch + 1) * 512],
                                start=(et == 0), stop=(et == CT - 1)),
                                wait=[ld, ev_w((fch + 1) * 512), psg[fch]],
                                inc=(et == CT - 1))
                        mm = ("eT", r.c["eT"])
                        ev = r.v(lambda E, gt=gt, fch=fch: E.tensor_copy(
                            s1t[gt % 2][:, fch * 512:(fch + 1) * 512], psO[fch][:, :]),
                            wait=[mm, st_g.get(gt - 2)])
                        evs.append(ev)
                        psg[fch] = ev
                    st_g[gt] = r.dma("scalar", g.s1_tf[gt * 128:(gt + 1) * 128, :],
                                     s1t[gt % 2][:, :], wait=evs, ch=f"p3s{gt % 2}")
                ld_g[tb] = st_g[tb * 4 + 3]
            r.fence()
            r.flush(block)


def _phase_l1ffn(nc, b, block, Rec, attn_batch, loc):
    g = _G(loc)
    for bi in range(2):
        with contextlib.ExitStack() as pb:
            A = lambda name, shape, dt=F32: pb.enter_context(nc.sbuf_tensor(name, shape, dt))
            qT_sb = A("p4q", [128, 8 * S], BF16)
            kT_sb = A("p4k", [128, 8 * S], BF16)
            v_sb = A("p4v", [128, NTB * 16 * 65], BF16)
            with contextlib.ExitStack() as px:
                A2 = lambda name, shape, dt=F32: px.enter_context(nc.sbuf_tensor(name, shape, dt))
                w_sb = A2("p4w", [128, CT * D], BF16)
                hch = [A2(f"p4h{i}", [128, CT * 512], BF16) for i in range(2)]
                qtf = A2("p4qt", [128, D])
                q2 = A2("p4q2", [128, D])
                sums = A2("p4sm", [128, 16])
                qb16 = A2("p4qb", [128, D], BF16)
                ps_q = [px.enter_context(nc.psum_tensor(f"p4pq{i}", [128, 512], F32))
                        for i in range(2)]
                ps_t = [px.enter_context(nc.psum_tensor(f"p4pt{i}", [128, 128], BF16))
                        for i in range(2)]
                r = Rec(b)
                r.fence()
                ev_ones = r.v(lambda E: E.memset(
                    v_sb.ap().rearrange("p (j h d) -> p j h d", h=16, d=65)[:, :, :, 64:65],
                    1.0))
                w3 = w_sb.ap().rearrange("p (c f) -> p c f", f=D)
                psq_g = [None, None]
                pst_g = [None, None]
                last_w_use = None
                ev_tile = None
                for which, srcw in enumerate((g.l1wqT, g.l1wkT, g.l1wvT)):
                    ev_w = wsplit(r, w_sb, srcw, D, D, CT, 128 * D, "p4w",
                                  wait=[last_w_use])
                    ld_g = {}
                    for ch in range(4):
                        ld = r.dma("gpsimd",
                                   hch[ch % 2][:, :].rearrange("p (c w) -> p c w", w=512),
                                   ftap(g.h1T_d, T, 0, 128, bi * S + ch * 512, 512,
                                        nblk=CT, blk_stride=128 * T), wait=[ld_g.get(ch - 2)],
                                   ch=f"p4h{ch % 2}")
                        h3 = hch[ch % 2].ap().rearrange("p (c w) -> p c w", w=512)
                        for t4 in range(4):
                            jt = ch * 4 + t4
                            if which < 2:
                                bcg, dstT = ((g.bc_qn, qT_sb) if which == 0
                                             else (g.bc_kn, kT_sb))
                                evf = []
                                for fch in range(2):
                                    for ct in range(CT):
                                        r.t(lambda E, h3=h3, ct=ct, t4=t4, fch=fch: E.matmul(
                                            ps_q[fch][:, :],
                                            h3[:, ct, t4 * 128:(t4 + 1) * 128],
                                            w3[:, ct, fch * 512:(fch + 1) * 512],
                                            start=(ct == 0), stop=(ct == CT - 1)),
                                            wait=[ld, ev_w((fch + 1) * 512), psq_g[fch]],
                                            inc=(ct == CT - 1))
                                    mm = ("eT", r.c["eT"])
                                    ev = r.v(lambda E, fch=fch: E.tensor_copy(
                                        qtf[:, fch * 512:(fch + 1) * 512], ps_q[fch][:, :]),
                                        wait=[mm, ev_tile])
                                    evf.append(ev)
                                    psq_g[fch] = ev
                                    last_w_use = ev
                                e1 = r.s(lambda E: E.activation(q2[:, :], qtf[:, :],
                                                                ACT.Square), wait=evf)
                                e2 = r.v(lambda E: E.tensor_reduce(
                                    sums.ap().unsqueeze(2),
                                    q2.ap().rearrange("p (h d) -> p h d", d=HD), axis=AX,
                                    op=OP.add), wait=[e1])
                                e3 = r.v(lambda E: E.tensor_scalar(
                                    sums[:, :], sums[:, :], 1.0 / HD, RMS_EPS,
                                    OP.mult, OP.add))
                                e4 = r.s(lambda E: E.activation(sums[:, :], sums[:, :],
                                                                ACT.Sqrt), wait=[e3])
                                e5 = r.v(lambda E: E.reciprocal(sums[:, :], sums[:, :]),
                                         wait=[e4])
                                rfb = bass.AP(sums.ap().tensor, sums.ap().offset,
                                              [[16, 128], [1, 16], [0, HD]])
                                qnb = bass.AP(bcg.ap().tensor, bcg.ap().offset,
                                              [[HD, 128], [0, 16], [1, HD]])
                                r.v(lambda E, rfb=rfb: E.tensor_tensor(
                                    qtf.ap().rearrange("p (h d) -> p h d", d=HD),
                                    qtf.ap().rearrange("p (h d) -> p h d", d=HD), rfb,
                                    OP.mult), inc=False)
                                e6 = r.v(lambda E, qnb=qnb: E.tensor_tensor(
                                    qb16.ap().rearrange("p (h d) -> p h d", d=HD),
                                    qtf.ap().rearrange("p (h d) -> p h d", d=HD), qnb,
                                    OP.mult))
                                for ct in range(CT):
                                    evt = r.t(lambda E, ct=ct: E.transpose(
                                        ps_t[ct % 2][:, :], qb16[:, ct * 128:(ct + 1) * 128],
                                        g.sb_idb[:, :]), wait=[e6, pst_g[ct % 2]])
                                    ev_tile = r.v(
                                        lambda E, dstT=dstT, ct=ct, jt=jt: E.tensor_copy(
                                            dstT[:, ct * S + jt * 128:
                                                 ct * S + (jt + 1) * 128],
                                            ps_t[ct % 2][:, :]), wait=[evt])
                                    pst_g[ct % 2] = ev_tile
                            else:
                                for fch in range(2):
                                    for ct in range(CT):
                                        r.t(lambda E, h3=h3, ct=ct, t4=t4, fch=fch: E.matmul(
                                            ps_q[fch][:, :],
                                            h3[:, ct, t4 * 128:(t4 + 1) * 128],
                                            w3[:, ct, fch * 512:(fch + 1) * 512],
                                            start=(ct == 0), stop=(ct == CT - 1)),
                                            wait=[ld, ev_w((fch + 1) * 512), psq_g[fch]],
                                            inc=(ct == CT - 1))
                                    mm = ("eT", r.c["eT"])
                                    psq_g[fch] = r.v(lambda E, jt=jt, fch=fch: E.tensor_copy(
                                        v_sb.ap().rearrange("p (j h d) -> p j h d",
                                                            h=16, d=65)
                                        [:, jt, fch * 8:(fch + 1) * 8, 0:64],
                                        ps_q[fch].ap().rearrange("p (h d) -> p h d", d=HD)),
                                        wait=[mm, ev_ones])
                                    last_w_use = psq_g[fch]
                        ld_g[ch] = last_w_use
                r.fence()
                r.flush(block)
            oft_sb = A("p4o", [128, 8 * S], BF16)
            attn_batch(qT_sb.ap(), kT_sb.ap(), v_sb, oft_sb.ap(), is_l0=False)
            with contextlib.ExitStack() as px:
                A2 = lambda name, shape, dt=F32: px.enter_context(nc.sbuf_tensor(name, shape, dt))
                wo_sb = A2("p4wo", [128, CT * D], BF16)
                xnt = [A2(f"p4xn{i}", [128, D]) for i in range(2)]
                xit = [A2(f"p4xi{i}", [128, D]) for i in range(2)]
                x2t = A2("p4x2", [128, D])
                rsum = A2("p4rs", [128, 1])
                xf16 = A2("p4xf", [128, D], BF16)
                xfb = [A2(f"p4fb{i}", [128, CT * 128], BF16) for i in range(2)]
                ps_w = [px.enter_context(nc.psum_tensor(f"p4pw{i}", [128, 512], F32))
                        for i in range(2)]
                ps_t = [px.enter_context(nc.psum_tensor(f"p4pt2{i}", [128, 128], BF16))
                        for i in range(2)]
                r = Rec(b)
                r.fence()
                ev_w = wsplit(r, wo_sb, g.l1woT, D, D, CT, 128 * D, "p4w")
                o3 = oft_sb.ap().rearrange("p (c w) -> p c w", w=S)
                wo3 = wo_sb.ap().rearrange("p (c f) -> p c f", f=D)
                ev_done = {}
                ev_xfst = {}
                psw_g = [None, None]
                pst_g = [None, None]
                for tt in range(NTB):
                    gt = bi * NTB + tt
                    ldn = r.dma("sync", xnt[tt % 2][:, :],
                                g.xn1_tf[gt * 128:(gt + 1) * 128, :],
                                wait=[ev_done.get(tt - 2)], ch=f"p4n{tt % 2}")
                    spans = [(MZ, 512), (768, 256)]
                    ev_ao = []
                    for fi, (c0, cw) in enumerate(spans):
                        for ct in range(CT):
                            r.t(lambda E, ct=ct, tt=tt, fi=fi, c0=c0, cw=cw: E.matmul(
                                ps_w[fi][:, 0:cw], o3[:, ct, tt * 128:(tt + 1) * 128],
                                wo3[:, ct, c0:c0 + cw],
                                start=(ct == 0), stop=(ct == CT - 1)),
                                wait=[ev_w(c0 + cw), psw_g[fi]], inc=(ct == CT - 1))
                        mm = ("eT", r.c["eT"])
                        ev = r.v(lambda E, tt=tt, fi=fi, c0=c0, cw=cw: E.scalar_tensor_tensor(
                            xit[tt % 2][:, c0:c0 + cw], ps_w[fi][:, 0:cw], 0.0,
                            xnt[tt % 2][:, c0:c0 + cw], OP.add, OP.add),
                            wait=[mm, ldn, ev_done.get(tt - 2)])
                        ev_ao.append(ev)
                        psw_g[fi] = ev
                    ev_cp = r.v(lambda E, tt=tt: E.tensor_copy(
                        xit[tt % 2][:, 0:MZ], xnt[tt % 2][:, 0:MZ]), wait=[ldn])
                    ev_done[tt] = r.dma("scalar", g.xi_tf[gt * 128:(gt + 1) * 128, :],
                                        xit[tt % 2][:, :], wait=ev_ao + [ev_cp],
                                        ch=f"p4i{tt % 2}")
                    e1 = r.s(lambda E, tt=tt: E.activation(x2t[:, :], xit[tt % 2][:, :],
                                                           ACT.Square), wait=ev_ao + [ev_cp])
                    e2 = r.v(lambda E: E.tensor_reduce(rsum[:, :], x2t[:, :], axis=AX,
                                                       op=OP.add), wait=[e1])
                    e3 = r.v(lambda E: E.tensor_scalar(rsum[:, :], rsum[:, :], 1.0 / D,
                                                       RMS_EPS, OP.mult, OP.add))
                    e4 = r.s(lambda E: E.activation(rsum[:, :], rsum[:, :], ACT.Sqrt), wait=[e3])
                    e5 = r.v(lambda E: E.reciprocal(rsum[:, :], rsum[:, :]), wait=[e4])
                    r.v(lambda E, tt=tt: E.tensor_scalar_mul(x2t[:, :], xit[tt % 2][:, :],
                                                             rsum[:, 0:1]), inc=False)
                    e6 = r.v(lambda E: E.tensor_tensor(xf16[:, :], x2t[:, :], g.bc_n2[:, :],
                                                       OP.mult))
                    for ct in range(CT):
                        evt = r.t(lambda E, ct=ct: E.transpose(
                            ps_t[ct % 2][:, :], xf16[:, ct * 128:(ct + 1) * 128],
                            g.sb_idb[:, :]), wait=[e6, pst_g[ct % 2]])
                        pst_g[ct % 2] = r.v(lambda E, tt=tt, ct=ct: E.tensor_copy(
                            xfb[tt % 2][:, ct * 128:(ct + 1) * 128], ps_t[ct % 2][:, :]),
                            wait=[evt, ev_xfst.get(tt - 2) if ct == 0 else None])
                    ev_xfst[tt] = r.dma("scalar",
                                        ftap(g.xfT_d, T, 0, 128, gt * 128, 128, nblk=CT,
                                             blk_stride=128 * T),
                                        xfb[tt % 2][:, :].rearrange("p (c w) -> p c w", w=128),
                                        wait=[pst_g[(CT - 1) % 2]], ch=f"p4f{tt % 2}")
                r.fence()
                r.flush(block)

    with contextlib.ExitStack() as px:
        A2 = lambda name, shape, dt=F32: px.enter_context(nc.sbuf_tensor(name, shape, dt))
        gw_sb = A2("p4gw", [128, CT * 4], BF16)
        xfblk = [A2(f"p4xb{i}", [128, CT * 128], BF16) for i in range(2)]
        lg = A2("p4lg", [128, 4])
        mx = A2("p4mx", [128, 1])
        sume = A2("p4su", [128, 1])
        m1 = A2("p4m1", [128, 1]); m2 = A2("p4m2", [128, 1])
        msk = A2("p4mk", [128, 4]); pm = A2("p4pm", [128, 4])
        den = A2("p4de", [128, 1])
        ps_g = px.enter_context(nc.psum_tensor("p4pg", [128, 4], F32))
        r = Rec(b)
        r.fence()
        ev_w = r.dma("gpsimd", gw_sb[:, :].rearrange("p (c f) -> p c f", f=4),
                     ftap(g.gate_wT, 4, 0, 128, 0, 4, nblk=CT, blk_stride=128 * 4),
                     ch="p4w")
        gw3 = gw_sb.ap().rearrange("p (c f) -> p c f", f=4)
        ev_fin = {}
        for tt in range(NT):
            ld = r.dma("sync", xfblk[tt % 2][:, :].rearrange("p (c w) -> p c w", w=128),
                       ftap(g.xfT_d, T, 0, 128, tt * 128, 128, nblk=CT, blk_stride=128 * T),
                       wait=[ev_fin.get(tt - 2)], ch=f"p4b{tt % 2}")
            x3 = xfblk[tt % 2].ap().rearrange("p (c w) -> p c w", w=128)
            for ct in range(CT):
                r.t(lambda E, ct=ct, x3=x3: E.matmul(
                    ps_g[:, :], x3[:, ct, :], gw3[:, ct, :],
                    start=(ct == 0), stop=(ct == CT - 1)),
                    wait=[ld, ev_w, ev_fin.get(tt - 1)], inc=(ct == CT - 1))
            mm = ("eT", r.c["eT"])
            e1 = r.v(lambda E: E.tensor_copy(lg[:, :], ps_g[:, :]), wait=[mm])
            e2 = r.v(lambda E: E.tensor_reduce(mx[:, :], lg[:, :], axis=AX, op=OP.max,
                                               negate=True))
            e3 = r.s(lambda E: E.activation(lg[:, :], lg[:, :], ACT.Exp, bias=mx[:, 0:1],
                                            accum_out=sume[:, 0:1]), wait=[e2])
            e4 = r.v(lambda E: E.reciprocal(den[:, :], sume[:, :]), wait=[e3])
            r.v(lambda E: E.tensor_scalar_mul(lg[:, :], lg[:, :], den[:, 0:1]), inc=False)
            r.v(lambda E: E.tensor_reduce(m1[:, :], lg[:, :], axis=AX, op=OP.max), inc=False)
            r.v(lambda E: E.tensor_scalar(msk[:, :], lg[:, :], m1[:, 0:1], None, OP.is_lt),
                inc=False)
            r.v(lambda E: E.tensor_tensor(pm[:, :], lg[:, :], msk[:, :], OP.mult), inc=False)
            r.v(lambda E: E.tensor_reduce(m2[:, :], pm[:, :], axis=AX, op=OP.max), inc=False)
            r.v(lambda E: E.tensor_tensor(den[:, :], m1[:, :], m2[:, :], OP.add), inc=False)
            r.v(lambda E: E.tensor_scalar_add(den[:, :], den[:, :], 1e-8), inc=False)
            r.v(lambda E: E.reciprocal(den[:, :], den[:, :]), inc=False)
            r.v(lambda E: E.tensor_scalar(msk[:, :], lg[:, :], m2[:, 0:1], None, OP.is_ge),
                inc=False)
            r.v(lambda E: E.tensor_tensor(pm[:, :], lg[:, :], msk[:, :], OP.mult), inc=False)
            ev_fin[tt] = r.v(lambda E, tt=tt: E.tensor_scalar_mul(
                g.wts_sb[:, tt * 4:(tt + 1) * 4], pm[:, :], den[:, 0:1]))
        r.fence()
        r.flush(block)

    def stream_xf(r, A2, pfx):
        xfch = [A2(f"{pfx}_x{i}", [128, CT * 512], BF16) for i in range(2)]
        def load(ch, wait):
            return xfch[ch % 2], r.dma(
                "gpsimd", xfch[ch % 2][:, :].rearrange("p (c w) -> p c w", w=512),
                ftap(g.xfT_d, T, 0, 128, ch * 512, 512, nblk=CT, blk_stride=128 * T),
                wait=wait, ch=f"{pfx}x{ch % 2}")
        return load

    with contextlib.ExitStack() as px:
        A2 = lambda name, shape, dt=F32: px.enter_context(nc.sbuf_tensor(name, shape, dt))
        up_sb = A2("e0up", [128, CT * HID], BF16)
        dn_sb = A2("e0dn", [128, 16 * D], BF16)
        m_sb = A2("e0m", [128, 16 * 512], BF16)
        ot = [A2(f"e0o{i}", [128, D - MZ]) for i in range(2)]
        psM = [px.enter_context(nc.psum_tensor(f"e0pm{i}", [128, 512], F32)) for i in range(2)]
        psO = [px.enter_context(nc.psum_tensor(f"e0po{i}", [128, 512], F32)) for i in range(2)]
        r = Rec(b)
        r.fence()
        evw1 = wsplit(r, up_sb, g.e0_upT, HID, HID, CT, 128 * HID, "ew0")
        evw2 = wsplit(r, dn_sb, g.e0_downT, D, D, 16, 128 * D, "ew1")
        loadxf = stream_xf(r, A2, "e0")
        up3 = up_sb.ap().rearrange("p (c f) -> p c f", f=HID)
        dn3 = dn_sb.ap().rearrange("p (i f) -> p i f", f=D)
        m3 = m_sb.ap().rearrange("p (i w) -> p i w", w=512)
        psm_g = [None, None]
        pso_g = [None, None]
        ld_g = {}
        st_g = {}
        for ch in range(8):
            xf_t, ld = loadxf(ch, [ld_g.get(ch - 2)])
            x3 = xf_t.ap().rearrange("p (c w) -> p c w", w=512)
            ev_m = None
            for ft in range(16):
                rot = ft % 2
                for ct in range(CT):
                    r.t(lambda E, x3=x3, ct=ct, ft=ft, rot=rot: E.matmul(
                        psM[rot][:, :], up3[:, ct, ft * 128:(ft + 1) * 128], x3[:, ct, :],
                        start=(ct == 0), stop=(ct == CT - 1)),
                        wait=[ld, evw1((ft + 1) * 128), psm_g[rot]], inc=(ct == CT - 1))
                mm = ("eT", r.c["eT"])
                ev_m = r.s(lambda E, ft=ft, rot=rot: E.activation(
                    m3[:, ft, :], psM[rot][:, :], ACT.Gelu), wait=[mm, ld_g.get(ch - 1)])
                psm_g[rot] = ev_m
            for t4 in range(4):
                gt = ch * 4 + t4
                evo = []
                for fi, (c0, cw) in enumerate(((MZ, 512), (768, 256))):
                    for it in range(16):
                        r.t(lambda E, it=it, t4=t4, fi=fi, c0=c0, cw=cw: E.matmul(
                            psO[fi][:, 0:cw], m3[:, it, t4 * 128:(t4 + 1) * 128],
                            dn3[:, it, c0:c0 + cw],
                            start=(it == 0), stop=(it == 15)),
                            wait=[ev_m, evw2(c0 + cw), pso_g[fi]], inc=(it == 15))
                    mm = ("eT", r.c["eT"])
                    ev = r.v(lambda E, gt=gt, fi=fi, c0=c0, cw=cw: E.tensor_scalar_mul(
                        ot[gt % 2][:, c0 - MZ:c0 - MZ + cw], psO[fi][:, 0:cw],
                        g.wts_sb[:, gt * 4 + 0: gt * 4 + 1]),
                        wait=[mm, st_g.get(gt - 2)])
                    evo.append(ev)
                    pso_g[fi] = ev
                st_g[gt] = r.dma("scalar", g.spill[0][gt * 128:(gt + 1) * 128, :],
                                 ot[gt % 2][:, :], wait=evo, ch=f"es{gt % 2}")
            ld_g[ch] = st_g[ch * 4 + 3]
        r.fence()
        r.flush(block)

    with contextlib.ExitStack() as px:
        A2 = lambda name, shape, dt=F32: px.enter_context(nc.sbuf_tensor(name, shape, dt))
        w1_sb = A2("e1w1", [128, CT * HID], BF16)
        w2_sb = A2("e1w2", [128, CT * HID], BF16)
        dn_sb = A2("e1dn", [128, 16 * D], BF16)
        m_sb = A2("e1m", [128, 16 * 512], BF16)
        silt = A2("e1sl", [128, 512])
        ot = [A2(f"e1o{i}", [128, D - MZ]) for i in range(2)]
        psA = [px.enter_context(nc.psum_tensor(f"e1pa{i}", [128, 512], F32)) for i in range(2)]
        psB = [px.enter_context(nc.psum_tensor(f"e1pb{i}", [128, 512], F32)) for i in range(2)]
        psO = [px.enter_context(nc.psum_tensor(f"e1po{i}", [128, 512], F32)) for i in range(2)]
        r = Rec(b)
        r.fence()
        evw1 = wsplit(r, w1_sb, g.e1_w1T, HID, HID, CT, 128 * HID, "ew0")
        evw2 = wsplit(r, w2_sb, g.e1_w2T, HID, HID, CT, 128 * HID, "ew1")
        evw3 = wsplit(r, dn_sb, g.e1_downT, D, D, 16, 128 * D, "ew2")
        loadxf = stream_xf(r, A2, "e1")
        w13 = w1_sb.ap().rearrange("p (c f) -> p c f", f=HID)
        w23 = w2_sb.ap().rearrange("p (c f) -> p c f", f=HID)
        dn3 = dn_sb.ap().rearrange("p (i f) -> p i f", f=D)
        m3 = m_sb.ap().rearrange("p (i w) -> p i w", w=512)
        psa_g = [None, None]; psb_g = [None, None]; pso_g = [None, None]
        ld_g = {}; st_g = {}
        silt_g = None
        for ch in range(8):
            xf_t, ld = loadxf(ch, [ld_g.get(ch - 2)])
            x3 = xf_t.ap().rearrange("p (c w) -> p c w", w=512)
            ev_m = None
            for ft in range(16):
                rot = ft % 2
                for ct in range(CT):
                    r.t(lambda E, x3=x3, ct=ct, ft=ft, rot=rot: E.matmul(
                        psA[rot][:, :], w13[:, ct, ft * 128:(ft + 1) * 128], x3[:, ct, :],
                        start=(ct == 0), stop=(ct == CT - 1)),
                        wait=[ld, evw1((ft + 1) * 128), psa_g[rot]], inc=(ct == CT - 1))
                mma = ("eT", r.c["eT"])
                for ct in range(CT):
                    r.t(lambda E, x3=x3, ct=ct, ft=ft, rot=rot: E.matmul(
                        psB[rot][:, :], w23[:, ct, ft * 128:(ft + 1) * 128], x3[:, ct, :],
                        start=(ct == 0), stop=(ct == CT - 1)),
                        wait=[evw2((ft + 1) * 128), psb_g[rot]], inc=(ct == CT - 1))
                mmb = ("eT", r.c["eT"])
                evs = r.s(lambda E, rot=rot: E.activation(silt[:, :], psA[rot][:, :], ACT.Silu),
                          wait=[mma, psa_g[rot], silt_g])
                psa_g[rot] = evs
                ev_m = r.v(lambda E, ft=ft, rot=rot: E.scalar_tensor_tensor(
                    m3[:, ft, :], psB[rot][:, :], 0.0, silt[:, :], OP.add, OP.mult),
                    wait=[mmb, evs, ld_g.get(ch - 1)])
                psb_g[rot] = ev_m
                silt_g = ev_m
            for t4 in range(4):
                gt = ch * 4 + t4
                evo = []
                for fi, (c0, cw) in enumerate(((MZ, 512), (768, 256))):
                    for it in range(16):
                        r.t(lambda E, it=it, t4=t4, fi=fi, c0=c0, cw=cw: E.matmul(
                            psO[fi][:, 0:cw], m3[:, it, t4 * 128:(t4 + 1) * 128],
                            dn3[:, it, c0:c0 + cw], start=(it == 0), stop=(it == 15)),
                            wait=[ev_m, evw3(c0 + cw), pso_g[fi]], inc=(it == 15))
                    mm = ("eT", r.c["eT"])
                    ev = r.v(lambda E, gt=gt, fi=fi, c0=c0, cw=cw: E.tensor_scalar_mul(
                        ot[gt % 2][:, c0 - MZ:c0 - MZ + cw], psO[fi][:, 0:cw],
                        g.wts_sb[:, gt * 4 + 1: gt * 4 + 2]),
                        wait=[mm, st_g.get(gt - 2)])
                    evo.append(ev)
                    pso_g[fi] = ev
                st_g[gt] = r.dma("scalar", g.spill[1][gt * 128:(gt + 1) * 128, :],
                                 ot[gt % 2][:, :], wait=evo, ch=f"es{gt % 2}")
            ld_g[ch] = st_g[ch * 4 + 3]
        r.fence()
        r.flush(block)

    with contextlib.ExitStack() as px:
        A2 = lambda name, shape, dt=F32: px.enter_context(nc.sbuf_tensor(name, shape, dt))
        l1_sb = A2("e2l1", [128, CT * HID], BF16)
        l2_sb = A2("e2l2", [128, 16 * HID], BF16)
        m1_sb = A2("e2m1", [128, 16 * 512], BF16)
        m2_sb = [A2(f"e2m2{i}", [128, 16 * 512], BF16) for i in range(2)]
        psM = [px.enter_context(nc.psum_tensor(f"e2pm{i}", [128, 512], F32)) for i in range(2)]
        r = Rec(b)
        r.fence()
        evw1 = wsplit(r, l1_sb, g.e2_l1T, HID, HID, CT, 128 * HID, "ew0")
        evw2 = wsplit(r, l2_sb, g.e2_l2T, HID, HID, 16, 128 * HID, "ew1")
        loadxf = stream_xf(r, A2, "e2")
        l13 = l1_sb.ap().rearrange("p (c f) -> p c f", f=HID)
        l23 = l2_sb.ap().rearrange("p (i f) -> p i f", f=HID)
        m13 = m1_sb.ap().rearrange("p (i w) -> p i w", w=512)
        psg = [None, None]
        ld_g = {}; st_g = {}
        for ch in range(8):
            xf_t, ld = loadxf(ch, [ld_g.get(ch - 2)])
            x3 = xf_t.ap().rearrange("p (c w) -> p c w", w=512)
            ev_m1 = None
            for ft in range(16):
                rot = ft % 2
                for ct in range(CT):
                    r.t(lambda E, x3=x3, ct=ct, ft=ft, rot=rot: E.matmul(
                        psM[rot][:, :], l13[:, ct, ft * 128:(ft + 1) * 128], x3[:, ct, :],
                        start=(ct == 0), stop=(ct == CT - 1)),
                        wait=[ld, evw1((ft + 1) * 128), psg[rot]], inc=(ct == CT - 1))
                mm = ("eT", r.c["eT"])
                ev_m1 = r.s(lambda E, ft=ft, rot=rot: E.activation(
                    m13[:, ft, :], psM[rot][:, :], ACT.Gelu), wait=[mm, ld_g.get(ch - 1)])
                psg[rot] = ev_m1
            m23 = m2_sb[ch % 2].ap().rearrange("p (i w) -> p i w", w=512)
            ev_m2l = None
            for ft in range(16):
                rot = ft % 2
                for it in range(16):
                    r.t(lambda E, ct=it, ft=ft, rot=rot, m13=m13: E.matmul(
                        psM[rot][:, :], l23[:, ct, ft * 128:(ft + 1) * 128], m13[:, ct, :],
                        start=(ct == 0), stop=(ct == 15)),
                        wait=[ev_m1, evw2((ft + 1) * 128), psg[rot]], inc=(it == 15))
                mm = ("eT", r.c["eT"])
                ev_m2l = r.s(lambda E, ft=ft, rot=rot, m23=m23: E.activation(
                    m23[:, ft, :], psM[rot][:, :], ACT.Gelu),
                    wait=[mm, st_g.get(ch - 2)])
                psg[rot] = ev_m2l
            st_g[ch] = r.dma("scalar",
                             ftap(g.m2_d, T, 0, 128, ch * 512, 512, nblk=16,
                                  blk_stride=128 * T),
                             m2_sb[ch % 2][:, :].rearrange("p (i w) -> p i w", w=512),
                             wait=[ev_m2l], ch=f"em{ch % 2}")
            ld_g[ch] = ev_m1
        r.fence()
        r.flush(block)

    with contextlib.ExitStack() as px:
        A2 = lambda name, shape, dt=F32: px.enter_context(nc.sbuf_tensor(name, shape, dt))
        l3_sb = A2("e2l3", [128, 16 * HID], BF16)
        l4_sb = A2("e2l4", [128, 16 * D], BF16)
        m2ch = [A2(f"e2c{i}", [128, 16 * 512], BF16) for i in range(2)]
        m3_sb = A2("e2m3", [128, 16 * 512], BF16)
        ot = [A2(f"e2o{i}", [128, D - MZ]) for i in range(2)]
        psM = [px.enter_context(nc.psum_tensor(f"e2bm{i}", [128, 512], F32)) for i in range(2)]
        psO = [px.enter_context(nc.psum_tensor(f"e2bo{i}", [128, 512], F32)) for i in range(2)]
        r = Rec(b)
        r.fence()
        evw1 = wsplit(r, l3_sb, g.e2_l3T, HID, HID, 16, 128 * HID, "ew0")
        evw2 = wsplit(r, l4_sb, g.e2_l4T, D, D, 16, 128 * D, "ew1")
        l33 = l3_sb.ap().rearrange("p (i f) -> p i f", f=HID)
        l43 = l4_sb.ap().rearrange("p (i f) -> p i f", f=D)
        m33 = m3_sb.ap().rearrange("p (i w) -> p i w", w=512)
        psg = [None, None]; pso_g = [None, None]
        ld_g = {}; st_g = {}
        for ch in range(8):
            ld = r.dma("gpsimd", m2ch[ch % 2][:, :].rearrange("p (i w) -> p i w", w=512),
                       ftap(g.m2_d, T, 0, 128, ch * 512, 512, nblk=16, blk_stride=128 * T),
                       wait=[ld_g.get(ch - 2)], ch=f"el{ch % 2}")
            c3 = m2ch[ch % 2].ap().rearrange("p (i w) -> p i w", w=512)
            ev_m3 = None
            for ft in range(16):
                rot = ft % 2
                for it in range(16):
                    r.t(lambda E, c3=c3, it=it, ft=ft, rot=rot: E.matmul(
                        psM[rot][:, :], l33[:, it, ft * 128:(ft + 1) * 128], c3[:, it, :],
                        start=(it == 0), stop=(it == 15)),
                        wait=[ld, evw1((ft + 1) * 128), psg[rot]], inc=(it == 15))
                mm = ("eT", r.c["eT"])
                ev_m3 = r.s(lambda E, ft=ft, rot=rot: E.activation(
                    m33[:, ft, :], psM[rot][:, :], ACT.Gelu), wait=[mm, ld_g.get(ch - 1)])
                psg[rot] = ev_m3
            for t4 in range(4):
                gt = ch * 4 + t4
                evo = []
                for fi, (c0, cw) in enumerate(((MZ, 512), (768, 256))):
                    for it in range(16):
                        r.t(lambda E, it=it, t4=t4, fi=fi, c0=c0, cw=cw: E.matmul(
                            psO[fi][:, 0:cw], m33[:, it, t4 * 128:(t4 + 1) * 128],
                            l43[:, it, c0:c0 + cw], start=(it == 0), stop=(it == 15)),
                            wait=[ev_m3, evw2(c0 + cw), pso_g[fi]], inc=(it == 15))
                    mm = ("eT", r.c["eT"])
                    ev = r.v(lambda E, gt=gt, fi=fi, c0=c0, cw=cw: E.tensor_scalar_mul(
                        ot[gt % 2][:, c0 - MZ:c0 - MZ + cw], psO[fi][:, 0:cw],
                        g.wts_sb[:, gt * 4 + 2: gt * 4 + 3]),
                        wait=[mm, st_g.get(gt - 2)])
                    evo.append(ev)
                    pso_g[fi] = ev
                st_g[gt] = r.dma("scalar", g.spill[2][gt * 128:(gt + 1) * 128, :],
                                 ot[gt % 2][:, :], wait=evo, ch=f"es{gt % 2}")
            ld_g[ch] = st_g[ch * 4 + 3]
        r.fence()
        r.flush(block)

    with contextlib.ExitStack() as px:
        A2 = lambda name, shape, dt=F32: px.enter_context(nc.sbuf_tensor(name, shape, dt))
        dn_sb = A2("e3dn", [128, CT * D], BF16)
        up_sb = A2("e3up", [128, CT * HID], BF16)
        out_sb = A2("e3ot", [128, 16 * D], BF16)
        d_sb = A2("e3d", [128, CT * 512], BF16)
        u_sb = A2("e3u", [128, 16 * 512], BF16)
        ot = [A2(f"e3o{i}", [128, D - MZ]) for i in range(2)]
        psM = [px.enter_context(nc.psum_tensor(f"e3pm{i}", [128, 512], F32)) for i in range(2)]
        psO = [px.enter_context(nc.psum_tensor(f"e3po{i}", [128, 512], F32)) for i in range(2)]
        r = Rec(b)
        r.fence()
        evw1 = wsplit(r, dn_sb, g.e3_downT, D, D, CT, 128 * D, "ew0")
        evw2 = wsplit(r, up_sb, g.e3_upT, HID, HID, CT, 128 * HID, "ew1")
        evw3 = wsplit(r, out_sb, g.e3_outT, D, D, 16, 128 * D, "ew2")
        loadxf = stream_xf(r, A2, "e3")
        dnw3 = dn_sb.ap().rearrange("p (c f) -> p c f", f=D)
        upw3 = up_sb.ap().rearrange("p (c f) -> p c f", f=HID)
        ow3 = out_sb.ap().rearrange("p (i f) -> p i f", f=D)
        d3 = d_sb.ap().rearrange("p (c w) -> p c w", w=512)
        u3 = u_sb.ap().rearrange("p (i w) -> p i w", w=512)
        psg = [None, None]; pso_g = [None, None]
        ld_g = {}; st_g = {}
        for ch in range(8):
            xf_t, ld = loadxf(ch, [ld_g.get(ch - 2)])
            x3 = xf_t.ap().rearrange("p (c w) -> p c w", w=512)
            ev_d = None
            for ft in range(CT):
                rot = ft % 2
                for ct in range(CT):
                    r.t(lambda E, x3=x3, ct=ct, ft=ft, rot=rot: E.matmul(
                        psM[rot][:, :], dnw3[:, ct, ft * 128:(ft + 1) * 128], x3[:, ct, :],
                        start=(ct == 0), stop=(ct == CT - 1)),
                        wait=[ld, evw1((ft + 1) * 128), psg[rot]], inc=(ct == CT - 1))
                mm = ("eT", r.c["eT"])
                ev_d = r.s(lambda E, ft=ft, rot=rot: E.activation(
                    d3[:, ft, :], psM[rot][:, :], ACT.Gelu), wait=[mm, ld_g.get(ch - 1)])
                psg[rot] = ev_d
            ev_u = None
            for ft in range(16):
                rot = ft % 2
                for ct in range(CT):
                    r.t(lambda E, ct=ct, ft=ft, rot=rot: E.matmul(
                        psM[rot][:, :], upw3[:, ct, ft * 128:(ft + 1) * 128], d3[:, ct, :],
                        start=(ct == 0), stop=(ct == CT - 1)),
                        wait=[ev_d, evw2((ft + 1) * 128), psg[rot]], inc=(ct == CT - 1))
                mm = ("eT", r.c["eT"])
                ev_u = r.s(lambda E, ft=ft, rot=rot: E.activation(
                    u3[:, ft, :], psM[rot][:, :], ACT.Gelu), wait=[mm])
                psg[rot] = ev_u
            for t4 in range(4):
                gt = ch * 4 + t4
                evo = []
                for fi, (c0, cw) in enumerate(((MZ, 512), (768, 256))):
                    for it in range(16):
                        r.t(lambda E, it=it, t4=t4, fi=fi, c0=c0, cw=cw: E.matmul(
                            psO[fi][:, 0:cw], u3[:, it, t4 * 128:(t4 + 1) * 128],
                            ow3[:, it, c0:c0 + cw], start=(it == 0), stop=(it == 15)),
                            wait=[ev_u, evw3(c0 + cw), pso_g[fi]], inc=(it == 15))
                    mm = ("eT", r.c["eT"])
                    ev = r.v(lambda E, gt=gt, fi=fi, c0=c0, cw=cw: E.tensor_scalar_mul(
                        ot[gt % 2][:, c0 - MZ:c0 - MZ + cw], psO[fi][:, 0:cw],
                        g.wts_sb[:, gt * 4 + 3: gt * 4 + 4]),
                        wait=[mm, st_g.get(gt - 2)])
                    evo.append(ev)
                    pso_g[fi] = ev
                st_g[gt] = r.dma("scalar", g.spill[3][gt * 128:(gt + 1) * 128, :],
                                 ot[gt % 2][:, :], wait=evo, ch=f"es{gt % 2}")
            ld_g[ch] = st_g[ch * 4 + 3]
        r.fence()
        r.flush(block)


def _phase_final(nc, b, block, Rec, loc):
    g = _G(loc)
    with contextlib.ExitStack() as px:
        A2 = lambda name, shape, dt=F32: px.enter_context(nc.sbuf_tensor(name, shape, dt))
        R = 2
        xit = [A2(f"p5x{i}", [128, D]) for i in range(R)]
        spt = [[A2(f"p5s{e}{i}", [128, D - MZ]) for i in range(R)] for e in range(4)]
        ut = [A2(f"p5u{i}", [128, D]) for i in range(2)]
        g1t = [A2(f"p5g{i}", [128, D]) for i in range(2)]
        x1t = [A2(f"p5a{i}", [128, D]) for i in range(R)]
        s1t = [A2(f"p5b{i}", [128, D]) for i in range(R)]
        r = Rec(b)
        r.fence()
        ev_fin = {}
        ev_used = {}
        for tt in range(NT):
            gd = [ev_used.get(tt - R)]
            ldx = r.dma("sync", xit[tt % R][:, :], g.xi_tf[tt * 128:(tt + 1) * 128, :],
                        wait=gd, ch=f"lnA{tt % R}")
            ld1 = r.dma("sync", x1t[tt % R][:, :], g.x1_tf[tt * 128:(tt + 1) * 128, :],
                        wait=gd, ch=f"p2x{tt % R}")
            ld2 = r.dma("sync", s1t[tt % R][:, :], g.s1_tf[tt * 128:(tt + 1) * 128, :],
                        wait=gd, ch=f"p2s{tt % R}")
            lsp = [r.dma("gpsimd", spt[e][tt % R][:, :],
                         g.spill[e][tt * 128:(tt + 1) * 128, :], wait=gd,
                         ch=f"e{e}_x{tt % R}") for e in range(4)]
            ev_u = r.v(lambda E, tt=tt: E.tensor_copy(ut[tt % 2][:, 0:MZ],
                                                      xit[tt % R][:, 0:MZ]),
                       wait=[ldx, ev_fin.get(tt - 2)])
            ev_u = r.v(lambda E, tt=tt: E.tensor_tensor(
                ut[tt % 2][:, MZ:D], xit[tt % R][:, MZ:D], spt[0][tt % R][:, :], OP.add),
                wait=[lsp[0]])
            for e in range(1, 4):
                ev_u = r.v(lambda E, tt=tt, e=e: E.tensor_tensor(
                    ut[tt % 2][:, MZ:D], ut[tt % 2][:, MZ:D], spt[e][tt % R][:, :], OP.add),
                    wait=[lsp[e]])
            ev_g = r.s(lambda E, tt=tt: E.activation(g1t[tt % 2][:, :], ut[tt % 2][:, :],
                                                     ACT.Silu),
                       wait=[ev_u, ev_fin.get(tt - 2)])
            r.v(lambda E, tt=tt: E.tensor_tensor(g1t[tt % 2][:, :], s1t[tt % R][:, :],
                                                 g1t[tt % 2][:, :], OP.mult),
                wait=[ev_g, ld2], inc=False)
            ev_o = r.v(lambda E, tt=tt: E.tensor_tensor(
                g1t[tt % 2][:, :], x1t[tt % R][:, :], g1t[tt % 2][:, :], OP.add),
                wait=[ld1])
            ev_used[tt] = ev_o
            ev_fin[tt] = r.dma("scalar", g.y_out[tt * 128:(tt + 1) * 128, :],
                               g1t[tt % 2][:, :], wait=[ev_o], ch=f"p5y{tt % 2}")
        r.fence()
        r.flush(block)


def make_in_maps(inputs):
    tr = lambda a: np.ascontiguousarray(np.asarray(a, dtype=np.float32).T)
    f = lambda a: np.ascontiguousarray(np.asarray(a, dtype=np.float32))
    W = inputs
    com = dict(
        wqT=tr(W["L0_attn_W_w"][0:D]), wkT=tr(W["L0_attn_W_w"][D:2 * D]),
        wvT=tr(W["L0_attn_W_w"][2 * D:3 * D]),
        w_b=f(W["L0_attn_W_b"]), projT=tr(W["L0_attn_proj_w"]),
        proj_b=f(W["L0_attn_proj_b"]), ln_g=f(W["L0_attn_ln_g"]),
        ln_b=f(W["L0_attn_ln_b"]), rpbT=tr(W["L0_attn_rpb"]),
        clip_x=f(W["L0_ffn_clip_x"]), clip_down=f(W["L0_ffn_clip_down"]),
        w_gate=f(W["L0_ffn_w_gate"]), w_up=f(W["L0_ffn_w_up"]),
        w_down=f(W["L0_ffn_w_down"]),
        w_inT=tr(W["L1_attn_w_in"]), w_conv=f(W["L1_attn_w_conv"].reshape(D, 4)),
        b_conv=f(W["L1_attn_b_conv"]), w_gatesT=tr(W["L1_attn_w_gates"]),
        b_gates=f(W["L1_attn_b_gates"]), fbase=f(W["L1_attn_forget_base"]),
        w_outT=tr(W["L1_attn_w_out"]),
        n1_w=f(W["L1_ffn_n1_w"]), n2_w=f(W["L1_ffn_n2_w"]),
        l1wqT=tr(W["L1_ffn_wq"]), l1wkT=tr(W["L1_ffn_wk"]),
        l1wvT=tr(W["L1_ffn_wv"]), l1woT=tr(W["L1_ffn_wo"]),
        qn_w=f(W["L1_ffn_qn_w"]), kn_w=f(W["L1_ffn_kn_w"]),
        gate_wT=tr(W["L1_ffn_gate_w"]),
        e0_upT=tr(W["L1_ffn_e0_up"]), e0_downT=tr(W["L1_ffn_e0_down"]),
        e1_w1T=tr(W["L1_ffn_e1_w1"]), e1_w2T=tr(W["L1_ffn_e1_w2"]),
        e1_downT=tr(W["L1_ffn_e1_down"]),
        e2_l1T=tr(W["L1_ffn_e2_l1"]), e2_l2T=tr(W["L1_ffn_e2_l2"]),
        e2_l3T=tr(W["L1_ffn_e2_l3"]), e2_l4T=tr(W["L1_ffn_e2_l4"]),
        e3_downT=tr(W["L1_ffn_e3_down"]), e3_upT=tr(W["L1_ffn_e3_up"]),
        e3_outT=tr(W["L1_ffn_e3_out"]),
    )
    x = np.asarray(W["L0_attn_x"], dtype=np.float32)
    nb = x.shape[0] // N_CORES
    in_maps = []
    for c in range(N_CORES):
        m = dict(com)
        m["x"] = np.ascontiguousarray(
            x[c * nb:(c + 1) * nb].reshape(nb * x.shape[1], x.shape[2]))
        in_maps.append(m)
    return in_maps


def kernel(**inputs):
    in_maps = make_in_maps(inputs)
    nc = build()
    res = run_bass_kernel_spmd(nc, in_maps, core_ids=list(range(N_CORES)))
    B, Sq, Dm = inputs["L0_attn_x"].shape
    nb = B // N_CORES
    out = np.concatenate(
        [res.results[c]["y"].reshape(nb, Sq, Dm) for c in range(N_CORES)], axis=0)
    return out


# version 13
# speedup vs baseline: 1.0251x; 1.0067x over previous
import contextlib
import numpy as np
import concourse.bass as bass
import concourse.mybir as mybir
from concourse.bass_utils import run_bass_kernel_spmd

F32 = mybir.dt.float32
BF16 = mybir.dt.bfloat16
I8 = mybir.dt.int8
AX = mybir.AxisListType.X
OP = mybir.AluOpType
ACT = mybir.ActivationFunctionType

N_CORES = 8
S = 2048
T = 2 * S
D = 1024
HD = 64
I_FFN = 2816
LN_EPS = 1e-5
RMS_EPS = 1.1920929e-07
HID = 2048
MZ = 256

CT = D // 128
NT = T // 128
NTB = S // 128
ITL = I_FFN // 128
SKEW_W = 2432
MAGIC_RND = 12582912.0

ENGS = ("sync", "gpsimd", "scalar", "vector", "tensor")
ESEM = {"tensor": "eT", "vector": "eV", "scalar": "eS", "gpsimd": "eG"}


class Bld:
    def __init__(self, nc, es=None):
        self.nc = nc
        self.es = es
        self.cnt = {}
        self.sems = {}

    def sem(self, name, es=None):
        h = (es or self.es).enter_context(self.nc.semaphore(name))
        self.sems[name] = h
        self.cnt[name] = 0
        return h

    def sem_for(self, ch):
        if ch not in self.sems:
            self.sem(ch)
        return ch


class Rec:

    def __init__(self, b):
        self.b = b
        self.st = {e: [] for e in ENGS}
        self.c = dict(b.cnt)

    def op(self, eng, fn, wait=(), sem=None, n=1):
        ev = None
        if sem is not None:
            self.c[sem] = self.c.get(sem, self.b.cnt.get(sem, 0)) + n
            ev = (sem, self.c[sem])
        self.st[eng].append((fn, tuple(w for w in wait if w is not None), sem, n))
        return ev

    def dma(self, eng, out, in_, wait=(), ch=None, **kw):
        assert ch is not None, "dma needs a channel"
        sem = self.b.sem_for(ch)
        return self.op(eng, lambda E: E.dma_start(out=out, in_=in_, **kw),
                       wait, sem, 16)

    def t(self, fn, wait=(), inc=True):
        return self.op("tensor", fn, wait, "eT" if inc else None)

    def v(self, fn, wait=(), inc=True):
        return self.op("vector", fn, wait, "eV")

    def s(self, fn, wait=(), inc=True):
        return self.op("scalar", fn, wait, "eS")

    def g(self, fn, wait=(), inc=True):
        return self.op("gpsimd", fn, wait, "eG")

    def fence(self):
        snap = tuple(self.c.items())
        for e in ENGS:
            self.st[e].append((None, snap, None, 0))

    def now(self):
        return tuple(self.c.items())

    def flush(self, block):
        b = self.b
        for eng in ENGS:
            items = self.st[eng]
            if not any(fn is not None or wait for fn, wait, _, _ in items):
                continue

            def make(items=items, eng=eng):
                own = ESEM.get(eng)
                serial = eng in ("vector", "scalar", "gpsimd")
                start = b.cnt.get(own, 0) if own else 0

                def sec(E):
                    hi = {}
                    cur = start
                    for fn, wait, sem, n in items:
                        for (s_, v) in wait:
                            if s_ == own:
                                continue
                            if hi.get(s_, -1) >= v:
                                continue
                            E.wait_ge(b.sems[s_], v)
                            hi[s_] = v
                        if fn is not None:
                            if serial and cur > 0:
                                E.wait_ge(b.sems[own], cur)
                            ins = fn(E)
                            if sem is not None:
                                ins.then_inc(b.sems[sem], n)
                                b.cnt[sem] += n
                                if sem == own:
                                    cur = b.cnt[sem]
                return sec

            getattr(block, eng)(make())
        for k, v in self.c.items():
            assert b.cnt.get(k, 0) == v, (k, b.cnt.get(k, 0), v)


def ftap(hbm_ap, row_pitch, row0, nrow, col0, ncol, nblk=1, blk_stride=0):
    off = row0 * row_pitch + col0
    dims = [[row_pitch, nrow]]
    if nblk > 1:
        dims.append([blk_stride, nblk])
    dims.append([1, ncol])
    return bass.AP(hbm_ap.tensor, off, dims)


def wsplit(r, dst_sb, hbm, row_pitch, f_total, nblk, blk_stride, ch,
           wait=(), nchunk=4, col0=0):
    fc = f_total // nchunk
    d3 = dst_sb[:, :].rearrange("p (c f) -> p c f", f=f_total)
    evs = []
    for i in range(nchunk):
        ev = r.dma("gpsimd", d3[:, :, i * fc:(i + 1) * fc],
                   ftap(hbm, row_pitch, 0, 128, col0 + i * fc, fc,
                        nblk=nblk, blk_stride=blk_stride),
                   wait=list(wait) if i == 0 else [], ch=ch)
        evs.append(((i + 1) * fc, ev))

    def at(f_end):
        for hi, ev in evs:
            if f_end <= hi:
                return ev
        return evs[-1][1]
    return at


def build(debug=(), stop_after=None, skip_qs=False):
    nc = bass.Bass("TRN2", target_bir_lowering=False, debug=False)
    import itertools as _it
    _ctr = _it.count()
    _sb, _ps = nc.sbuf_tensor, nc.psum_tensor
    nc.sbuf_tensor = lambda name, *a, **k: _sb(f"{name}_{next(_ctr)}", *a, **k)
    nc.psum_tensor = lambda name, *a, **k: _ps(f"{name}_{next(_ctr)}", *a, **k)

    def dram(name, shape, dt=F32, out=False):
        return nc.dram_tensor(name, shape, dt,
                              kind="ExternalOutput" if out else "ExternalInput").ap()

    def scratch(name, shape, dt=F32):
        if name in debug:
            return nc.dram_tensor(name, shape, dt, kind="ExternalOutput").ap()
        return nc.dram_tensor(name, shape, dt).ap()

    x_in = dram("x", [T, D])
    wqT = dram("wqT", [D, D]); wkT = dram("wkT", [D, D]); wvT = dram("wvT", [D, D])
    w_b = dram("w_b", [3 * D])
    projT = dram("projT", [D, D]); proj_b = dram("proj_b", [D])
    ln_g = dram("ln_g", [D]); ln_b = dram("ln_b", [D])
    rpbT = dram("rpbT", [16, 2 * S - 1])
    clip_x = dram("clip_x", [1]); clip_down = dram("clip_down", [1])
    w_gate = dram("w_gate", [I_FFN, D]); w_up = dram("w_up", [I_FFN, D])
    w_down = dram("w_down", [D, I_FFN])
    w_inT = dram("w_inT", [D, 2 * D])
    w_conv = dram("w_conv", [D, 4]); b_conv = dram("b_conv", [D])
    w_gatesT = dram("w_gatesT", [D, 2 * D]); b_gates = dram("b_gates", [2 * D])
    fbase = dram("fbase", [D])
    w_outT = dram("w_outT", [D, D])
    n1_w = dram("n1_w", [D]); n2_w = dram("n2_w", [D])
    l1wqT = dram("l1wqT", [D, D]); l1wkT = dram("l1wkT", [D, D])
    l1wvT = dram("l1wvT", [D, D]); l1woT = dram("l1woT", [D, D])
    qn_w = dram("qn_w", [HD]); kn_w = dram("kn_w", [HD])
    gate_wT = dram("gate_wT", [D, 4])
    e0_upT = dram("e0_upT", [D, HID]); e0_downT = dram("e0_downT", [HID, D])
    e1_w1T = dram("e1_w1T", [D, HID]); e1_w2T = dram("e1_w2T", [D, HID])
    e1_downT = dram("e1_downT", [HID, D])
    e2_l1T = dram("e2_l1T", [D, HID]); e2_l2T = dram("e2_l2T", [HID, HID])
    e2_l3T = dram("e2_l3T", [HID, HID]); e2_l4T = dram("e2_l4T", [HID, D])
    e3_downT = dram("e3_downT", [D, D]); e3_upT = dram("e3_upT", [D, HID])
    e3_outT = dram("e3_outT", [HID, D])
    y_out = dram("y", [T, D], out=True)

    ident = nc.inline_tensor(np.eye(128, dtype=np.float32), name="ident").ap()
    import ml_dtypes as _mld
    identb = nc.inline_tensor(np.eye(128, dtype=_mld.bfloat16), name="identb").ap()
    c01np = np.zeros(2 * S - 1, dtype=np.float32); c01np[:S] = 1.0
    causal01 = nc.inline_tensor(c01np, name="causal01").ap()

    xn_tf = scratch("xn_tf", [T, D])
    xnT = scratch("xnT", [D, T])
    s0_tf = scratch("s0_tf", [T, D])
    hpre = scratch("hpre", [I_FFN, T])
    hq_d = scratch("hq_d", [I_FFN, T], BF16)
    x1_tf = scratch("x1_tf", [T, D])
    xn1_tf = scratch("xn1_tf", [T, D])
    xn1T = scratch("xn1T", [D, T])
    h1T_d = scratch("h1T_d", [D, T], BF16)
    gg_d = scratch("gg_d", [D, T], BF16)
    y_hk = scratch("y_hk", [D, T], BF16)
    s1_tf = scratch("s1_tf", [T, D])
    xi_tf = scratch("xi_tf", [T, D])
    xfT_d = scratch("xfT_d", [D, T], BF16)
    m2_d = scratch("m2_d", [HID, T], BF16)
    spill = [scratch(f"spill{e}", [T, D - MZ]) for e in range(4)]
    exprpb = scratch("exprpb", [16, 2 * S - 1], BF16)
    bounce_h = scratch("bounce_h", [256, 1024])
    dbg_qT = scratch("dbg_qT", [128, 8 * S], BF16)
    dbg_kT = scratch("dbg_kT", [128, 8 * S], BF16)
    dbg_v = scratch("dbg_v", [128, NTB * 16 * 65], BF16)
    dbg_oft = scratch("dbg_oft", [128, 8 * S], BF16)
    dbg_xq = scratch("dbg_xq", [128, CT * T], BF16)
    dbg_wgq = scratch("dbg_wgq", [128, CT * I_FFN], BF16)
    dbg_wuq = scratch("dbg_wuq", [128, CT * I_FFN], BF16)
    dbg_sc = scratch("dbg_sc", [128, 2 * ITL + 2])
    sd_row = scratch("sd_row", [1, D])
    cc_io = [nc.dram_tensor(f"cc_{i}", [1, 2], F32).ap() for i in range(4)]
    m128a = nc.dram_tensor("m128a", [128], F32).ap()
    m128b = nc.dram_tensor("m128b", [128], F32).ap()

    es = contextlib.ExitStack()
    with es:
        block = es.enter_context(nc.Block())
        b = Bld(nc, es)
        for s_ in ("eT", "eV", "eS", "eG", "cc"):
            b.sem(s_)

        P = lambda name, shape, dt=F32: es.enter_context(nc.sbuf_tensor(name, shape, dt))
        sb_id = P("sb_id", [128, 128])
        sb_idb = P("sb_idb", [128, 128], BF16)
        bc_projb = P("bc_projb", [128, D]); bc_lng = P("bc_lng", [128, D])
        bc_lnb = P("bc_lnb", [128, D]); bc_vb = P("bc_vb", [128, D])
        bc_n1 = P("bc_n1", [128, D]); bc_n2 = P("bc_n2", [128, D])
        bc_qn = P("bc_qn", [128, HD]); bc_kn = P("bc_kn", [128, HD])
        bc_sd = P("bc_sd", [128, D])
        colv_qb = P("colv_qb", [128, CT]); colv_kb = P("colv_kb", [128, CT])
        colv_bc = P("colv_bc", [128, CT]); colv_bgf = P("colv_bgf", [128, CT])
        colv_bgi = P("colv_bgi", [128, CT]); colv_ce = P("colv_ce", [128, CT])
        wconv_c = P("wconv_c", [128, CT * 4])
        sig1r = P("sig1r", [128, 1]); sig1s = P("sig1s", [128, 1])
        sig2r = P("sig2r", [128, 1]); sig2s = P("sig2s", [128, 1])
        sgx = P("sgx", [128, ITL]); sux = P("sux", [128, ITL])
        sgt = P("sgt", [128, ITL]); sut = P("sut", [128, ITL]); sdt = P("sdt", [128, CT])
        skew01 = P("skew01", [128, SKEW_W], BF16)
        rmax1 = P("rmax1", [128, 1]); rmax2 = P("rmax2", [128, 1])
        wts_sb = P("wts_sb", [128, 4 * NT])
        r1_sb = P("r1_sb", [128, NT])
        stat_m = P("stat_m", [128, NT]); stat_r = P("stat_r", [128, NT])
        clip1 = P("clip1", [128, 1]); clip2 = P("clip2", [128, 1])
        tiny = P("tiny", [128, 64])

        with contextlib.ExitStack() as px:
            rpbs = px.enter_context(nc.sbuf_tensor("p0rpb", [16, 2 * S - 1], F32))
            rpbb = px.enter_context(nc.sbuf_tensor("p0rpbb", [16, 2 * S - 1], BF16))
            r = Rec(b)

            def brow(sb_t, src, n, q="sync"):
                ap = bass.AP(src.tensor, src.offset, [[0, 128], [1, n]])
                return r.dma(q, sb_t[:, 0:n], ap, ch="k0")

            def cload(sb_t, src, ntile=CT):
                ap = bass.AP(src.tensor, src.offset, [[1, 128], [128, ntile]])
                return r.dma("sync", sb_t[:, 0:ntile], ap, ch="k0",
                             allow_slow_non_contiguous=True)

            evs = [
                brow(bc_projb, proj_b, D), brow(bc_lng, ln_g, D), brow(bc_lnb, ln_b, D),
                brow(bc_vb, w_b[2 * D:3 * D], D), brow(bc_n1, n1_w, D), brow(bc_n2, n2_w, D),
                brow(bc_qn, qn_w, HD), brow(bc_kn, kn_w, HD),
                cload(colv_qb, w_b[0:D]), cload(colv_kb, w_b[D:2 * D]),
                cload(colv_bc, b_conv), cload(colv_bgf, b_gates[0:D]),
                cload(colv_bgi, b_gates[D:2 * D]), cload(colv_ce, fbase),
                brow(clip1, clip_x, 1), brow(clip2, clip_down, 1),
            ]
            wc_ap = bass.AP(w_conv.tensor, 0, [[4, 128], [512, CT], [1, 4]])
            evs.append(r.dma("sync", wconv_c.ap().rearrange("p (c k) -> p c k", k=4),
                             wc_ap, ch="k0"))
            ev_rpb = r.dma("sync", rpbs[:, :], rpbT, ch="k0")
            sk_ap = bass.AP(causal01.tensor, 0, [[1, 128], [1, SKEW_W]])
            evs.append(r.dma("gpsimd", skew01[:, :], sk_ap, ch="k0g"))
            evs.append(r.dma("sync", sb_id[:, :], ident, ch="k0"))
            evs.append(r.dma("sync", sb_idb[:, :], identb, ch="k0"))
            e_all = ("k0", r.c["k0"])
            e_allg = ("k0g", r.c["k0g"])

            ev1 = r.s(lambda E: E.activation(rpbb[:, :], rpbs[:, :], ACT.Exp),
                      wait=[e_all, e_allg])
            r.s(lambda E: E.activation(clip1[:, :], clip1[:, :], ACT.Sigmoid), wait=[e_all])
            r.s(lambda E: E.activation(clip2[:, :], clip2[:, :], ACT.Sigmoid), wait=[e_all])
            ev2 = r.s(lambda E: E.activation(colv_ce[:, :], colv_ce[:, :], ACT.Exp), wait=[e_all])
            ev3 = r.s(lambda E: E.activation(colv_ce[:, :], colv_ce[:, :], ACT.Ln, bias=1.0), wait=[ev2])
            ev4 = r.v(lambda E: E.memset(rpbb[:, S:2 * S - 1], 0.0), wait=[ev1])
            r.v(lambda E: E.memset(rmax1[:, :], 0.0))
            r.v(lambda E: E.memset(rmax2[:, :], 0.0))
            ev5 = r.v(lambda E: E.tensor_scalar_mul(colv_ce[:, :], colv_ce[:, :], -8.0), wait=[ev3])
            r.v(lambda E: E.tensor_scalar_mul(bc_qn[:, :], bc_qn[:, :], 0.125), wait=[e_all])
            r.dma("gpsimd", exprpb, rpbb[:, :], wait=[ev4], ch="k1")
            r.fence()
            r.flush(block)

        def ln_two_pass(src_tf, dst_tf, dstT, absmax_to=None, h1_also=False):
            with contextlib.ExitStack() as px:
                A = lambda name, shape, dt=F32: px.enter_context(nc.sbuf_tensor(name, shape, dt))
                xt = [A(f"lnx{i}", [128, D]) for i in range(3)]
                sqsc = A("lnsq", [128, D])
                vee = A("lnve2", [128, NT])
                nb_ = [A(f"lnn{i}", [128, D]) for i in range(2)]
                stT = [A(f"lnsT{i}", [128, CT * 128]) for i in range(2)]
                stH = [A(f"lnsH{i}", [128, CT * 128], BF16) for i in range(2)] if h1_also else None
                h1b = A("lnh1", [128, D], BF16) if h1_also else None
                red = A("lnred", [128, 1])
                pst = [px.enter_context(nc.psum_tensor(f"lnp{i}", [128, 128], F32)) for i in range(2)]
                psth = ([px.enter_context(nc.psum_tensor(f"lnph{i}", [128, 128], BF16))
                         for i in range(2)] if h1_also else None)

                r = Rec(b)
                r.fence()
                ev_st = {}
                ev_sq = {}
                ld = {}
                for tt in range(NT):
                    ld[tt] = r.dma("sync", xt[tt % 3][:, :],
                                   src_tf[tt * 128:(tt + 1) * 128, :],
                                   wait=[ev_st.get(tt - 3), ev_sq.get(tt - 3)],
                                   ch=f"lnA{tt % 3}")
                    ev_st[tt] = r.v(lambda E, tt=tt: E.tensor_reduce(
                        stat_m[:, tt:tt + 1], xt[tt % 3][:, :], axis=AX, op=OP.add),
                        wait=[ld[tt]])
                    ev_sq[tt] = r.s(lambda E, tt=tt: E.activation(
                        sqsc[:, :], xt[tt % 3][:, :], ACT.Square,
                        accum_out=stat_r[:, tt:tt + 1]), wait=[ld[tt]])
                e1 = r.v(lambda E: E.tensor_scalar_mul(stat_m[:, 0:NT], stat_m[:, 0:NT],
                                                       1.0 / D),
                         wait=[ev_st[NT - 1], ev_sq[NT - 1]])
                r.v(lambda E: E.tensor_tensor(vee[:, 0:NT], stat_m[:, 0:NT],
                                              stat_m[:, 0:NT], OP.mult), inc=False)
                r.v(lambda E: E.tensor_scalar_mul(stat_r[:, 0:NT], stat_r[:, 0:NT], 1.0 / D),
                    inc=False)
                r.v(lambda E: E.tensor_tensor(stat_r[:, 0:NT], stat_r[:, 0:NT], vee[:, 0:NT],
                                              OP.subtract), inc=False)
                e1 = r.v(lambda E: E.tensor_scalar(vee[:, 0:NT], stat_r[:, 0:NT], 1.0, LN_EPS,
                                                   OP.mult, OP.add))
                if h1_also:
                    r.v(lambda E: E.reciprocal(r1_sb[:, 0:NT], vee[:, 0:NT]), wait=[e1],
                        inc=False)
                    r.v(lambda E: E.tensor_tensor(r1_sb[:, 0:NT], r1_sb[:, 0:NT],
                                                  stat_r[:, 0:NT], OP.mult), inc=False)
                    e1b = r.v(lambda E: E.tensor_scalar_add(r1_sb[:, 0:NT], r1_sb[:, 0:NT],
                                                            RMS_EPS))
                    e2b = r.s(lambda E: E.activation(r1_sb[:, 0:NT], r1_sb[:, 0:NT],
                                                     ACT.Sqrt), wait=[e1b])
                    e3b = r.v(lambda E: E.reciprocal(r1_sb[:, 0:NT], r1_sb[:, 0:NT]),
                              wait=[e2b])
                else:
                    e3b = None
                e2 = r.s(lambda E: E.activation(vee[:, 0:NT], vee[:, 0:NT], ACT.Sqrt),
                         wait=[e1])
                e3 = r.v(lambda E: E.reciprocal(stat_r[:, 0:NT], vee[:, 0:NT]), wait=[e2])

                ev_cons = {}
                st_done = {}
                psg = [None, None]
                psgh = [None, None]
                stTg = [None, None]
                stHg = [None, None]
                last_tr = {}
                last_trh = {}
                for tt in range(NT):
                    ldb = r.dma("sync", xt[tt % 3][:, :], src_tf[tt * 128:(tt + 1) * 128, :],
                                wait=[e3, ev_cons.get(tt - 3)], ch=f"lnA{tt % 3}")
                    nb = nb_[tt % 2]
                    evn = r.v(lambda E, tt=tt, nb=nb: E.tensor_scalar(
                        nb[:, :], xt[tt % 3][:, :], stat_m[:, tt:tt + 1], stat_r[:, tt:tt + 1],
                        OP.subtract, OP.mult),
                        wait=[ldb, st_done.get(tt - 2), last_tr.get(tt - 2)])
                    ev_cons[tt] = evn
                    if absmax_to is not None:
                        r.v(lambda E, nb=nb: E.tensor_reduce(red[:, :], nb[:, :], axis=AX,
                                                             op=OP.max, apply_absolute_value=True),
                            inc=False)
                        r.v(lambda E: E.tensor_max(absmax_to[:, :], absmax_to[:, :], red[:, :]),
                            inc=False)
                    evh = None
                    if h1_also:
                        r.v(lambda E, tt=tt, nb=nb: E.tensor_scalar_mul(
                            h1b[:, :], nb[:, :], r1_sb[:, tt:tt + 1]),
                            wait=[e3b, last_trh.get(tt - 1)], inc=False)
                        evh = r.v(lambda E: E.tensor_tensor(h1b[:, :], h1b[:, :], bc_n1[:, :], OP.mult))
                    st_done[tt] = r.dma("scalar", dst_tf[tt * 128:(tt + 1) * 128, :], nb[:, :],
                                        wait=[evn], ch=f"lnB{tt % 2}")
                    for ct in range(CT):
                        i = ct % 2
                        evt = r.t(lambda E, nb=nb, ct=ct: E.transpose(
                            pst[ct % 2][:, :], nb[:, ct * 128:(ct + 1) * 128], sb_id[:, :]),
                            wait=[evn, psg[i]])
                        last_tr[tt] = evt
                        psg[i] = r.v(lambda E, tt=tt, ct=ct: E.tensor_copy(
                            stT[tt % 2][:, ct * 128:(ct + 1) * 128], pst[ct % 2][:, :]),
                            wait=[evt, stTg[tt % 2] if ct == 0 else None])
                    stTg[tt % 2] = r.dma(
                        "scalar",
                        ftap(dstT, T, 0, 128, tt * 128, 128, nblk=CT, blk_stride=128 * T),
                        stT[tt % 2].ap().rearrange("p (c w) -> p c w", w=128),
                        wait=[psg[(CT - 1) % 2]], ch=f"lnC{tt % 2}")
                    if h1_also:
                        for ct in range(CT):
                            i = ct % 2
                            evt = r.t(lambda E, ct=ct: E.transpose(
                                psth[ct % 2][:, :], h1b[:, ct * 128:(ct + 1) * 128],
                                sb_idb[:, :]), wait=[evh, psgh[i]])
                            last_trh[tt] = evt
                            psgh[i] = r.v(lambda E, tt=tt, ct=ct: E.tensor_copy(
                                stH[tt % 2][:, ct * 128:(ct + 1) * 128], psth[ct % 2][:, :]),
                                wait=[evt, stHg[tt % 2] if ct == 0 else None])
                        stHg[tt % 2] = r.dma(
                            "scalar",
                            ftap(h1T_d, T, 0, 128, tt * 128, 128, nblk=CT, blk_stride=128 * T),
                            stH[tt % 2].ap().rearrange("p (c w) -> p c w", w=128),
                            wait=[psgh[(CT - 1) % 2]], ch=f"lnD{tt % 2}")
                r.fence()
                r.flush(block)

        ln_two_pass(x_in, xn_tf, xnT, absmax_to=rmax1)

        def quant_scale_phase(rmax_t, cc_in_t, cc_out_t, clip_t, sig_s, sig_r, m128_d):
            with contextlib.ExitStack() as px:
                A = lambda name, shape, dt=F32: px.enter_context(nc.sbuf_tensor(name, shape, dt))
                row = A("qsrow", [1, 128])
                mloc = A("qsm", [128, 2])
                r = Rec(b)
                r.fence()
                e1 = r.dma("gpsimd", m128_d, rmax_t[:, :], ch="qs")
                e2 = r.dma("gpsimd", row[:, :], m128_d.unsqueeze(0), wait=[e1], ch="qs")
                e3 = r.v(lambda E: E.tensor_reduce(mloc[0:1, 0:1], row[:, :], axis=AX,
                                                   op=OP.max), wait=[e2])
                e4 = r.v(lambda E: E.tensor_copy(
                    mloc[0:1, 0:2], mloc.ap()[0:1, 0:1].broadcast_to([1, 2])))
                e5 = r.dma("gpsimd", cc_in_t, mloc[0:1, 0:2], wait=[e4], ch="qs")
                e6 = r.op("gpsimd", lambda E: E.collective_compute(
                    "AllReduce", OP.max, replica_groups=[list(range(N_CORES))],
                    ins=[cc_in_t.opt()], outs=[cc_out_t.opt()]), wait=[e5], sem="cc")
                bcap = bass.AP(cc_out_t.tensor, 0, [[0, 128], [1, 1]])
                e7 = r.dma("gpsimd", mloc[:, 0:1], bcap, wait=[e6], ch="qs")
                r.v(lambda E: E.tensor_tensor(sig_s[:, :], clip_t[:, :], mloc[:, 0:1],
                                              OP.mult), wait=[e7], inc=False)
                r.v(lambda E: E.tensor_scalar_mul(sig_s[:, :], sig_s[:, :], 1.0 / 127.0),
                    inc=False)
                r.v(lambda E: E.reciprocal(sig_r[:, :], sig_s[:, :]), inc=False)
                r.fence()
                r.flush(block)

        if not skip_qs:
            quant_scale_phase(rmax1, cc_io[0], cc_io[1], clip1, sig1s, sig1r,
                              m128a)

        def attn_batch(qT_ap, kT_ap, v_sb, oft_ap, is_l0):
            ROT = 2
            SR = 3
            v4 = v_sb.ap().rearrange("p (j h d) -> p j h d", h=16, d=65)
            with contextlib.ExitStack() as px:
                A = lambda name, shape, dt=F32: px.enter_context(nc.sbuf_tensor(name, shape, dt))
                PS = lambda name, shape: px.enter_context(nc.psum_tensor(name, shape, F32))
                skew = ([A(f"at_sk{i}", [128, 2 * SKEW_W], BF16) for i in range(2)]
                        if is_l0 else None)
                pe16 = [A(f"at_pe{i}", [128, 1024], BF16) for i in range(SR)]
                pr = [A(f"at_pr{i}", [128, 1024], BF16) for i in range(SR)]
                srow = [A(f"at_sr{i}", [1, 1024]) for i in range(2)]
                rr8 = [A(f"at_r8{i}", [128, 8]) for i in range(2)]
                rro = [A(f"at_ro{i}", [128, 8]) for i in range(2)]
                rbc = [A(f"at_rb{i}", [64, 1024], BF16) for i in range(2)]
                ps_s = [PS(f"at_ps{i}", [128, 1024]) for i in range(ROT)]
                ps_o = [PS(f"at_po{h}", [65, 512]) for h in range(4)]

                r = Rec(b)
                r.fence()
                ev_exp, ev_prb, ev_ocons = {}, {}, {}
                last_pe_rd = {}
                t_mults = [None, None]
                t_dma1 = [None, None]
                t_dma3 = [None, None]
                t_recip = [None, None]
                ev_skg = [None, None]
                step = 0
                pend_o = None

                def emit_o(po):
                    jt, jm, sr, h0, h1, first, st_, par = po
                    w_o = [ev_prb[st_]]
                    if first:
                        w_o += list(t_mults[par] or ())
                    r.t(lambda E, jt=jt, h0=h0, sr=sr, jm=jm, par=par: E.matmul(
                        ps_o[2 * par][:, :], v4[:, jt, h0, 0:65], pr[sr][:, 0:512],
                        start=(jt == 0), stop=(jt == jm)), wait=w_o, inc=False)
                    ev = r.t(lambda E, jt=jt, h1=h1, sr=sr, jm=jm, par=par: E.matmul(
                        ps_o[2 * par + 1][:, :], v4[:, jt, h1, 0:65], pr[sr][:, 512:1024],
                        start=(jt == 0), stop=(jt == jm)))
                    ev_ocons[st_] = ev
                    return ev

                for hp in range(8):
                    h0, h1 = 2 * hp, 2 * hp + 1
                    ev_sk = None
                    if is_l0:
                        skb = hp % 2
                        ev_sk = r.dma(
                            "scalar",
                            skew[skb].ap().rearrange("p (h w) -> p h w", h=2),
                            bass.AP(exprpb.tensor, 2 * hp * (2 * S - 1),
                                    [[1, 128], [2 * S - 1, 2], [1, SKEW_W]]),
                            wait=[ev_skg[skb]], ch=f"atsk{skb}")
                    for ic in range(4):
                        par = ic % 2
                        jmax = 4 * ic + 3
                        for jt in range(jmax + 1):
                            rot = step % ROT
                            sr = step % SR
                            w_sc = [ev_exp.get(step - ROT)]
                            c0 = 2047 + 128 * jt - 512 * ic
                            mm0 = r.t(lambda E, hp=hp, jt=jt, ic=ic, rot=rot: E.matmul(
                                ps_s[rot][:, 0:512],
                                kT_ap[0:64, hp * S + jt * 128: hp * S + (jt + 1) * 128],
                                qT_ap[0:64, hp * S + ic * 512: hp * S + (ic + 1) * 512],
                                start=True, stop=True, tile_position=(0, 0)), wait=w_sc)
                            mm1 = r.t(lambda E, hp=hp, jt=jt, ic=ic, rot=rot: E.matmul(
                                ps_s[rot][:, 512:1024],
                                kT_ap[64:128, hp * S + jt * 128: hp * S + (jt + 1) * 128],
                                qT_ap[64:128, hp * S + ic * 512: hp * S + (ic + 1) * 512],
                                start=True, stop=True, tile_position=(64, 0)), wait=w_sc)
                            if pend_o is not None:
                                emit_o(pend_o)
                                pend_o = None
                            diag = jt >= 4 * ic
                            if is_l0 or diag:
                                ev_e = r.s(lambda E, rot=rot, sr=sr: E.activation(
                                    pe16[sr][:, :], ps_s[rot][:, :], ACT.Exp),
                                    wait=[mm1, last_pe_rd.get(sr)])
                                ev_exp[step] = ev_e
                                if is_l0:
                                    skt = skew[hp % 2].ap().rearrange(
                                        "p (h w) -> p h w", h=2)
                                    if c0 - 512 < 0:
                                        sl = (lambda skt=skt, c0=c0: skt[:, :, c0::-1])
                                    else:
                                        sl = (lambda skt=skt, c0=c0:
                                              skt[:, :, c0:c0 - 512:-1])
                                else:
                                    if c0 - 512 < 0:
                                        sl_ = skew01[:, c0::-1]
                                    else:
                                        sl_ = skew01[:, c0:c0 - 512:-1]
                                    sl_ = sl_.unsqueeze(1).broadcast_to([128, 2, 512])
                                    sl = (lambda sl_=sl_: sl_)
                                ev_p = r.v(lambda E, sr=sr, sl=sl: E.tensor_tensor(
                                    pr[sr].ap().rearrange("p (h w) -> p h w", h=2),
                                    pe16[sr].ap().rearrange("p (h w) -> p h w", h=2),
                                    sl(), OP.mult),
                                    wait=[ev_e, ev_sk, ev_ocons.get(step - SR)])
                                last_pe_rd[sr] = ev_p
                                ev_prb[step] = ev_p
                                ev_skg[hp % 2] = ev_p
                            else:
                                ev_e = r.s(lambda E, rot=rot, sr=sr: E.activation(
                                    pr[sr][:, :], ps_s[rot][:, :], ACT.Exp),
                                    wait=[mm1, ev_ocons.get(step - SR)])
                                ev_exp[step] = ev_e
                                ev_prb[step] = ev_e
                            pend_o = (jt, jmax, sr, h0, h1, jt == 0, step, par)
                            step += 1
                        ev_o = emit_o(pend_o)
                        pend_o = None
                        idx = attn_batch.slot
                        attn_batch.slot += 1
                        ev_c0 = r.v(lambda E, par=par: E.tensor_copy(
                            srow[par][0:1, 0:512], ps_o[2 * par][64:65, 0:512]),
                            wait=[ev_o, t_dma1[par]])
                        ev_c1 = r.v(lambda E, par=par: E.tensor_copy(
                            srow[par][0:1, 512:1024], ps_o[2 * par + 1][64:65, 0:512]))
                        d1 = r.dma("sync", ftap(bounce_h, 1024, 2 * idx, 1, 0, 1024),
                                   srow[par][:, :], wait=[ev_c0, ev_c1],
                                   ch=f"tl{par}")
                        t_dma1[par] = d1
                        d2 = r.dma("sync", rr8[par][:, :],
                                   bass.AP(bounce_h.tensor, 2 * idx * 1024,
                                           [[8, 128], [1, 8]]),
                                   wait=[d1, t_recip[par]], ch=f"tl{par}")
                        ev_r = r.v(lambda E, par=par: E.reciprocal(
                            rro[par][:, :], rr8[par][:, :]), wait=[d2, t_dma3[par]])
                        t_recip[par] = ev_r
                        d3 = r.dma("sync",
                                   bass.AP(bounce_h.tensor, (2 * idx + 1) * 1024,
                                           [[8, 128], [1, 8]]),
                                   rro[par][:, :], wait=[ev_r], ch=f"tl{par}")
                        t_dma3[par] = d3
                        d4 = r.dma("gpsimd", rbc[par][:, :],
                                   bass.AP(bounce_h.tensor, (2 * idx + 1) * 1024,
                                           [[0, 64], [1, 1024]]),
                                   wait=[d3] + list(t_mults[par] or ()),
                                   ch=f"tl{par}")
                        ev_m0 = r.v(lambda E, hp=hp, ic=ic, par=par: E.tensor_tensor(
                            oft_ap[0:64, hp * S + ic * 512:hp * S + (ic + 1) * 512],
                            ps_o[2 * par][0:64, 0:512],
                            rbc[par][:, 0:512], OP.mult), wait=[d4])
                        ev_m1 = r.v(lambda E, hp=hp, ic=ic, par=par: E.tensor_tensor(
                            oft_ap[64:128, hp * S + ic * 512:hp * S + (ic + 1) * 512],
                            ps_o[2 * par + 1][0:64, 0:512],
                            rbc[par][:, 512:1024], OP.mult))
                        t_mults[par] = (ev_m0, ev_m1)
                r.fence()
                r.flush(block)
        attn_batch.slot = 0

        for bi in range(2) if stop_after != "p0" else ():
            with contextlib.ExitStack() as pb:
                A = lambda name, shape, dt=F32: pb.enter_context(nc.sbuf_tensor(name, shape, dt))
                qT_sb = A("p1q", [128, 8 * S], BF16)
                kT_sb = A("p1k", [128, 8 * S], BF16)
                v_sb = A("p1v", [128, NTB * 16 * 65], BF16)
                with contextlib.ExitStack() as px:
                    A2 = lambda name, shape, dt=F32: px.enter_context(nc.sbuf_tensor(name, shape, dt))
                    w_sb = A2("p1w", [128, CT * D], BF16)
                    xch = [A2(f"p1x{i}", [128, CT * 512], BF16) for i in range(2)]
                    ps_a = [px.enter_context(nc.psum_tensor(f"p1pa{i}", [128, 512], F32))
                            for i in range(2)]
                    r = Rec(b)
                    r.fence()
                    ev_ones = r.v(lambda E: E.memset(
                        v_sb.ap().rearrange("p (j h d) -> p j h d", h=16, d=65)[:, :, :, 64:65],
                        1.0))
                    w3 = w_sb.ap().rearrange("p (c f) -> p c f", f=D)
                    psa_g = [None, None]
                    last_w_use = None
                    for which, srcw in enumerate((wqT, wkT, wvT)):
                        ev_w = wsplit(r, w_sb, srcw, D, D, CT, 128 * D, "p1w",
                                      wait=[last_w_use])
                        ld_g = {}
                        for ch in range(4):
                            ld = r.dma("gpsimd",
                                       xch[ch % 2][:, :].rearrange("p (c w) -> p c w", w=512),
                                       ftap(xnT, T, 0, 128, bi * S + ch * 512, 512,
                                            nblk=CT, blk_stride=128 * T),
                                       wait=[ld_g.get(ch - 2)], ch=f"p1x{ch % 2}")
                            x3 = xch[ch % 2].ap().rearrange("p (c w) -> p c w", w=512)
                            if which < 2:
                                colv, sc, dst = ((colv_qb, 1.0 / 32.0, qT_sb) if which == 0
                                                 else (colv_kb, 1.0, kT_sb))
                                for ft in range(8):
                                    rot = ft % 2
                                    for ct in range(CT):
                                        r.t(lambda E, ct=ct, ft=ft, rot=rot, x3=x3: E.matmul(
                                            ps_a[rot][:, :],
                                            w3[:, ct, ft * 128:(ft + 1) * 128],
                                            x3[:, ct, :], start=(ct == 0),
                                            stop=(ct == CT - 1)),
                                            wait=[ld, ev_w((ft + 1) * 128), psa_g[rot]],
                                            inc=(ct == CT - 1))
                                    ev_mm = ("eT", r.c["eT"])
                                    psa_g[rot] = r.v(
                                        lambda E, dst=dst, colv=colv, sc=sc, ft=ft, ch=ch,
                                        rot=rot: E.tensor_scalar(
                                            dst[:, ft * S + ch * 512: ft * S + (ch + 1) * 512],
                                            ps_a[rot][:, :], colv[:, ft:ft + 1], sc,
                                            OP.add, OP.mult), wait=[ev_mm])
                                    last_w_use = psa_g[rot]
                            else:
                                for tt4 in range(4):
                                    for fch in range(2):
                                        rot = (tt4 * 2 + fch) % 2
                                        for ct in range(CT):
                                            r.t(lambda E, x3=x3, ct=ct, tt4=tt4, fch=fch,
                                                rot=rot: E.matmul(
                                                    ps_a[rot][:, :],
                                                    x3[:, ct, tt4 * 128:(tt4 + 1) * 128],
                                                    w3[:, ct, fch * 512:(fch + 1) * 512],
                                                    start=(ct == 0), stop=(ct == CT - 1)),
                                                wait=[ld, ev_w((fch + 1) * 512), psa_g[rot]],
                                                inc=(ct == CT - 1))
                                        ev_mm = ("eT", r.c["eT"])
                                        jt = ch * 4 + tt4
                                        psa_g[rot] = r.v(
                                            lambda E, jt=jt, fch=fch, rot=rot:
                                            E.scalar_tensor_tensor(
                                                v_sb.ap().rearrange(
                                                    "p (j h d) -> p j h d", h=16, d=65)
                                                [:, jt, fch * 8:(fch + 1) * 8, 0:64],
                                                ps_a[rot].ap().rearrange(
                                                    "p (h d) -> p h d", d=64),
                                                0.0,
                                                bc_vb.ap().rearrange("p (h d) -> p h d", d=64)
                                                [:, fch * 8:(fch + 1) * 8, :],
                                                OP.add, OP.add),
                                            wait=[ev_mm, ev_ones])
                                        last_w_use = psa_g[rot]
                            ld_g[ch] = last_w_use
                    r.fence()
                    r.flush(block)
                oft_sb = A("p1o", [128, 8 * S], BF16)
                if bi == 0 and "dbg_qT" in debug:
                    r = Rec(b); r.fence()
                    r.dma("gpsimd", dbg_qT, qT_sb[:, :], ch="dbg")
                    r.dma("gpsimd", dbg_kT, kT_sb[:, :], ch="dbg")
                    r.dma("gpsimd", dbg_v, v_sb[:, :], ch="dbg")
                    r.fence(); r.flush(block)
                attn_batch(qT_sb.ap(), kT_sb.ap(), v_sb, oft_sb.ap(), is_l0=True)
                if bi == 0 and "dbg_oft" in debug:
                    r = Rec(b); r.fence()
                    r.dma("gpsimd", dbg_oft, oft_sb[:, :], ch="dbg")
                    r.fence(); r.flush(block)
                with contextlib.ExitStack() as px:
                    A2 = lambda name, shape, dt=F32: px.enter_context(nc.sbuf_tensor(name, shape, dt))
                    pj_sb = A2("p1pj", [128, CT * D], BF16)
                    xnt = [A2(f"p1xn{i}", [128, D]) for i in range(2)]
                    yt = [A2(f"p1y{i}", [128, D]) for i in range(2)]
                    sq = A2("p1sq", [128, D])
                    sm = A2("p1sm", [128, 1]); ss = A2("p1ss", [128, 1])
                    rv = A2("p1rv", [128, 1])
                    ps_p = [px.enter_context(nc.psum_tensor(f"p1pp{i}", [128, 512], F32))
                            for i in range(2)]
                    r = Rec(b)
                    r.fence()
                    ev_w = wsplit(r, pj_sb, projT, D, D, CT, 128 * D, "p1w")
                    o3 = oft_sb.ap().rearrange("p (c w) -> p c w", w=S)
                    pj3 = pj_sb.ap().rearrange("p (c f) -> p c f", f=D)
                    ev_done = {}
                    ev_ytile = {}
                    prev_fch = [None, None]
                    for tt in range(NTB):
                        gt = bi * NTB + tt
                        ldx = r.dma("sync", xnt[tt % 2][:, :],
                                    xn_tf[gt * 128:(gt + 1) * 128, :],
                                    wait=[ev_ytile.get(tt - 2)], ch=f"p1n{tt % 2}")
                        ev_fch = []
                        for fch in range(2):
                            for ct in range(CT):
                                r.t(lambda E, ct=ct, tt=tt, fch=fch: E.matmul(
                                    ps_p[fch][:, :], o3[:, ct, tt * 128:(tt + 1) * 128],
                                    pj3[:, ct, fch * 512:(fch + 1) * 512],
                                    start=(ct == 0), stop=(ct == CT - 1)),
                                    wait=[ev_w((fch + 1) * 512), prev_fch[fch]],
                                    inc=(ct == CT - 1))
                            ev_mm = ("eT", r.c["eT"])
                            evf = r.v(lambda E, tt=tt, fch=fch: E.scalar_tensor_tensor(
                                yt[tt % 2][:, fch * 512:(fch + 1) * 512], ps_p[fch][:, :], 0.0,
                                bc_projb[:, fch * 512:(fch + 1) * 512], OP.add, OP.add),
                                wait=[ev_mm, ev_done.get(tt - 2)])
                            ev_fch.append(evf)
                            prev_fch[fch] = evf
                        ev_add = r.v(lambda E, tt=tt: E.tensor_tensor(
                            yt[tt % 2][:, :], yt[tt % 2][:, :], xnt[tt % 2][:, :], OP.add),
                            wait=[ldx] + ev_fch)
                        ev_m = r.v(lambda E, tt=tt: E.tensor_reduce(
                            sm[:, :], yt[tt % 2][:, :], axis=AX, op=OP.add))
                        ev_q = r.s(lambda E, tt=tt: E.activation(
                            sq[:, :], yt[tt % 2][:, :], ACT.Square, accum_out=ss[:, 0:1]),
                            wait=[ev_add])
                        r.v(lambda E: E.tensor_scalar_mul(sm[:, :], sm[:, :], 1.0 / D),
                            wait=[ev_q], inc=False)
                        r.v(lambda E: E.tensor_scalar_mul(ss[:, :], ss[:, :], 1.0 / D),
                            inc=False)
                        r.v(lambda E: E.tensor_tensor(rv[:, :], sm[:, :], sm[:, :], OP.mult),
                            inc=False)
                        r.v(lambda E: E.tensor_tensor(rv[:, :], ss[:, :], rv[:, :],
                                                      OP.subtract), inc=False)
                        ev_v = r.v(lambda E: E.tensor_scalar_add(rv[:, :], rv[:, :], LN_EPS))
                        ev_s = r.s(lambda E: E.activation(rv[:, :], rv[:, :], ACT.Sqrt),
                                   wait=[ev_v])
                        r.v(lambda E: E.reciprocal(rv[:, :], rv[:, :]), wait=[ev_s], inc=False)
                        r.v(lambda E, tt=tt: E.tensor_scalar(
                            yt[tt % 2][:, :], yt[tt % 2][:, :], sm[:, 0:1], rv[:, 0:1],
                            OP.subtract, OP.mult), inc=False)
                        r.v(lambda E, tt=tt: E.tensor_tensor(
                            yt[tt % 2][:, :], yt[tt % 2][:, :], bc_lng[:, :], OP.mult), inc=False)
                        ev_y = r.v(lambda E, tt=tt: E.tensor_tensor(
                            yt[tt % 2][:, :], yt[tt % 2][:, :], bc_lnb[:, :], OP.add))
                        ev_ytile[tt] = ev_y
                        ev_done[tt] = r.dma("scalar", s0_tf[gt * 128:(gt + 1) * 128, :],
                                            yt[tt % 2][:, :], wait=[ev_y],
                                            ch=f"p1s{tt % 2}")
                    r.fence()
                    r.flush(block)

        if stop_after not in ("p0", "p1"):
            _phase_l0ffn(nc, b, block, Rec, attn_batch, loc=locals())
            if stop_after != "p2":
                _phase_hawk(nc, b, block, Rec, loc=locals())
                if stop_after != "p3":
                    _phase_l1ffn(nc, b, block, Rec, attn_batch, loc=locals())
                    if stop_after != "p4":
                        _phase_final(nc, b, block, Rec, loc=locals())

    return nc


class _G:
    def __init__(self, loc):
        self.__dict__.update(loc)


def _phase_l0ffn(nc, b, block, Rec, attn_batch, loc):
    g = _G(loc)
    debug = loc["debug"]
    loc_dbg = {k: loc[k] for k in ("dbg_xq", "dbg_wgq", "dbg_wuq", "dbg_sc")}
    loc_sc = None
    with contextlib.ExitStack() as pb:
        A = lambda name, shape, dt=F32: pb.enter_context(nc.sbuf_tensor(name, shape, dt))
        wgq_sb = A("p2wg", [128, CT * I_FFN], BF16)
        wuq_sb = A("p2wu", [128, CT * I_FFN], BF16)
        xq_sb = A("p2xq", [128, CT * T], BF16)
        loc_sc = A("p2dsc", [128, 2 * ITL + 2]) if "dbg_xq" in debug else None
        with contextlib.ExitStack() as px:
            A2 = lambda name, shape, dt=F32: px.enter_context(nc.sbuf_tensor(name, shape, dt))
            nat = [A2(f"p2n{i}", [128, D]) for i in range(2)]
            i8t = A2("p2i8", [128, D], I8)
            bfst = A2("p2bf", [128, D], BF16)
            red = A2("p2red", [128, 1])
            rr = A2("p2rr", [128, 2])
            pst = [px.enter_context(nc.psum_tensor(f"p2ps{i}", [128, 128], BF16))
                   for i in range(2)]
            r = Rec(b)
            r.fence()
            psg = [None, None]
            ld_g = {}
            eva_g = {}
            for wi, (nat_src, dst_sb, sc_dst) in enumerate(
                    ((g.w_gate, wgq_sb, g.sgt), (g.w_up, wuq_sb, g.sut))):
                for itl in range(ITL):
                    k = wi * ITL + itl
                    ld = r.dma("sync", nat[k % 2][:, :],
                               ftap(nat_src, D, itl * 128, 128, 0, D),
                               wait=[ld_g.get(k - 2)], ch=f"p2n{k % 2}")
                    r.v(lambda E, k=k: E.tensor_reduce(red[:, :], nat[k % 2][:, :], axis=AX,
                                                       op=OP.max, apply_absolute_value=True),
                        wait=[ld], inc=False)
                    evr = r.v(lambda E, k=k: E.reciprocal(rr[:, k % 2:k % 2 + 1], red[:, :]),
                              wait=[eva_g.get(k - 2)], inc=True)
                    r.v(lambda E, k=k: E.tensor_scalar_mul(rr[:, k % 2:k % 2 + 1],
                                                           rr[:, k % 2:k % 2 + 1], 127.0),
                        inc=False)
                    evm = r.v(lambda E: E.engine_nop())
                    eva = r.s(lambda E, k=k: E.activation(
                        nat[k % 2][:, :], nat[k % 2][:, :], ACT.Copy,
                        scale=rr[:, k % 2:k % 2 + 1], bias=MAGIC_RND), wait=[evm])
                    eva_g[k] = eva
                    r.v(lambda E, k=k: E.tensor_scalar(nat[k % 2][:, :], nat[k % 2][:, :],
                                                       -MAGIC_RND, -128.0, OP.add, OP.max),
                        wait=[eva], inc=False)
                    ev_bf = r.v(lambda E, k=k: E.tensor_scalar_min(bfst[:, :],
                                                                   nat[k % 2][:, :], 127.0))
                    ld_g[k] = ev_bf
                    r.v(lambda E, sc_dst=sc_dst, itl=itl: E.tensor_scalar_mul(
                        sc_dst[:, itl:itl + 1], red[:, :], 1.0 / 127.0), inc=False)
                    for ct in range(CT):
                        evt = r.t(lambda E, ct=ct: E.transpose(
                            pst[ct % 2][:, :], bfst[:, ct * 128:(ct + 1) * 128],
                            g.sb_idb[:, :]), wait=[ev_bf, psg[ct % 2]])
                        psg[ct % 2] = r.v(lambda E, dst_sb=dst_sb, ct=ct, itl=itl: E.tensor_copy(
                            dst_sb[:, ct * I_FFN + itl * 128: ct * I_FFN + (itl + 1) * 128],
                            pst[ct % 2][:, :]), wait=[evt])
            r.v(lambda E: E.tensor_scalar_mul(g.sgx[:, :], g.sgt[:, :], g.sig1s[:, 0:1]),
                wait=[psg[0], psg[1]], inc=False)
            r.v(lambda E: E.tensor_scalar_mul(g.sux[:, :], g.sut[:, :], g.sig1s[:, 0:1]),
                inc=False)
            ev_q = {}
            for k in range(4 * CT):
                ct, qt = k // 4, k % 4
                ld = r.dma("sync", nat[k % 2][:, :],
                           ftap(g.xnT, T, ct * 128, 128, qt * 1024, 1024),
                           wait=[ev_q.get(k - 2, ld_g.get(2 * ITL - 2 + (k % 2)))],
                           ch=f"p2n{k % 2}")
                eva = r.s(lambda E, k=k: E.activation(
                    nat[k % 2][:, :], nat[k % 2][:, :], ACT.Copy,
                    scale=g.sig1r[:, 0:1], bias=MAGIC_RND), wait=[ld])
                r.v(lambda E, k=k: E.tensor_scalar(
                    nat[k % 2][:, :], nat[k % 2][:, :], -MAGIC_RND, -128.0,
                    OP.add, OP.max), wait=[eva], inc=False)
                ev_q[k] = r.v(lambda E, k=k, ct=ct, qt=qt: E.tensor_scalar_min(
                    xq_sb[:, ct * T + qt * 1024: ct * T + (qt + 1) * 1024],
                    nat[k % 2][:, :], 127.0))
            r.fence()
            r.flush(block)

        if "dbg_xq" in debug:
            r = Rec(b); r.fence()
            r.dma("gpsimd", loc_dbg["dbg_xq"], xq_sb[:, :], ch="dbg")
            r.dma("gpsimd", loc_dbg["dbg_wgq"], wgq_sb[:, :], ch="dbg")
            r.dma("gpsimd", loc_dbg["dbg_wuq"], wuq_sb[:, :], ch="dbg")
            r.v(lambda E: E.tensor_copy(loc_sc[:, 0:ITL], g.sgt[:, :]))
            r.v(lambda E: E.tensor_copy(loc_sc[:, ITL:2 * ITL], g.sut[:, :]))
            r.v(lambda E: E.tensor_copy(loc_sc[:, 2 * ITL:2 * ITL + 1], g.sig1s[:, :]))
            ev = r.v(lambda E: E.tensor_copy(loc_sc[:, 2 * ITL + 1:2 * ITL + 2],
                                             g.sig1r[:, :]))
            r.dma("gpsimd", loc_dbg["dbg_sc"], loc_sc[:, :], wait=[ev], ch="dbg")
            r.fence(); r.flush(block)
        with contextlib.ExitStack() as px:
            A2 = lambda name, shape, dt=F32: px.enter_context(nc.sbuf_tensor(name, shape, dt))
            hstrip = [A2(f"p2h{i}", [128, 512]) for i in range(2)]
            silt = A2("p2sil", [128, 512])
            red = A2("p2red2", [128, 1])
            psG = [px.enter_context(nc.psum_tensor(f"p2pg{i}", [128, 512], F32)) for i in range(2)]
            psU = [px.enter_context(nc.psum_tensor(f"p2pu{i}", [128, 512], F32)) for i in range(2)]
            r = Rec(b)
            r.fence()
            x3 = xq_sb.ap().rearrange("p (c w) -> p c w", w=T)
            wg3 = wgq_sb.ap().rearrange("p (c i) -> p c i", i=I_FFN)
            wu3 = wuq_sb.ap().rearrange("p (c i) -> p c i", i=I_FFN)
            st_g = {}
            gg_ = [None, None]
            gu_ = [None, None]
            silt_g = None
            for itl in range(ITL):
                for ch in range(8):
                    k = itl * 8 + ch
                    rot = ch % 2
                    for ct in range(CT):
                        r.t(lambda E, ct=ct, itl=itl, ch=ch, rot=rot: E.matmul(
                            psG[rot][:, :], wg3[:, ct, itl * 128:(itl + 1) * 128],
                            x3[:, ct, ch * 512:(ch + 1) * 512],
                            start=(ct == 0), stop=(ct == CT - 1)),
                            wait=[gg_[rot]], inc=(ct == CT - 1))
                    mmg = ("eT", r.c["eT"])
                    for ct in range(CT):
                        r.t(lambda E, ct=ct, itl=itl, ch=ch, rot=rot: E.matmul(
                            psU[rot][:, :], wu3[:, ct, itl * 128:(itl + 1) * 128],
                            x3[:, ct, ch * 512:(ch + 1) * 512],
                            start=(ct == 0), stop=(ct == CT - 1)),
                            wait=[gu_[rot]], inc=(ct == CT - 1))
                    mmu = ("eT", r.c["eT"])
                    ev_sil = r.s(lambda E, itl=itl, rot=rot: E.activation(
                        silt[:, :], psG[rot][:, :], ACT.Silu, scale=g.sgx[:, itl:itl + 1]),
                        wait=[mmg, silt_g])
                    gg_[rot] = ev_sil
                    ev_h = r.v(lambda E, k=k, itl=itl, rot=rot: E.scalar_tensor_tensor(
                        hstrip[k % 2][:, :], psU[rot][:, :],
                        g.sux[:, itl:itl + 1], silt[:, :], OP.mult, OP.mult),
                        wait=[mmu, ev_sil, st_g.get(k - 2)])
                    gu_[rot] = ev_h
                    silt_g = ev_h
                    r.v(lambda E, k=k: E.tensor_reduce(red[:, :], hstrip[k % 2][:, :], axis=AX,
                                                       op=OP.max, apply_absolute_value=True),
                        inc=False)
                    evm = r.v(lambda E: E.tensor_max(g.rmax2[:, :], g.rmax2[:, :], red[:, :]))
                    st_g[k] = r.dma("scalar",
                                    ftap(g.hpre, T, itl * 128, 128, ch * 512, 512),
                                    hstrip[k % 2][:, :], wait=[evm], ch=f"p2h{k % 2}")
            r.fence()
            r.flush(block)

    loc["quant_scale_phase"](g.rmax2, g.cc_io[2], g.cc_io[3], g.clip2, g.sig2s, g.sig2r,
                             g.m128b)
    with contextlib.ExitStack() as pb:
        A = lambda name, shape, dt=F32: pb.enter_context(nc.sbuf_tensor(name, shape, dt))
        wdq_sb = A("p2wd", [128, ITL * D], BF16)
        with contextlib.ExitStack() as px:
            A2 = lambda name, shape, dt=F32: px.enter_context(nc.sbuf_tensor(name, shape, dt))
            nat = [A2(f"p2dn{i}", [128, I_FFN]) for i in range(2)]
            i8t = A2("p2di8", [128, I_FFN], I8)
            bfst = A2("p2dbf", [128, I_FFN], BF16)
            red = A2("p2dr", [128, 1]); rr = A2("p2drr", [128, 2])
            pst = [px.enter_context(nc.psum_tensor(f"p2dps{i}", [128, 128], BF16))
                   for i in range(2)]
            hq8 = A2("p2hq8", [128, T], I8)
            hqb = A2("p2hqb", [128, T], BF16)
            r = Rec(b)
            r.fence()
            psg = [None, None]
            ld_g = {}
            eva_g = {}
            ev_bf = None
            for ht in range(CT):
                ld = r.dma("sync", nat[ht % 2][:, :],
                           ftap(g.w_down, I_FFN, ht * 128, 128, 0, I_FFN),
                           wait=[ld_g.get(ht - 2)], ch=f"p2n{ht % 2}")
                r.v(lambda E, ht=ht: E.tensor_reduce(red[:, :], nat[ht % 2][:, :], axis=AX,
                                                     op=OP.max, apply_absolute_value=True),
                    wait=[ld], inc=False)
                r.v(lambda E, ht=ht: E.reciprocal(rr[:, ht % 2:ht % 2 + 1], red[:, :]),
                    wait=[eva_g.get(ht - 2)], inc=False)
                r.v(lambda E, ht=ht: E.tensor_scalar_mul(rr[:, ht % 2:ht % 2 + 1],
                                                         rr[:, ht % 2:ht % 2 + 1], 127.0),
                    inc=False)
                evm = r.v(lambda E: E.engine_nop())
                eva = r.s(lambda E, ht=ht: E.activation(
                    nat[ht % 2][:, :], nat[ht % 2][:, :], ACT.Copy,
                    scale=rr[:, ht % 2:ht % 2 + 1], bias=MAGIC_RND), wait=[evm])
                eva_g[ht] = eva
                r.v(lambda E, ht=ht: E.tensor_scalar(nat[ht % 2][:, :], nat[ht % 2][:, :],
                                                     -MAGIC_RND, -128.0, OP.add, OP.max),
                    wait=[eva], inc=False)
                ev_bf = r.v(lambda E, ht=ht: E.tensor_scalar_min(bfst[:, :],
                                                                 nat[ht % 2][:, :], 127.0))
                ld_g[ht] = ev_bf
                r.v(lambda E, ht=ht: E.tensor_scalar_mul(g.sdt[:, ht:ht + 1], red[:, :],
                                                         1.0 / 127.0), inc=False)
                for itl in range(ITL):
                    evt = r.t(lambda E, itl=itl: E.transpose(
                        pst[itl % 2][:, :], bfst[:, itl * 128:(itl + 1) * 128],
                        g.sb_idb[:, :]), wait=[ev_bf, psg[itl % 2]])
                    psg[itl % 2] = r.v(lambda E, itl=itl, ht=ht: E.tensor_copy(
                        wdq_sb[:, itl * D + ht * 128: itl * D + (ht + 1) * 128],
                        pst[itl % 2][:, :]), wait=[evt])
            ev1 = r.dma("scalar", bass.AP(g.sd_row.tensor, 0, [[1, 128], [128, CT]]),
                        g.sdt[:, 0:CT], wait=[psg[0], psg[1], ev_bf],
                        allow_slow_non_contiguous=True, ch="qs2")
            ev2 = r.dma("sync", g.bc_sd[:, :],
                        bass.AP(g.sd_row.tensor, 0, [[0, 128], [1, D]]), wait=[ev1],
                        ch="qs3")
            r.v(lambda E: E.tensor_scalar_mul(g.bc_sd[:, :], g.bc_sd[:, :], g.sig2s[:, 0:1]),
                wait=[ev2], inc=False)
            r.fence()
            r.flush(block)
        with contextlib.ExitStack() as px:
            A2 = lambda name, shape, dt=F32: px.enter_context(nc.sbuf_tensor(name, shape, dt))
            hpq = [A2(f"p2hp{i}", [128, ITL * 128]) for i in range(2)]
            hqblk = [A2(f"p2k{i}", [128, ITL * 512], BF16) for i in range(2)]
            ut = [A2(f"p2u{i}", [128, D]) for i in range(2)]
            gt32 = A2("p2g32", [128, D])
            xt_ = [A2(f"p2xx{i}", [128, D]) for i in range(2)]
            st_ = [A2(f"p2ss{i}", [128, D]) for i in range(2)]
            psD = [px.enter_context(nc.psum_tensor(f"p2pd{i}", [128, 512], F32))
                   for i in range(2)]
            r = Rec(b)
            r.fence()
            wd3 = wdq_sb.ap().rearrange("p (i f) -> p i f", f=D)
            ev_x1 = {}
            psg = [None, None]
            ld_g = {}
            qz_g = {}
            for tb in range(8):
                qdone = {}
                h3 = hqblk[tb % 2].ap().rearrange("p (i w) -> p i w", w=512)
                for q in range(4):
                    j = tb * 4 + q
                    ldh = r.dma("gpsimd",
                                hpq[j % 2][:, :].rearrange("p (i w) -> p i w", w=128),
                                ftap(g.hpre, T, 0, 128, tb * 512 + q * 128, 128,
                                     nblk=ITL, blk_stride=128 * T),
                                wait=[qz_g.get(j - 2)], ch=f"p2k{j % 2}")
                    eq1 = r.s(lambda E, j=j: E.activation(
                        hpq[j % 2][:, :], hpq[j % 2][:, :], ACT.Copy,
                        scale=g.sig2r[:, 0:1], bias=MAGIC_RND), wait=[ldh])
                    r.v(lambda E, j=j: E.tensor_scalar(
                        hpq[j % 2][:, :], hpq[j % 2][:, :], -MAGIC_RND, -128.0,
                        OP.add, OP.max), wait=[eq1, ld_g.get(tb - 2)],
                        inc=False)
                    ev = r.v(lambda E, j=j, q=q, h3=h3: E.tensor_scalar_min(
                        h3[:, :, q * 128:(q + 1) * 128],
                        hpq[j % 2][:, :].rearrange("p (i w) -> p i w", w=128), 127.0))
                    qz_g[j] = ev
                    qdone[q] = ev
                for t4 in range(4):
                    gt = tb * 4 + t4
                    ldx = r.dma("sync", xt_[gt % 2][:, :],
                                g.x_in[gt * 128:(gt + 1) * 128, :], wait=[ev_x1.get(gt - 2)],
                                ch=f"p2x{gt % 2}")
                    lds = r.dma("sync", st_[gt % 2][:, :],
                                g.s0_tf[gt * 128:(gt + 1) * 128, :], wait=[ev_x1.get(gt - 2)],
                                ch=f"p2s{gt % 2}")
                    ev_u = []
                    for fch in range(2):
                        for itl in range(ITL):
                            r.t(lambda E, itl=itl, t4=t4, fch=fch, h3=h3: E.matmul(
                                psD[fch][:, :], h3[:, itl, t4 * 128:(t4 + 1) * 128],
                                wd3[:, itl, fch * 512:(fch + 1) * 512],
                                start=(itl == 0), stop=(itl == ITL - 1)),
                                wait=[qdone[t4], psg[fch]], inc=(itl == ITL - 1))
                        mmd = ("eT", r.c["eT"])
                        evu = r.v(lambda E, gt=gt, fch=fch: E.scalar_tensor_tensor(
                            ut[gt % 2][:, fch * 512:(fch + 1) * 512], psD[fch][:, :], 0.0,
                            g.bc_sd[:, fch * 512:(fch + 1) * 512], OP.add, OP.mult),
                            wait=[mmd, ev_x1.get(gt - 2)])
                        ev_u.append(evu)
                        psg[fch] = evu
                    ev_g = r.s(lambda E, gt=gt: E.activation(gt32[:, :], ut[gt % 2][:, :],
                                                             ACT.Silu),
                               wait=ev_u + [ev_x1.get(gt - 1)])
                    r.v(lambda E, gt=gt: E.tensor_tensor(gt32[:, :], st_[gt % 2][:, :],
                                                         gt32[:, :], OP.mult),
                        wait=[ev_g, lds], inc=False)
                    ev_v = r.v(lambda E, gt=gt: E.tensor_tensor(
                        xt_[gt % 2][:, :], xt_[gt % 2][:, :], gt32[:, :], OP.add), wait=[ldx])
                    ev_x1[gt] = r.dma("scalar", g.x1_tf[gt * 128:(gt + 1) * 128, :],
                                      xt_[gt % 2][:, :], wait=[ev_v], ch=f"p2y{gt % 2}")
                ld_g[tb] = ev_x1[tb * 4 + 3]
            r.fence()
            r.flush(block)

    loc["ln_two_pass"](g.x1_tf, g.xn1_tf, g.xn1T, absmax_to=None, h1_also=True)


def _phase_hawk(nc, b, block, Rec, loc):
    g = _G(loc)
    with contextlib.ExitStack() as pb:
        A = lambda name, shape, dt=F32: pb.enter_context(nc.sbuf_tensor(name, shape, dt))
        xc_sb = A("p3xc", [128, CT * T], BF16)
        with contextlib.ExitStack() as px:
            A2 = lambda name, shape, dt=F32: px.enter_context(nc.sbuf_tensor(name, shape, dt))
            win_sb = A2("p3wi", [128, CT * 2 * D], BF16)
            xch = [A2(f"p3x{i}", [128, CT * 512], BF16) for i in range(2)]
            ggbuf = [A2(f"p3gg{i}", [128, 8 * 512], BF16) for i in range(2)]
            xap = A2("p3xa", [128, 515])
            tails = A2("p3tl", [128, CT * 3])
            acc = [A2(f"p3ac{i}", [128, 512]) for i in range(2)]
            ps_ = [px.enter_context(nc.psum_tensor(f"p3p{i}", [128, 512], F32))
                   for i in range(2)]
            r = Rec(b)
            r.fence()
            ev_w = wsplit(r, win_sb, g.w_inT, 2 * D, 2 * D, CT, 128 * 2 * D, "p3w")
            w3 = win_sb.ap().rearrange("p (c f) -> p c f", f=2 * D)
            psg = [None, None]
            ld_g = {}
            gg_st = {}
            for ch in range(8):
                ld = r.dma("gpsimd", xch[ch % 2][:, :].rearrange("p (c w) -> p c w", w=512),
                           ftap(g.xn1T, T, 0, 128, ch * 512, 512, nblk=CT, blk_stride=128 * T),
                           wait=[ld_g.get(ch - 2)], ch=f"p3x{ch % 2}")
                x3 = xch[ch % 2].ap().rearrange("p (c w) -> p c w", w=512)
                last_ev = None
                for ft in range(16):
                    rot = ft % 2
                    for ct in range(CT):
                        r.t(lambda E, ct=ct, ft=ft, rot=rot, x3=x3: E.matmul(
                            ps_[rot][:, :], w3[:, ct, ft * 128:(ft + 1) * 128], x3[:, ct, :],
                            start=(ct == 0), stop=(ct == CT - 1)),
                            wait=[ld, ev_w((ft + 1) * 128), psg[rot]], inc=(ct == CT - 1))
                    mm = ("eT", r.c["eT"])
                    if ft < 8:
                        ev = r.s(lambda E, ft=ft, ch=ch, rot=rot: E.activation(
                            ggbuf[ch % 2][:, ft * 512:(ft + 1) * 512], ps_[rot][:, :],
                            ACT.Gelu), wait=[mm, gg_st.get(ch - 2)])
                        psg[rot] = ev
                        gg_last = ev
                    else:
                        et = ft - 8
                        ev0 = r.v(lambda E, rot=rot: E.tensor_copy(xap[:, 3:515], ps_[rot][:, :]),
                                  wait=[mm, last_ev])
                        psg[rot] = ev0
                        if ch % 4 == 0:
                            r.v(lambda E: E.memset(xap[:, 0:3], 0.0), inc=False)
                        else:
                            r.v(lambda E, et=et: E.tensor_copy(
                                xap[:, 0:3], tails[:, et * 3:(et + 1) * 3]), inc=False)
                        r.v(lambda E, et=et: E.tensor_copy(
                            tails[:, et * 3:(et + 1) * 3], xap[:, 512:515]), inc=False)
                        wc4 = g.wconv_c.ap().rearrange("p (c k) -> p c k", k=4)
                        r.v(lambda E, et=et: E.tensor_scalar_mul(
                            acc[0][:, :], xap[:, 0:512], wc4[:, et, 0:1]), inc=False)
                        r.v(lambda E, et=et: E.scalar_tensor_tensor(
                            acc[1][:, :], xap[:, 1:513], wc4[:, et, 1:2], acc[0][:, :],
                            OP.mult, OP.add), inc=False)
                        r.v(lambda E, et=et: E.scalar_tensor_tensor(
                            acc[0][:, :], xap[:, 2:514], wc4[:, et, 2:3], acc[1][:, :],
                            OP.mult, OP.add), inc=False)
                        r.v(lambda E, et=et: E.scalar_tensor_tensor(
                            acc[1][:, :], xap[:, 3:515], wc4[:, et, 3:4], acc[0][:, :],
                            OP.mult, OP.add), inc=False)
                        last_ev = r.v(lambda E, et=et, ch=ch: E.tensor_single_scalar(
                            xc_sb[:, et * T + ch * 512: et * T + (ch + 1) * 512],
                            acc[1][:, :], g.colv_bc[:, et:et + 1], OP.add))
                ld_g[ch] = last_ev
                gg_st[ch] = r.dma("scalar",
                                  ftap(g.gg_d, T, 0, 128, ch * 512, 512, nblk=8,
                                       blk_stride=128 * T),
                                  ggbuf[ch % 2][:, :].rearrange("p (f w) -> p f w", w=512),
                                  wait=[psg[0], psg[1], gg_last], ch=f"p3g{ch % 2}")
            r.fence()
            r.flush(block)

        with contextlib.ExitStack() as px:
            A2 = lambda name, shape, dt=F32: px.enter_context(nc.sbuf_tensor(name, shape, dt))
            wgs = [A2(f"p3wg{i}", [128, CT * 256], BF16) for i in range(2)]
            ggs = A2("p3ggs", [128, T], BF16)
            ys = A2("p3ys", [128, T], BF16)
            alph = [A2(f"p3al{i}", [128, 8 * 512]) for i in range(2)]
            sgi = [A2(f"p3si{i}", [128, 8 * 512], BF16) for i in range(2)]
            sq8 = A2("p3sq", [128, 4 * 512])
            bet8 = A2("p3be", [128, 8 * 512], BF16)
            ub = A2("p3ub", [128, 512])
            hb = [A2(f"p3h{i}", [128, 512]) for i in range(2)]
            psF = [px.enter_context(nc.psum_tensor(f"p3pf{i}", [128, 512], F32))
                   for i in range(2)]
            psI = [px.enter_context(nc.psum_tensor(f"p3pi{i}", [128, 512], F32))
                   for i in range(2)]
            r = Rec(b)
            r.fence()
            xc3 = xc_sb.ap().rearrange("p (c w) -> p c w", w=T)
            y_st = {}
            gg_ld = {}
            pf_g = [None, None]
            pi_g = [None, None]
            scan_ev = {}
            ub_ev = {}
            sqr_ev = {}
            mm_done = {}
            ld_w = {}
            for gi in range(8):
                gp = gi % 2
                w4 = wgs[gp].ap().rearrange("p (c h f) -> p c h f", h=2, f=128)
                ev_wa = r.dma("gpsimd", w4[:, :, 0, :],
                              ftap(g.w_gatesT, 2 * D, 0, 128, gi * 128, 128, nblk=CT,
                                   blk_stride=128 * 2 * D),
                              wait=[mm_done.get(gi - 2)], ch=f"p3w{gp}")
                ev_w = r.dma("gpsimd", w4[:, :, 1, :],
                             ftap(g.w_gatesT, 2 * D, 0, 128, (8 + gi) * 128, 128,
                                  nblk=CT, blk_stride=128 * 2 * D), ch=f"p3w{gp}")
                ldg = r.dma("sync", ggs[:, :], ftap(g.gg_d, T, gi * 128, 128, 0, T),
                            wait=[gg_ld.get(gi - 1)], ch="p3gg")
                sigf_ev = {}
                sigi_ev = {}
                for k in range(8):
                    t0 = (k // 4) * S + (k % 4) * 512
                    rot = k % 2
                    for ct in range(CT):
                        r.t(lambda E, ct=ct, w4=w4, t0=t0, rot=rot: E.matmul(
                            psF[rot][:, :], w4[:, ct, 0, :],
                            xc3[:, ct, t0:t0 + 512],
                            start=(ct == 0), stop=(ct == CT - 1)),
                            wait=[ev_wa, ev_w, pf_g[rot]], inc=(ct == CT - 1))
                    mmf = ("eT", r.c["eT"])
                    for ct in range(CT):
                        r.t(lambda E, ct=ct, w4=w4, t0=t0, rot=rot: E.matmul(
                            psI[rot][:, :], w4[:, ct, 1, :],
                            xc3[:, ct, t0:t0 + 512],
                            start=(ct == 0), stop=(ct == CT - 1)),
                            wait=[ev_w, pi_g[rot]], inc=(ct == CT - 1))
                    mmi = ("eT", r.c["eT"])
                    if k == 7:
                        mm_done[gi] = mmi
                    e1 = r.s(lambda E, gi=gi, k=k, gp=gp, rot=rot: E.activation(
                        alph[gp][:, k * 512:(k + 1) * 512], psF[rot][:, :], ACT.Sigmoid,
                        bias=g.colv_bgf[:, gi:gi + 1]),
                        wait=[mmf, scan_ev.get((gi - 2, k))])
                    sigf_ev[k] = e1
                    pf_g[rot] = e1
                    e3 = r.s(lambda E, gi=gi, k=k, gp=gp, rot=rot: E.activation(
                        sgi[gp][:, k * 512:(k + 1) * 512], psI[rot][:, :], ACT.Sigmoid,
                        bias=g.colv_bgi[:, gi:gi + 1]),
                        wait=[mmi, ub_ev.get((gi - 2, k))])
                    sigi_ev[k] = e3
                    pi_g[rot] = e3
                exp_ev = {}
                for k in range(8):
                    exp_ev[k] = r.s(lambda E, gi=gi, k=k, gp=gp: E.activation(
                        alph[gp][:, k * 512:(k + 1) * 512],
                        alph[gp][:, k * 512:(k + 1) * 512], ACT.Exp,
                        scale=g.colv_ce[:, gi:gi + 1]), wait=[sigf_ev[k]])
                sq_ev = {}
                for k in range(8):
                    ks = k % 4
                    r.v(lambda E, k=k, ks=ks, gp=gp: E.tensor_tensor(
                        sq8[:, ks * 512:(ks + 1) * 512],
                        alph[gp][:, k * 512:(k + 1) * 512],
                        alph[gp][:, k * 512:(k + 1) * 512], OP.mult),
                        wait=[exp_ev[k],
                              sqr_ev.get((gi, k - 4), sqr_ev.get((gi - 1, k + 4)))],
                        inc=False)
                    sq_ev[k] = r.v(lambda E, ks=ks: E.tensor_scalar(
                        sq8[:, ks * 512:(ks + 1) * 512], sq8[:, ks * 512:(ks + 1) * 512],
                        -1.0, 1.0 + 1e-6, OP.mult, OP.add))
                    sqr_ev[(gi, k)] = r.s(lambda E, k=k, ks=ks: E.activation(
                        bet8[:, k * 512:(k + 1) * 512], sq8[:, ks * 512:(ks + 1) * 512],
                        ACT.Sqrt), wait=[sq_ev[k], ub_ev.get((gi - 1, k))])
                ev_y = None
                for k in range(8):
                    t0 = (k // 4) * S + (k % 4) * 512
                    e7 = r.v(lambda E, k=k, gp=gp: E.tensor_tensor(
                        ub[:, :], bet8[:, k * 512:(k + 1) * 512],
                        sgi[gp][:, k * 512:(k + 1) * 512], OP.mult),
                        wait=[sqr_ev[(gi, k)], sigi_ev[k]])
                    ub_ev[(gi, k)] = e7
                    r.v(lambda E, gi=gi, t0=t0: E.tensor_tensor(
                        ub[:, :], ub[:, :], xc3[:, gi, t0:t0 + 512], OP.mult),
                        inc=False)
                    init = 0.0 if k % 4 == 0 else hb[(k - 1) % 2][:, 511:512]
                    e9 = r.v(lambda E, k=k, gp=gp, init=init: E.tensor_tensor_scan(
                        hb[k % 2][:, :], alph[gp][:, k * 512:(k + 1) * 512], ub[:, :],
                        init, OP.mult, OP.add))
                    scan_ev[(gi, k)] = e9
                    ev_y = r.v(lambda E, gi=gi, t0=t0, k=k: E.tensor_tensor(
                        ys[:, t0:t0 + 512], ggs[:, t0:t0 + 512], hb[k % 2][:, :],
                        OP.mult), wait=[ldg, y_st.get(gi - 1)])
                gg_ld[gi] = ev_y
                y_st[gi] = r.dma("scalar", ftap(g.y_hk, T, gi * 128, 128, 0, T), ys[:, :],
                                 wait=[ev_y], ch="p3y")
            r.fence()
            r.flush(block)

        with contextlib.ExitStack() as px:
            A2 = lambda name, shape, dt=F32: px.enter_context(nc.sbuf_tensor(name, shape, dt))
            wo_sb = A2("p3wo", [128, CT * D], BF16)
            yblk = [A2(f"p3yb{i}", [128, CT * 512], BF16) for i in range(2)]
            s1t = [A2(f"p3s1{i}", [128, D]) for i in range(2)]
            psO = [px.enter_context(nc.psum_tensor(f"p3po{i}", [128, 512], F32))
                   for i in range(2)]
            r = Rec(b)
            r.fence()
            ev_w = wsplit(r, wo_sb, g.w_outT, D, D, CT, 128 * D, "p3w")
            wo3 = wo_sb.ap().rearrange("p (c f) -> p c f", f=D)
            st_g = {}
            ld_g = {}
            psg = [None, None]
            for tb in range(8):
                ld = r.dma("gpsimd", yblk[tb % 2][:, :].rearrange("p (c w) -> p c w", w=512),
                           ftap(g.y_hk, T, 0, 128, tb * 512, 512, nblk=CT, blk_stride=128 * T),
                           wait=[ld_g.get(tb - 2)], ch=f"p3b{tb % 2}")
                y3 = yblk[tb % 2].ap().rearrange("p (c w) -> p c w", w=512)
                for t4 in range(4):
                    gt = tb * 4 + t4
                    evs = []
                    for fch in range(2):
                        for et in range(CT):
                            r.t(lambda E, et=et, t4=t4, fch=fch, y3=y3: E.matmul(
                                psO[fch][:, :], y3[:, et, t4 * 128:(t4 + 1) * 128],
                                wo3[:, et, fch * 512:(fch + 1) * 512],
                                start=(et == 0), stop=(et == CT - 1)),
                                wait=[ld, ev_w((fch + 1) * 512), psg[fch]],
                                inc=(et == CT - 1))
                        mm = ("eT", r.c["eT"])
                        ev = r.v(lambda E, gt=gt, fch=fch: E.tensor_copy(
                            s1t[gt % 2][:, fch * 512:(fch + 1) * 512], psO[fch][:, :]),
                            wait=[mm, st_g.get(gt - 2)])
                        evs.append(ev)
                        psg[fch] = ev
                    st_g[gt] = r.dma("scalar", g.s1_tf[gt * 128:(gt + 1) * 128, :],
                                     s1t[gt % 2][:, :], wait=evs, ch=f"p3s{gt % 2}")
                ld_g[tb] = st_g[tb * 4 + 3]
            r.fence()
            r.flush(block)


def _phase_l1ffn(nc, b, block, Rec, attn_batch, loc):
    g = _G(loc)
    for bi in range(2):
        with contextlib.ExitStack() as pb:
            A = lambda name, shape, dt=F32: pb.enter_context(nc.sbuf_tensor(name, shape, dt))
            qT_sb = A("p4q", [128, 8 * S], BF16)
            kT_sb = A("p4k", [128, 8 * S], BF16)
            v_sb = A("p4v", [128, NTB * 16 * 65], BF16)
            with contextlib.ExitStack() as px:
                A2 = lambda name, shape, dt=F32: px.enter_context(nc.sbuf_tensor(name, shape, dt))
                w_sb = A2("p4w", [128, CT * D], BF16)
                hch = [A2(f"p4h{i}", [128, CT * 512], BF16) for i in range(2)]
                qtf = A2("p4qt", [128, D])
                q2 = A2("p4q2", [128, D])
                sums = A2("p4sm", [128, 16])
                qb16 = A2("p4qb", [128, D], BF16)
                ps_q = [px.enter_context(nc.psum_tensor(f"p4pq{i}", [128, 512], F32))
                        for i in range(2)]
                ps_t = [px.enter_context(nc.psum_tensor(f"p4pt{i}", [128, 128], BF16))
                        for i in range(2)]
                r = Rec(b)
                r.fence()
                ev_ones = r.v(lambda E: E.memset(
                    v_sb.ap().rearrange("p (j h d) -> p j h d", h=16, d=65)[:, :, :, 64:65],
                    1.0))
                w3 = w_sb.ap().rearrange("p (c f) -> p c f", f=D)
                psq_g = [None, None]
                pst_g = [None, None]
                last_w_use = None
                ev_tile = None
                for which, srcw in enumerate((g.l1wqT, g.l1wkT, g.l1wvT)):
                    ev_w = wsplit(r, w_sb, srcw, D, D, CT, 128 * D, "p4w",
                                  wait=[last_w_use])
                    ld_g = {}
                    for ch in range(4):
                        ld = r.dma("gpsimd",
                                   hch[ch % 2][:, :].rearrange("p (c w) -> p c w", w=512),
                                   ftap(g.h1T_d, T, 0, 128, bi * S + ch * 512, 512,
                                        nblk=CT, blk_stride=128 * T), wait=[ld_g.get(ch - 2)],
                                   ch=f"p4h{ch % 2}")
                        h3 = hch[ch % 2].ap().rearrange("p (c w) -> p c w", w=512)
                        for t4 in range(4):
                            jt = ch * 4 + t4
                            if which < 2:
                                bcg, dstT = ((g.bc_qn, qT_sb) if which == 0
                                             else (g.bc_kn, kT_sb))
                                evf = []
                                for fch in range(2):
                                    for ct in range(CT):
                                        r.t(lambda E, h3=h3, ct=ct, t4=t4, fch=fch: E.matmul(
                                            ps_q[fch][:, :],
                                            h3[:, ct, t4 * 128:(t4 + 1) * 128],
                                            w3[:, ct, fch * 512:(fch + 1) * 512],
                                            start=(ct == 0), stop=(ct == CT - 1)),
                                            wait=[ld, ev_w((fch + 1) * 512), psq_g[fch]],
                                            inc=(ct == CT - 1))
                                    mm = ("eT", r.c["eT"])
                                    ev = r.v(lambda E, fch=fch: E.tensor_copy(
                                        qtf[:, fch * 512:(fch + 1) * 512], ps_q[fch][:, :]),
                                        wait=[mm, ev_tile])
                                    evf.append(ev)
                                    psq_g[fch] = ev
                                    last_w_use = ev
                                e1 = r.s(lambda E: E.activation(q2[:, :], qtf[:, :],
                                                                ACT.Square), wait=evf)
                                e2 = r.v(lambda E: E.tensor_reduce(
                                    sums.ap().unsqueeze(2),
                                    q2.ap().rearrange("p (h d) -> p h d", d=HD), axis=AX,
                                    op=OP.add), wait=[e1])
                                e3 = r.v(lambda E: E.tensor_scalar(
                                    sums[:, :], sums[:, :], 1.0 / HD, RMS_EPS,
                                    OP.mult, OP.add))
                                e4 = r.s(lambda E: E.activation(sums[:, :], sums[:, :],
                                                                ACT.Sqrt), wait=[e3])
                                e5 = r.v(lambda E: E.reciprocal(sums[:, :], sums[:, :]),
                                         wait=[e4])
                                rfb = bass.AP(sums.ap().tensor, sums.ap().offset,
                                              [[16, 128], [1, 16], [0, HD]])
                                qnb = bass.AP(bcg.ap().tensor, bcg.ap().offset,
                                              [[HD, 128], [0, 16], [1, HD]])
                                r.v(lambda E, rfb=rfb: E.tensor_tensor(
                                    qtf.ap().rearrange("p (h d) -> p h d", d=HD),
                                    qtf.ap().rearrange("p (h d) -> p h d", d=HD), rfb,
                                    OP.mult), inc=False)
                                e6 = r.v(lambda E, qnb=qnb: E.tensor_tensor(
                                    qb16.ap().rearrange("p (h d) -> p h d", d=HD),
                                    qtf.ap().rearrange("p (h d) -> p h d", d=HD), qnb,
                                    OP.mult))
                                for ct in range(CT):
                                    evt = r.t(lambda E, ct=ct: E.transpose(
                                        ps_t[ct % 2][:, :], qb16[:, ct * 128:(ct + 1) * 128],
                                        g.sb_idb[:, :]), wait=[e6, pst_g[ct % 2]])
                                    ev_tile = r.v(
                                        lambda E, dstT=dstT, ct=ct, jt=jt: E.tensor_copy(
                                            dstT[:, ct * S + jt * 128:
                                                 ct * S + (jt + 1) * 128],
                                            ps_t[ct % 2][:, :]), wait=[evt])
                                    pst_g[ct % 2] = ev_tile
                            else:
                                for fch in range(2):
                                    for ct in range(CT):
                                        r.t(lambda E, h3=h3, ct=ct, t4=t4, fch=fch: E.matmul(
                                            ps_q[fch][:, :],
                                            h3[:, ct, t4 * 128:(t4 + 1) * 128],
                                            w3[:, ct, fch * 512:(fch + 1) * 512],
                                            start=(ct == 0), stop=(ct == CT - 1)),
                                            wait=[ld, ev_w((fch + 1) * 512), psq_g[fch]],
                                            inc=(ct == CT - 1))
                                    mm = ("eT", r.c["eT"])
                                    psq_g[fch] = r.v(lambda E, jt=jt, fch=fch: E.tensor_copy(
                                        v_sb.ap().rearrange("p (j h d) -> p j h d",
                                                            h=16, d=65)
                                        [:, jt, fch * 8:(fch + 1) * 8, 0:64],
                                        ps_q[fch].ap().rearrange("p (h d) -> p h d", d=HD)),
                                        wait=[mm, ev_ones])
                                    last_w_use = psq_g[fch]
                        ld_g[ch] = last_w_use
                r.fence()
                r.flush(block)
            oft_sb = A("p4o", [128, 8 * S], BF16)
            attn_batch(qT_sb.ap(), kT_sb.ap(), v_sb, oft_sb.ap(), is_l0=False)
            with contextlib.ExitStack() as px:
                A2 = lambda name, shape, dt=F32: px.enter_context(nc.sbuf_tensor(name, shape, dt))
                wo_sb = A2("p4wo", [128, CT * D], BF16)
                xnt = [A2(f"p4xn{i}", [128, D]) for i in range(2)]
                xit = [A2(f"p4xi{i}", [128, D]) for i in range(2)]
                x2t = A2("p4x2", [128, D])
                rsum = A2("p4rs", [128, 1])
                xf16 = A2("p4xf", [128, D], BF16)
                xfb = [A2(f"p4fb{i}", [128, CT * 128], BF16) for i in range(2)]
                ps_w = [px.enter_context(nc.psum_tensor(f"p4pw{i}", [128, 512], F32))
                        for i in range(2)]
                ps_t = [px.enter_context(nc.psum_tensor(f"p4pt2{i}", [128, 128], BF16))
                        for i in range(2)]
                r = Rec(b)
                r.fence()
                ev_w = wsplit(r, wo_sb, g.l1woT, D, D, CT, 128 * D, "p4w")
                o3 = oft_sb.ap().rearrange("p (c w) -> p c w", w=S)
                wo3 = wo_sb.ap().rearrange("p (c f) -> p c f", f=D)
                ev_done = {}
                ev_xfst = {}
                psw_g = [None, None]
                pst_g = [None, None]
                for tt in range(NTB):
                    gt = bi * NTB + tt
                    ldn = r.dma("sync", xnt[tt % 2][:, :],
                                g.xn1_tf[gt * 128:(gt + 1) * 128, :],
                                wait=[ev_done.get(tt - 2)], ch=f"p4n{tt % 2}")
                    spans = [(MZ, 512), (768, 256)]
                    ev_ao = []
                    for fi, (c0, cw) in enumerate(spans):
                        for ct in range(CT):
                            r.t(lambda E, ct=ct, tt=tt, fi=fi, c0=c0, cw=cw: E.matmul(
                                ps_w[fi][:, 0:cw], o3[:, ct, tt * 128:(tt + 1) * 128],
                                wo3[:, ct, c0:c0 + cw],
                                start=(ct == 0), stop=(ct == CT - 1)),
                                wait=[ev_w(c0 + cw), psw_g[fi]], inc=(ct == CT - 1))
                        mm = ("eT", r.c["eT"])
                        ev = r.v(lambda E, tt=tt, fi=fi, c0=c0, cw=cw: E.scalar_tensor_tensor(
                            xit[tt % 2][:, c0:c0 + cw], ps_w[fi][:, 0:cw], 0.0,
                            xnt[tt % 2][:, c0:c0 + cw], OP.add, OP.add),
                            wait=[mm, ldn, ev_done.get(tt - 2)])
                        ev_ao.append(ev)
                        psw_g[fi] = ev
                    ev_cp = r.v(lambda E, tt=tt: E.tensor_copy(
                        xit[tt % 2][:, 0:MZ], xnt[tt % 2][:, 0:MZ]), wait=[ldn])
                    ev_done[tt] = r.dma("scalar", g.xi_tf[gt * 128:(gt + 1) * 128, :],
                                        xit[tt % 2][:, :], wait=ev_ao + [ev_cp],
                                        ch=f"p4i{tt % 2}")
                    e1 = r.s(lambda E, tt=tt: E.activation(x2t[:, :], xit[tt % 2][:, :],
                                                           ACT.Square), wait=ev_ao + [ev_cp])
                    e2 = r.v(lambda E: E.tensor_reduce(rsum[:, :], x2t[:, :], axis=AX,
                                                       op=OP.add), wait=[e1])
                    e3 = r.v(lambda E: E.tensor_scalar(rsum[:, :], rsum[:, :], 1.0 / D,
                                                       RMS_EPS, OP.mult, OP.add))
                    e4 = r.s(lambda E: E.activation(rsum[:, :], rsum[:, :], ACT.Sqrt), wait=[e3])
                    e5 = r.v(lambda E: E.reciprocal(rsum[:, :], rsum[:, :]), wait=[e4])
                    r.v(lambda E, tt=tt: E.tensor_scalar_mul(x2t[:, :], xit[tt % 2][:, :],
                                                             rsum[:, 0:1]), inc=False)
                    e6 = r.v(lambda E: E.tensor_tensor(xf16[:, :], x2t[:, :], g.bc_n2[:, :],
                                                       OP.mult))
                    for ct in range(CT):
                        evt = r.t(lambda E, ct=ct: E.transpose(
                            ps_t[ct % 2][:, :], xf16[:, ct * 128:(ct + 1) * 128],
                            g.sb_idb[:, :]), wait=[e6, pst_g[ct % 2]])
                        pst_g[ct % 2] = r.v(lambda E, tt=tt, ct=ct: E.tensor_copy(
                            xfb[tt % 2][:, ct * 128:(ct + 1) * 128], ps_t[ct % 2][:, :]),
                            wait=[evt, ev_xfst.get(tt - 2) if ct == 0 else None])
                    ev_xfst[tt] = r.dma("scalar",
                                        ftap(g.xfT_d, T, 0, 128, gt * 128, 128, nblk=CT,
                                             blk_stride=128 * T),
                                        xfb[tt % 2][:, :].rearrange("p (c w) -> p c w", w=128),
                                        wait=[pst_g[(CT - 1) % 2]], ch=f"p4f{tt % 2}")
                r.fence()
                r.flush(block)

    with contextlib.ExitStack() as px:
        A2 = lambda name, shape, dt=F32: px.enter_context(nc.sbuf_tensor(name, shape, dt))
        gw_sb = A2("p4gw", [128, CT * 4], BF16)
        xfblk = [A2(f"p4xb{i}", [128, CT * 128], BF16) for i in range(2)]
        lg = A2("p4lg", [128, 4])
        mx = A2("p4mx", [128, 1])
        sume = A2("p4su", [128, 1])
        m1 = A2("p4m1", [128, 1]); m2 = A2("p4m2", [128, 1])
        msk = A2("p4mk", [128, 4]); pm = A2("p4pm", [128, 4])
        den = A2("p4de", [128, 1])
        ps_g = px.enter_context(nc.psum_tensor("p4pg", [128, 4], F32))
        r = Rec(b)
        r.fence()
        ev_w = r.dma("gpsimd", gw_sb[:, :].rearrange("p (c f) -> p c f", f=4),
                     ftap(g.gate_wT, 4, 0, 128, 0, 4, nblk=CT, blk_stride=128 * 4),
                     ch="p4w")
        gw3 = gw_sb.ap().rearrange("p (c f) -> p c f", f=4)
        ev_fin = {}
        for tt in range(NT):
            ld = r.dma("sync", xfblk[tt % 2][:, :].rearrange("p (c w) -> p c w", w=128),
                       ftap(g.xfT_d, T, 0, 128, tt * 128, 128, nblk=CT, blk_stride=128 * T),
                       wait=[ev_fin.get(tt - 2)], ch=f"p4b{tt % 2}")
            x3 = xfblk[tt % 2].ap().rearrange("p (c w) -> p c w", w=128)
            for ct in range(CT):
                r.t(lambda E, ct=ct, x3=x3: E.matmul(
                    ps_g[:, :], x3[:, ct, :], gw3[:, ct, :],
                    start=(ct == 0), stop=(ct == CT - 1)),
                    wait=[ld, ev_w, ev_fin.get(tt - 1)], inc=(ct == CT - 1))
            mm = ("eT", r.c["eT"])
            e1 = r.v(lambda E: E.tensor_copy(lg[:, :], ps_g[:, :]), wait=[mm])
            e2 = r.v(lambda E: E.tensor_reduce(mx[:, :], lg[:, :], axis=AX, op=OP.max,
                                               negate=True))
            e3 = r.s(lambda E: E.activation(lg[:, :], lg[:, :], ACT.Exp, bias=mx[:, 0:1],
                                            accum_out=sume[:, 0:1]), wait=[e2])
            e4 = r.v(lambda E: E.reciprocal(den[:, :], sume[:, :]), wait=[e3])
            r.v(lambda E: E.tensor_scalar_mul(lg[:, :], lg[:, :], den[:, 0:1]), inc=False)
            r.v(lambda E: E.tensor_reduce(m1[:, :], lg[:, :], axis=AX, op=OP.max), inc=False)
            r.v(lambda E: E.tensor_scalar(msk[:, :], lg[:, :], m1[:, 0:1], None, OP.is_lt),
                inc=False)
            r.v(lambda E: E.tensor_tensor(pm[:, :], lg[:, :], msk[:, :], OP.mult), inc=False)
            r.v(lambda E: E.tensor_reduce(m2[:, :], pm[:, :], axis=AX, op=OP.max), inc=False)
            r.v(lambda E: E.tensor_tensor(den[:, :], m1[:, :], m2[:, :], OP.add), inc=False)
            r.v(lambda E: E.tensor_scalar_add(den[:, :], den[:, :], 1e-8), inc=False)
            r.v(lambda E: E.reciprocal(den[:, :], den[:, :]), inc=False)
            r.v(lambda E: E.tensor_scalar(msk[:, :], lg[:, :], m2[:, 0:1], None, OP.is_ge),
                inc=False)
            r.v(lambda E: E.tensor_tensor(pm[:, :], lg[:, :], msk[:, :], OP.mult), inc=False)
            ev_fin[tt] = r.v(lambda E, tt=tt: E.tensor_scalar_mul(
                g.wts_sb[:, tt * 4:(tt + 1) * 4], pm[:, :], den[:, 0:1]))
        r.fence()
        r.flush(block)

    def stream_xf(r, A2, pfx):
        xfch = [A2(f"{pfx}_x{i}", [128, CT * 512], BF16) for i in range(2)]
        def load(ch, wait):
            return xfch[ch % 2], r.dma(
                "gpsimd", xfch[ch % 2][:, :].rearrange("p (c w) -> p c w", w=512),
                ftap(g.xfT_d, T, 0, 128, ch * 512, 512, nblk=CT, blk_stride=128 * T),
                wait=wait, ch=f"{pfx}x{ch % 2}")
        return load

    with contextlib.ExitStack() as px:
        A2 = lambda name, shape, dt=F32: px.enter_context(nc.sbuf_tensor(name, shape, dt))
        up_sb = A2("e0up", [128, CT * HID], BF16)
        dn_sb = A2("e0dn", [128, 16 * D], BF16)
        m_sb = A2("e0m", [128, 16 * 512], BF16)
        ot = [A2(f"e0o{i}", [128, D - MZ]) for i in range(2)]
        psM = [px.enter_context(nc.psum_tensor(f"e0pm{i}", [128, 512], F32)) for i in range(2)]
        psO = [px.enter_context(nc.psum_tensor(f"e0po{i}", [128, 512], F32)) for i in range(2)]
        r = Rec(b)
        r.fence()
        evw1 = wsplit(r, up_sb, g.e0_upT, HID, HID, CT, 128 * HID, "ew0")
        evw2 = wsplit(r, dn_sb, g.e0_downT, D, D, 16, 128 * D, "ew1")
        loadxf = stream_xf(r, A2, "e0")
        up3 = up_sb.ap().rearrange("p (c f) -> p c f", f=HID)
        dn3 = dn_sb.ap().rearrange("p (i f) -> p i f", f=D)
        m3 = m_sb.ap().rearrange("p (i w) -> p i w", w=512)
        psm_g = [None, None]
        pso_g = [None, None]
        ld_g = {}
        st_g = {}
        for ch in range(8):
            xf_t, ld = loadxf(ch, [ld_g.get(ch - 2)])
            x3 = xf_t.ap().rearrange("p (c w) -> p c w", w=512)
            ev_m = None
            for ft in range(16):
                rot = ft % 2
                for ct in range(CT):
                    r.t(lambda E, x3=x3, ct=ct, ft=ft, rot=rot: E.matmul(
                        psM[rot][:, :], up3[:, ct, ft * 128:(ft + 1) * 128], x3[:, ct, :],
                        start=(ct == 0), stop=(ct == CT - 1)),
                        wait=[ld, evw1((ft + 1) * 128), psm_g[rot]], inc=(ct == CT - 1))
                mm = ("eT", r.c["eT"])
                ev_m = r.s(lambda E, ft=ft, rot=rot: E.activation(
                    m3[:, ft, :], psM[rot][:, :], ACT.Gelu), wait=[mm, ld_g.get(ch - 1)])
                psm_g[rot] = ev_m
            for t4 in range(4):
                gt = ch * 4 + t4
                evo = []
                for fi, (c0, cw) in enumerate(((MZ, 512), (768, 256))):
                    for it in range(16):
                        r.t(lambda E, it=it, t4=t4, fi=fi, c0=c0, cw=cw: E.matmul(
                            psO[fi][:, 0:cw], m3[:, it, t4 * 128:(t4 + 1) * 128],
                            dn3[:, it, c0:c0 + cw],
                            start=(it == 0), stop=(it == 15)),
                            wait=[ev_m, evw2(c0 + cw), pso_g[fi]], inc=(it == 15))
                    mm = ("eT", r.c["eT"])
                    ev = r.v(lambda E, gt=gt, fi=fi, c0=c0, cw=cw: E.tensor_scalar_mul(
                        ot[gt % 2][:, c0 - MZ:c0 - MZ + cw], psO[fi][:, 0:cw],
                        g.wts_sb[:, gt * 4 + 0: gt * 4 + 1]),
                        wait=[mm, st_g.get(gt - 2)])
                    evo.append(ev)
                    pso_g[fi] = ev
                st_g[gt] = r.dma("scalar", g.spill[0][gt * 128:(gt + 1) * 128, :],
                                 ot[gt % 2][:, :], wait=evo, ch=f"es{gt % 2}")
            ld_g[ch] = st_g[ch * 4 + 3]
        r.fence()
        r.flush(block)

    with contextlib.ExitStack() as px:
        A2 = lambda name, shape, dt=F32: px.enter_context(nc.sbuf_tensor(name, shape, dt))
        w1_sb = A2("e1w1", [128, CT * HID], BF16)
        w2_sb = A2("e1w2", [128, CT * HID], BF16)
        dn_sb = A2("e1dn", [128, 16 * D], BF16)
        m_sb = A2("e1m", [128, 16 * 512], BF16)
        silt = A2("e1sl", [128, 512])
        ot = [A2(f"e1o{i}", [128, D - MZ]) for i in range(2)]
        psA = [px.enter_context(nc.psum_tensor(f"e1pa{i}", [128, 512], F32)) for i in range(2)]
        psB = [px.enter_context(nc.psum_tensor(f"e1pb{i}", [128, 512], F32)) for i in range(2)]
        psO = [px.enter_context(nc.psum_tensor(f"e1po{i}", [128, 512], F32)) for i in range(2)]
        r = Rec(b)
        r.fence()
        evw1 = wsplit(r, w1_sb, g.e1_w1T, HID, HID, CT, 128 * HID, "ew0")
        evw2 = wsplit(r, w2_sb, g.e1_w2T, HID, HID, CT, 128 * HID, "ew1")
        evw3 = wsplit(r, dn_sb, g.e1_downT, D, D, 16, 128 * D, "ew2")
        loadxf = stream_xf(r, A2, "e1")
        w13 = w1_sb.ap().rearrange("p (c f) -> p c f", f=HID)
        w23 = w2_sb.ap().rearrange("p (c f) -> p c f", f=HID)
        dn3 = dn_sb.ap().rearrange("p (i f) -> p i f", f=D)
        m3 = m_sb.ap().rearrange("p (i w) -> p i w", w=512)
        psa_g = [None, None]; psb_g = [None, None]; pso_g = [None, None]
        ld_g = {}; st_g = {}
        silt_g = None
        for ch in range(8):
            xf_t, ld = loadxf(ch, [ld_g.get(ch - 2)])
            x3 = xf_t.ap().rearrange("p (c w) -> p c w", w=512)
            ev_m = None
            for ft in range(16):
                rot = ft % 2
                for ct in range(CT):
                    r.t(lambda E, x3=x3, ct=ct, ft=ft, rot=rot: E.matmul(
                        psA[rot][:, :], w13[:, ct, ft * 128:(ft + 1) * 128], x3[:, ct, :],
                        start=(ct == 0), stop=(ct == CT - 1)),
                        wait=[ld, evw1((ft + 1) * 128), psa_g[rot]], inc=(ct == CT - 1))
                mma = ("eT", r.c["eT"])
                for ct in range(CT):
                    r.t(lambda E, x3=x3, ct=ct, ft=ft, rot=rot: E.matmul(
                        psB[rot][:, :], w23[:, ct, ft * 128:(ft + 1) * 128], x3[:, ct, :],
                        start=(ct == 0), stop=(ct == CT - 1)),
                        wait=[evw2((ft + 1) * 128), psb_g[rot]], inc=(ct == CT - 1))
                mmb = ("eT", r.c["eT"])
                evs = r.s(lambda E, rot=rot: E.activation(silt[:, :], psA[rot][:, :], ACT.Silu),
                          wait=[mma, psa_g[rot], silt_g])
                psa_g[rot] = evs
                ev_m = r.v(lambda E, ft=ft, rot=rot: E.scalar_tensor_tensor(
                    m3[:, ft, :], psB[rot][:, :], 0.0, silt[:, :], OP.add, OP.mult),
                    wait=[mmb, evs, ld_g.get(ch - 1)])
                psb_g[rot] = ev_m
                silt_g = ev_m
            for t4 in range(4):
                gt = ch * 4 + t4
                evo = []
                for fi, (c0, cw) in enumerate(((MZ, 512), (768, 256))):
                    for it in range(16):
                        r.t(lambda E, it=it, t4=t4, fi=fi, c0=c0, cw=cw: E.matmul(
                            psO[fi][:, 0:cw], m3[:, it, t4 * 128:(t4 + 1) * 128],
                            dn3[:, it, c0:c0 + cw], start=(it == 0), stop=(it == 15)),
                            wait=[ev_m, evw3(c0 + cw), pso_g[fi]], inc=(it == 15))
                    mm = ("eT", r.c["eT"])
                    ev = r.v(lambda E, gt=gt, fi=fi, c0=c0, cw=cw: E.tensor_scalar_mul(
                        ot[gt % 2][:, c0 - MZ:c0 - MZ + cw], psO[fi][:, 0:cw],
                        g.wts_sb[:, gt * 4 + 1: gt * 4 + 2]),
                        wait=[mm, st_g.get(gt - 2)])
                    evo.append(ev)
                    pso_g[fi] = ev
                st_g[gt] = r.dma("scalar", g.spill[1][gt * 128:(gt + 1) * 128, :],
                                 ot[gt % 2][:, :], wait=evo, ch=f"es{gt % 2}")
            ld_g[ch] = st_g[ch * 4 + 3]
        r.fence()
        r.flush(block)

    with contextlib.ExitStack() as px:
        A2 = lambda name, shape, dt=F32: px.enter_context(nc.sbuf_tensor(name, shape, dt))
        l1_sb = A2("e2l1", [128, CT * HID], BF16)
        l2_sb = A2("e2l2", [128, 16 * HID], BF16)
        m1_sb = A2("e2m1", [128, 16 * 512], BF16)
        m2_sb = [A2(f"e2m2{i}", [128, 16 * 512], BF16) for i in range(2)]
        psM = [px.enter_context(nc.psum_tensor(f"e2pm{i}", [128, 512], F32)) for i in range(2)]
        r = Rec(b)
        r.fence()
        evw1 = wsplit(r, l1_sb, g.e2_l1T, HID, HID, CT, 128 * HID, "ew0")
        evw2 = wsplit(r, l2_sb, g.e2_l2T, HID, HID, 16, 128 * HID, "ew1")
        loadxf = stream_xf(r, A2, "e2")
        l13 = l1_sb.ap().rearrange("p (c f) -> p c f", f=HID)
        l23 = l2_sb.ap().rearrange("p (i f) -> p i f", f=HID)
        m13 = m1_sb.ap().rearrange("p (i w) -> p i w", w=512)
        psg = [None, None]
        ld_g = {}; st_g = {}
        for ch in range(8):
            xf_t, ld = loadxf(ch, [ld_g.get(ch - 2)])
            x3 = xf_t.ap().rearrange("p (c w) -> p c w", w=512)
            ev_m1 = None
            for ft in range(16):
                rot = ft % 2
                for ct in range(CT):
                    r.t(lambda E, x3=x3, ct=ct, ft=ft, rot=rot: E.matmul(
                        psM[rot][:, :], l13[:, ct, ft * 128:(ft + 1) * 128], x3[:, ct, :],
                        start=(ct == 0), stop=(ct == CT - 1)),
                        wait=[ld, evw1((ft + 1) * 128), psg[rot]], inc=(ct == CT - 1))
                mm = ("eT", r.c["eT"])
                ev_m1 = r.s(lambda E, ft=ft, rot=rot: E.activation(
                    m13[:, ft, :], psM[rot][:, :], ACT.Gelu), wait=[mm, ld_g.get(ch - 1)])
                psg[rot] = ev_m1
            m23 = m2_sb[ch % 2].ap().rearrange("p (i w) -> p i w", w=512)
            ev_m2l = None
            for ft in range(16):
                rot = ft % 2
                for it in range(16):
                    r.t(lambda E, ct=it, ft=ft, rot=rot, m13=m13: E.matmul(
                        psM[rot][:, :], l23[:, ct, ft * 128:(ft + 1) * 128], m13[:, ct, :],
                        start=(ct == 0), stop=(ct == 15)),
                        wait=[ev_m1, evw2((ft + 1) * 128), psg[rot]], inc=(it == 15))
                mm = ("eT", r.c["eT"])
                ev_m2l = r.s(lambda E, ft=ft, rot=rot, m23=m23: E.activation(
                    m23[:, ft, :], psM[rot][:, :], ACT.Gelu),
                    wait=[mm, st_g.get(ch - 2)])
                psg[rot] = ev_m2l
            st_g[ch] = r.dma("scalar",
                             ftap(g.m2_d, T, 0, 128, ch * 512, 512, nblk=16,
                                  blk_stride=128 * T),
                             m2_sb[ch % 2][:, :].rearrange("p (i w) -> p i w", w=512),
                             wait=[ev_m2l], ch=f"em{ch % 2}")
            ld_g[ch] = ev_m1
        r.fence()
        r.flush(block)

    with contextlib.ExitStack() as px:
        A2 = lambda name, shape, dt=F32: px.enter_context(nc.sbuf_tensor(name, shape, dt))
        l3_sb = A2("e2l3", [128, 16 * HID], BF16)
        l4_sb = A2("e2l4", [128, 16 * D], BF16)
        m2ch = [A2(f"e2c{i}", [128, 16 * 512], BF16) for i in range(2)]
        m3_sb = A2("e2m3", [128, 16 * 512], BF16)
        ot = [A2(f"e2o{i}", [128, D - MZ]) for i in range(2)]
        psM = [px.enter_context(nc.psum_tensor(f"e2bm{i}", [128, 512], F32)) for i in range(2)]
        psO = [px.enter_context(nc.psum_tensor(f"e2bo{i}", [128, 512], F32)) for i in range(2)]
        r = Rec(b)
        r.fence()
        evw1 = wsplit(r, l3_sb, g.e2_l3T, HID, HID, 16, 128 * HID, "ew0")
        evw2 = wsplit(r, l4_sb, g.e2_l4T, D, D, 16, 128 * D, "ew1")
        l33 = l3_sb.ap().rearrange("p (i f) -> p i f", f=HID)
        l43 = l4_sb.ap().rearrange("p (i f) -> p i f", f=D)
        m33 = m3_sb.ap().rearrange("p (i w) -> p i w", w=512)
        psg = [None, None]; pso_g = [None, None]
        ld_g = {}; st_g = {}
        for ch in range(8):
            ld = r.dma("gpsimd", m2ch[ch % 2][:, :].rearrange("p (i w) -> p i w", w=512),
                       ftap(g.m2_d, T, 0, 128, ch * 512, 512, nblk=16, blk_stride=128 * T),
                       wait=[ld_g.get(ch - 2)], ch=f"el{ch % 2}")
            c3 = m2ch[ch % 2].ap().rearrange("p (i w) -> p i w", w=512)
            ev_m3 = None
            for ft in range(16):
                rot = ft % 2
                for it in range(16):
                    r.t(lambda E, c3=c3, it=it, ft=ft, rot=rot: E.matmul(
                        psM[rot][:, :], l33[:, it, ft * 128:(ft + 1) * 128], c3[:, it, :],
                        start=(it == 0), stop=(it == 15)),
                        wait=[ld, evw1((ft + 1) * 128), psg[rot]], inc=(it == 15))
                mm = ("eT", r.c["eT"])
                ev_m3 = r.s(lambda E, ft=ft, rot=rot: E.activation(
                    m33[:, ft, :], psM[rot][:, :], ACT.Gelu), wait=[mm, ld_g.get(ch - 1)])
                psg[rot] = ev_m3
            for t4 in range(4):
                gt = ch * 4 + t4
                evo = []
                for fi, (c0, cw) in enumerate(((MZ, 512), (768, 256))):
                    for it in range(16):
                        r.t(lambda E, it=it, t4=t4, fi=fi, c0=c0, cw=cw: E.matmul(
                            psO[fi][:, 0:cw], m33[:, it, t4 * 128:(t4 + 1) * 128],
                            l43[:, it, c0:c0 + cw], start=(it == 0), stop=(it == 15)),
                            wait=[ev_m3, evw2(c0 + cw), pso_g[fi]], inc=(it == 15))
                    mm = ("eT", r.c["eT"])
                    ev = r.v(lambda E, gt=gt, fi=fi, c0=c0, cw=cw: E.tensor_scalar_mul(
                        ot[gt % 2][:, c0 - MZ:c0 - MZ + cw], psO[fi][:, 0:cw],
                        g.wts_sb[:, gt * 4 + 2: gt * 4 + 3]),
                        wait=[mm, st_g.get(gt - 2)])
                    evo.append(ev)
                    pso_g[fi] = ev
                st_g[gt] = r.dma("scalar", g.spill[2][gt * 128:(gt + 1) * 128, :],
                                 ot[gt % 2][:, :], wait=evo, ch=f"es{gt % 2}")
            ld_g[ch] = st_g[ch * 4 + 3]
        r.fence()
        r.flush(block)

    with contextlib.ExitStack() as px:
        A2 = lambda name, shape, dt=F32: px.enter_context(nc.sbuf_tensor(name, shape, dt))
        dn_sb = A2("e3dn", [128, CT * D], BF16)
        up_sb = A2("e3up", [128, CT * HID], BF16)
        out_sb = A2("e3ot", [128, 16 * D], BF16)
        d_sb = A2("e3d", [128, CT * 512], BF16)
        u_sb = A2("e3u", [128, 16 * 512], BF16)
        ot = [A2(f"e3o{i}", [128, D - MZ]) for i in range(2)]
        psM = [px.enter_context(nc.psum_tensor(f"e3pm{i}", [128, 512], F32)) for i in range(2)]
        psO = [px.enter_context(nc.psum_tensor(f"e3po{i}", [128, 512], F32)) for i in range(2)]
        r = Rec(b)
        r.fence()
        evw1 = wsplit(r, dn_sb, g.e3_downT, D, D, CT, 128 * D, "ew0")
        evw2 = wsplit(r, up_sb, g.e3_upT, HID, HID, CT, 128 * HID, "ew1")
        evw3 = wsplit(r, out_sb, g.e3_outT, D, D, 16, 128 * D, "ew2")
        loadxf = stream_xf(r, A2, "e3")
        dnw3 = dn_sb.ap().rearrange("p (c f) -> p c f", f=D)
        upw3 = up_sb.ap().rearrange("p (c f) -> p c f", f=HID)
        ow3 = out_sb.ap().rearrange("p (i f) -> p i f", f=D)
        d3 = d_sb.ap().rearrange("p (c w) -> p c w", w=512)
        u3 = u_sb.ap().rearrange("p (i w) -> p i w", w=512)
        psg = [None, None]; pso_g = [None, None]
        ld_g = {}; st_g = {}
        for ch in range(8):
            xf_t, ld = loadxf(ch, [ld_g.get(ch - 2)])
            x3 = xf_t.ap().rearrange("p (c w) -> p c w", w=512)
            ev_d = None
            for ft in range(CT):
                rot = ft % 2
                for ct in range(CT):
                    r.t(lambda E, x3=x3, ct=ct, ft=ft, rot=rot: E.matmul(
                        psM[rot][:, :], dnw3[:, ct, ft * 128:(ft + 1) * 128], x3[:, ct, :],
                        start=(ct == 0), stop=(ct == CT - 1)),
                        wait=[ld, evw1((ft + 1) * 128), psg[rot]], inc=(ct == CT - 1))
                mm = ("eT", r.c["eT"])
                ev_d = r.s(lambda E, ft=ft, rot=rot: E.activation(
                    d3[:, ft, :], psM[rot][:, :], ACT.Gelu), wait=[mm, ld_g.get(ch - 1)])
                psg[rot] = ev_d
            ev_u = None
            for ft in range(16):
                rot = ft % 2
                for ct in range(CT):
                    r.t(lambda E, ct=ct, ft=ft, rot=rot: E.matmul(
                        psM[rot][:, :], upw3[:, ct, ft * 128:(ft + 1) * 128], d3[:, ct, :],
                        start=(ct == 0), stop=(ct == CT - 1)),
                        wait=[ev_d, evw2((ft + 1) * 128), psg[rot]], inc=(ct == CT - 1))
                mm = ("eT", r.c["eT"])
                ev_u = r.s(lambda E, ft=ft, rot=rot: E.activation(
                    u3[:, ft, :], psM[rot][:, :], ACT.Gelu), wait=[mm])
                psg[rot] = ev_u
            for t4 in range(4):
                gt = ch * 4 + t4
                evo = []
                for fi, (c0, cw) in enumerate(((MZ, 512), (768, 256))):
                    for it in range(16):
                        r.t(lambda E, it=it, t4=t4, fi=fi, c0=c0, cw=cw: E.matmul(
                            psO[fi][:, 0:cw], u3[:, it, t4 * 128:(t4 + 1) * 128],
                            ow3[:, it, c0:c0 + cw], start=(it == 0), stop=(it == 15)),
                            wait=[ev_u, evw3(c0 + cw), pso_g[fi]], inc=(it == 15))
                    mm = ("eT", r.c["eT"])
                    ev = r.v(lambda E, gt=gt, fi=fi, c0=c0, cw=cw: E.tensor_scalar_mul(
                        ot[gt % 2][:, c0 - MZ:c0 - MZ + cw], psO[fi][:, 0:cw],
                        g.wts_sb[:, gt * 4 + 3: gt * 4 + 4]),
                        wait=[mm, st_g.get(gt - 2)])
                    evo.append(ev)
                    pso_g[fi] = ev
                st_g[gt] = r.dma("scalar", g.spill[3][gt * 128:(gt + 1) * 128, :],
                                 ot[gt % 2][:, :], wait=evo, ch=f"es{gt % 2}")
            ld_g[ch] = st_g[ch * 4 + 3]
        r.fence()
        r.flush(block)


def _phase_final(nc, b, block, Rec, loc):
    g = _G(loc)
    with contextlib.ExitStack() as px:
        A2 = lambda name, shape, dt=F32: px.enter_context(nc.sbuf_tensor(name, shape, dt))
        R = 2
        xit = [A2(f"p5x{i}", [128, D]) for i in range(R)]
        spt = [[A2(f"p5s{e}{i}", [128, D - MZ]) for i in range(R)] for e in range(4)]
        ut = [A2(f"p5u{i}", [128, D]) for i in range(2)]
        g1t = [A2(f"p5g{i}", [128, D]) for i in range(2)]
        x1t = [A2(f"p5a{i}", [128, D]) for i in range(R)]
        s1t = [A2(f"p5b{i}", [128, D]) for i in range(R)]
        r = Rec(b)
        r.fence()
        ev_fin = {}
        ev_used = {}
        for tt in range(NT):
            gd = [ev_used.get(tt - R)]
            ldx = r.dma("sync", xit[tt % R][:, :], g.xi_tf[tt * 128:(tt + 1) * 128, :],
                        wait=gd, ch=f"lnA{tt % R}")
            ld1 = r.dma("sync", x1t[tt % R][:, :], g.x1_tf[tt * 128:(tt + 1) * 128, :],
                        wait=gd, ch=f"p2x{tt % R}")
            ld2 = r.dma("sync", s1t[tt % R][:, :], g.s1_tf[tt * 128:(tt + 1) * 128, :],
                        wait=gd, ch=f"p2s{tt % R}")
            lsp = [r.dma("gpsimd", spt[e][tt % R][:, :],
                         g.spill[e][tt * 128:(tt + 1) * 128, :], wait=gd,
                         ch=f"e{e}_x{tt % R}") for e in range(4)]
            ev_u = r.v(lambda E, tt=tt: E.tensor_copy(ut[tt % 2][:, 0:MZ],
                                                      xit[tt % R][:, 0:MZ]),
                       wait=[ldx, ev_fin.get(tt - 2)])
            ev_u = r.v(lambda E, tt=tt: E.tensor_tensor(
                ut[tt % 2][:, MZ:D], xit[tt % R][:, MZ:D], spt[0][tt % R][:, :], OP.add),
                wait=[lsp[0]])
            for e in range(1, 4):
                ev_u = r.v(lambda E, tt=tt, e=e: E.tensor_tensor(
                    ut[tt % 2][:, MZ:D], ut[tt % 2][:, MZ:D], spt[e][tt % R][:, :], OP.add),
                    wait=[lsp[e]])
            ev_g = r.s(lambda E, tt=tt: E.activation(g1t[tt % 2][:, :], ut[tt % 2][:, :],
                                                     ACT.Silu),
                       wait=[ev_u, ev_fin.get(tt - 2)])
            r.v(lambda E, tt=tt: E.tensor_tensor(g1t[tt % 2][:, :], s1t[tt % R][:, :],
                                                 g1t[tt % 2][:, :], OP.mult),
                wait=[ev_g, ld2], inc=False)
            ev_o = r.v(lambda E, tt=tt: E.tensor_tensor(
                g1t[tt % 2][:, :], x1t[tt % R][:, :], g1t[tt % 2][:, :], OP.add),
                wait=[ld1])
            ev_used[tt] = ev_o
            ev_fin[tt] = r.dma("scalar", g.y_out[tt * 128:(tt + 1) * 128, :],
                               g1t[tt % 2][:, :], wait=[ev_o], ch=f"p5y{tt % 2}")
        r.fence()
        r.flush(block)


def make_in_maps(inputs):
    tr = lambda a: np.ascontiguousarray(np.asarray(a, dtype=np.float32).T)
    f = lambda a: np.ascontiguousarray(np.asarray(a, dtype=np.float32))
    W = inputs
    com = dict(
        wqT=tr(W["L0_attn_W_w"][0:D]), wkT=tr(W["L0_attn_W_w"][D:2 * D]),
        wvT=tr(W["L0_attn_W_w"][2 * D:3 * D]),
        w_b=f(W["L0_attn_W_b"]), projT=tr(W["L0_attn_proj_w"]),
        proj_b=f(W["L0_attn_proj_b"]), ln_g=f(W["L0_attn_ln_g"]),
        ln_b=f(W["L0_attn_ln_b"]), rpbT=tr(W["L0_attn_rpb"]),
        clip_x=f(W["L0_ffn_clip_x"]), clip_down=f(W["L0_ffn_clip_down"]),
        w_gate=f(W["L0_ffn_w_gate"]), w_up=f(W["L0_ffn_w_up"]),
        w_down=f(W["L0_ffn_w_down"]),
        w_inT=tr(W["L1_attn_w_in"]), w_conv=f(W["L1_attn_w_conv"].reshape(D, 4)),
        b_conv=f(W["L1_attn_b_conv"]), w_gatesT=tr(W["L1_attn_w_gates"]),
        b_gates=f(W["L1_attn_b_gates"]), fbase=f(W["L1_attn_forget_base"]),
        w_outT=tr(W["L1_attn_w_out"]),
        n1_w=f(W["L1_ffn_n1_w"]), n2_w=f(W["L1_ffn_n2_w"]),
        l1wqT=tr(W["L1_ffn_wq"]), l1wkT=tr(W["L1_ffn_wk"]),
        l1wvT=tr(W["L1_ffn_wv"]), l1woT=tr(W["L1_ffn_wo"]),
        qn_w=f(W["L1_ffn_qn_w"]), kn_w=f(W["L1_ffn_kn_w"]),
        gate_wT=tr(W["L1_ffn_gate_w"]),
        e0_upT=tr(W["L1_ffn_e0_up"]), e0_downT=tr(W["L1_ffn_e0_down"]),
        e1_w1T=tr(W["L1_ffn_e1_w1"]), e1_w2T=tr(W["L1_ffn_e1_w2"]),
        e1_downT=tr(W["L1_ffn_e1_down"]),
        e2_l1T=tr(W["L1_ffn_e2_l1"]), e2_l2T=tr(W["L1_ffn_e2_l2"]),
        e2_l3T=tr(W["L1_ffn_e2_l3"]), e2_l4T=tr(W["L1_ffn_e2_l4"]),
        e3_downT=tr(W["L1_ffn_e3_down"]), e3_upT=tr(W["L1_ffn_e3_up"]),
        e3_outT=tr(W["L1_ffn_e3_out"]),
    )
    x = np.asarray(W["L0_attn_x"], dtype=np.float32)
    nb = x.shape[0] // N_CORES
    in_maps = []
    for c in range(N_CORES):
        m = dict(com)
        m["x"] = np.ascontiguousarray(
            x[c * nb:(c + 1) * nb].reshape(nb * x.shape[1], x.shape[2]))
        in_maps.append(m)
    return in_maps


def kernel(**inputs):
    in_maps = make_in_maps(inputs)
    nc = build()
    res = run_bass_kernel_spmd(nc, in_maps, core_ids=list(range(N_CORES)))
    B, Sq, Dm = inputs["L0_attn_x"].shape
    nb = B // N_CORES
    out = np.concatenate(
        [res.results[c]["y"].reshape(nb, Sq, Dm) for c in range(N_CORES)], axis=0)
    return out
